# version 06
# baseline (speedup 1.0000x reference)
import contextlib
import numpy as np
import concourse.bass as bass
import concourse.mybir as mybir

F32 = mybir.dt.float32
BF16 = mybir.dt.bfloat16
I32 = mybir.dt.int32
AF = mybir.ActivationFunctionType
ALU = mybir.AluOpType
AX = mybir.AxisListType

ENGINES = ["tensor", "vector", "scalar", "gpsimd", "sync"]
DMA_QUEUES = ["gpsimd", "sync", "scalar"]


DMA_SLOTS = 8

class Op:
    __slots__ = ("eng", "fn", "reads", "writes", "idx", "dma", "deps",
                 "awaited", "sem_val", "vclock", "slot")
    def __init__(self, eng, fn, reads, writes, idx, dma):
        self.eng = eng; self.fn = fn; self.reads = reads; self.writes = writes
        self.idx = idx; self.dma = dma
        self.deps = set(); self.awaited = False; self.sem_val = None
        self.vclock = None; self.slot = None


class MK:
    def __init__(self, nc):
        self.nc = nc
        self.stack = contextlib.ExitStack()
        self.ops = []
        self.last_write = {}
        self.readers = {}
        self.sbuf_bytes = 0
        self._consts = {}
        self._const_tile = None
        self._const_used = 0

    def sbuf(self, name, shape, dtype):
        h = self.stack.enter_context(self.nc.sbuf_tensor(name, list(shape), dtype))
        bpe = 2 if dtype == BF16 else 4
        self.sbuf_bytes += int(np.prod(shape[1:])) * bpe
        return h

    def psum(self, name, shape, dtype=F32):
        return self.stack.enter_context(self.nc.psum_tensor(name, list(shape), dtype))

    def sem(self, name):
        return self.stack.enter_context(self.nc.semaphore(name))

    def dram(self, name, shape, dtype, kind="Internal"):
        return self.nc.dram_tensor(name, list(shape), dtype, kind=kind).ap()

    def constcol(self, value, parts=128):
        key = float(value)
        if key in self._consts:
            ap128, tag = self._consts[key]
            return ap128[0:parts, 0:1], tag
        if self._const_tile is None:
            self._const_tile = self.sbuf("mk_consts", [128, 64], F32)
        i = self._const_used; self._const_used += 1
        assert i < 64
        full = self._const_tile[:, i:i+1]
        tag = f"const.{i}"
        self.op("gpsimd", lambda e, full=full, v=float(value): e.memset(full, v),
                reads=[], writes=[tag])
        self._consts[key] = (full, tag)
        return full[0:parts, 0:1], tag

    def op(self, eng, fn, reads=(), writes=(), dma=False):
        assert eng in ENGINES
        idx = len(self.ops)
        o = Op(eng, fn, list(reads), list(writes), idx, dma)
        for t in o.reads:
            if t in self.last_write:
                o.deps.add(self.last_write[t])
        for t in o.writes:
            if t in self.last_write:
                o.deps.add(self.last_write[t])
            per_eng_last = {}
            for r in self.readers.get(t, ()):
                ro = self.ops[r]
                if ro.dma:
                    o.deps.add(r)
                else:
                    k = ro.eng
                    if k not in per_eng_last or per_eng_last[k] < r:
                        per_eng_last[k] = r
            for r in per_eng_last.values():
                o.deps.add(r)
        for t in o.reads:
            self.readers.setdefault(t, []).append(idx)
        for t in o.writes:
            self.last_write[t] = idx
            self.readers[t] = []
        self.ops.append(o)
        return idx

    def dma(self, queue, out, in_, reads=(), writes=(), **kw):
        def fn(e, out=out, in_=in_, kw=kw):
            return e.dma_start(out=out, in_=in_, **kw)
        return self.op(queue, fn, reads, writes, dma=True)

    def barrier_read(self, tags):
        self._barrier_idx = self.op("gpsimd", lambda e: e.nop(), reads=list(tags), writes=[])

    def build(self, verbose=True):
        ops = self.ops
        n_eng = len(ENGINES)
        eidx = {e: i for i, e in enumerate(ENGINES)}

        def needs_edge(dep, o):
            if dep.dma:
                return True
            if dep.eng == o.eng == "tensor":
                return False
            return True

        for o in ops:
            for d in o.deps:
                dep = ops[d]
                if needs_edge(dep, o):
                    dep.awaited = True
        eng_await_ctr = {e: 0 for e in ENGINES}
        q_issue_ctr = {e: 0 for e in DMA_QUEUES}
        slot_ctr = {(q, sl): 0 for q in DMA_QUEUES for sl in range(DMA_SLOTS)}
        for o in ops:
            if o.dma:
                o.slot = q_issue_ctr[o.eng] % DMA_SLOTS
                q_issue_ctr[o.eng] += 1
                slot_ctr[(o.eng, o.slot)] += 16
                o.sem_val = slot_ctr[(o.eng, o.slot)]
            elif o.awaited:
                eng_await_ctr[o.eng] += 1
                o.sem_val = eng_await_ctr[o.eng]

        qidx = {q: i for i, q in enumerate(DMA_QUEUES)}
        NCH = n_eng + len(DMA_QUEUES) * DMA_SLOTS
        dchan = lambda q, sl: n_eng + qidx[q] * DMA_SLOTS + sl
        eng_run = {e: [0] * NCH for e in ENGINES}
        waits = {}
        n_waits = 0
        for o in ops:
            run = eng_run[o.eng]
            w = []
            for d in sorted(o.deps):
                dep = ops[d]
                if dep.dma:
                    ch = dchan(dep.eng, dep.slot)
                    val = dep.sem_val
                elif not needs_edge(dep, o):
                    if dep.vclock is not None:
                        for c2 in range(NCH):
                            if dep.vclock[c2] > run[c2]:
                                run[c2] = dep.vclock[c2]
                    continue
                else:
                    ch = eidx[dep.eng]
                    val = dep.sem_val
                if run[ch] < val:
                    w.append((ch, val))
                    run[ch] = val
                    if dep.vclock is not None:
                        for c2 in range(NCH):
                            if dep.vclock[c2] > run[c2]:
                                run[c2] = dep.vclock[c2]
            if o.dma and o.sem_val > 16:
                ch = dchan(o.eng, o.slot)
                if run[ch] < o.sem_val - 16:
                    w.append((ch, o.sem_val - 16))
                    run[ch] = o.sem_val - 16
            waits[o.idx] = w
            n_waits += len(w)
            o.vclock = list(run)

        self.stats = dict(
            n_ops=len(ops), awaited=sum(1 for o in ops if o.awaited),
            dmas=sum(1 for o in ops if o.dma), waits=n_waits,
            per_eng={e: sum(1 for o in ops if o.eng == e) for e in ENGINES},
            sbuf_kb=self.sbuf_bytes / 1024)
        if verbose:
            print("MK stats:", self.stats)

        bi = getattr(self, "_barrier_idx", None)
        if bi is not None:
            brun = eng_run["gpsimd"]
            extra = waits[bi]
            have = {c: v for c, v in extra}
            for (q, sl), tot in slot_ctr.items():
                ch = dchan(q, sl)
                known = max(self.ops[bi].vclock[ch], have.get(ch, 0))
                if tot > known:
                    extra.append((ch, tot))

        esems = {e: self.sem(f"sem_{e}") for e in ENGINES}
        dsem_pool = {(q, sl): self.sem(f"dsem_{q}{sl}")
                     for q in DMA_QUEUES for sl in range(DMA_SLOTS)}

        def chan_sem(ch):
            if ch < n_eng:
                return esems[ENGINES[ch]]
            r = ch - n_eng
            return dsem_pool[(DMA_QUEUES[r // DMA_SLOTS], r % DMA_SLOTS)]

        by_eng = {e: [o for o in ops if o.eng == e] for e in ENGINES}

        def emit(engine_obj, ename):
            for o in by_eng[ename]:
                for ch, val in waits[o.idx]:
                    engine_obj.wait_ge(chan_sem(ch), val)
                inst = o.fn(engine_obj)
                if o.dma:
                    inst.then_inc(dsem_pool[(o.eng, o.slot)], 16)
                elif o.awaited:
                    inst.then_inc(esems[o.eng], 1)

        block = self.stack.enter_context(self.nc.Block())
        @block.tensor
        def _(t): emit(t, "tensor")
        @block.vector
        def _(v): emit(v, "vector")
        @block.scalar
        def _(s): emit(s, "scalar")
        @block.gpsimd
        def _(g): emit(g, "gpsimd")
        @block.sync
        def _(s): emit(s, "sync")
        self.stack.close()
        return self.nc


class T2:
    def __init__(self, mk, name, rows, cols, dtype):
        assert rows % 128 == 0 or rows < 128
        self.name = name; self.rows = rows; self.cols = cols; self.dtype = dtype
        self.nch = max(1, rows // 128)
        self.parts = min(rows, 128)
        self.h = mk.sbuf(name, [self.parts, self.nch * cols], dtype)
    def ap(self, chunk=0, cs=None, ce=None, p0=0, p1=None):
        cs = 0 if cs is None else cs
        ce = self.cols if ce is None else ce
        p1 = self.parts if p1 is None else p1
        return self.h[p0:p1, chunk * self.cols + cs: chunk * self.cols + ce]
    def ap3(self, cs=0, ce=None):
        ce = self.cols if ce is None else ce
        v = self.h[:, :].rearrange("p (c t) -> p c t", c=self.nch)
        return v[:, :, cs:ce]
    def tag(self, chunk=0):
        return f"{self.name}.{chunk}"
    def tags(self):
        return [self.tag(c) for c in range(self.nch)]


def fm_hbm_view(hbm_ap, p=128):
    return hbm_ap.rearrange("(c p) t -> p c t", p=p)



def emit_ln(mk, x, xb, xn, u, rows, rep, onescol, onesrow_f, bsum, bsq, brr, brm,
            TC, scale_col=None, shift_col=None, sc_tags=()):
    NCH = x.nch
    ceps, teps = mk.constcol(1e-5, parts=1)
    mk.op("scalar", lambda e, a=xb.h[:, :], b=x.h[:, :]: e.activation(a, b, AF.Copy),
          reads=x.tags(), writes=xb.tags())
    mk.op("scalar", lambda e, a=xn.h[:, :], b=xb.h[:, :]: e.activation(a, b, AF.Square),
          reads=xb.tags(), writes=xn.tags())
    for k in range(NCH):
        mk.op("tensor", lambda e, l=onescol.ap(), r=xb.ap(k), o=bsum[0][0:1, 0:TC],
              s=(k == 0), p=(k == NCH - 1): e.matmul(o, l, r, start=s, stop=p),
              reads=[onescol.tag(), xb.tag(k)], writes=[bsum[1]])
    for k in range(NCH):
        mk.op("tensor", lambda e, l=onescol.ap(), r=xn.ap(k), o=bsq[0][0:1, 0:TC],
              s=(k == 0), p=(k == NCH - 1): e.matmul(o, l, r, start=s, stop=p),
              reads=[onescol.tag(), xn.tag(k)], writes=[bsq[1]])
    R = lambda i: rows.ap(0, i * TC, (i + 1) * TC)
    rt = lambda i: f"{rows.name}.r{i}"
    mk.op("vector", lambda e, o=R(0), i_=bsum[0][0:1, 0:TC]:
          e.tensor_scalar(o, i_, 1.0 / 1024, None, ALU.mult),
          reads=[bsum[1]], writes=[rt(0)])
    mk.op("vector", lambda e, o=R(1), i_=bsq[0][0:1, 0:TC]:
          e.tensor_scalar(o, i_, 1.0 / 1024, None, ALU.mult),
          reads=[bsq[1]], writes=[rt(1)])
    mk.op("vector", lambda e, o=R(4), a=R(0): e.tensor_mul(o, a, a),
          reads=[rt(0)], writes=[rt(4)])
    mk.op("vector", lambda e, o=R(1), a=R(1), b=R(4): e.tensor_sub(o, a, b),
          reads=[rt(1), rt(4)], writes=[rt(1)])
    mk.op("scalar", lambda e, o=R(5), i_=R(1), b=ceps: e.activation(o, i_, AF.Sqrt, bias=b),
          reads=[rt(1), teps], writes=[rt(5)])
    mk.op("vector", lambda e, o=R(2), i_=R(5): e.reciprocal(o, i_),
          reads=[rt(5)], writes=[rt(2)])
    mk.op("vector", lambda e, o=R(3), a=R(0): e.tensor_scalar(o, a, -1.0, None, ALU.mult),
          reads=[rt(0)], writes=[rt(3)])
    mk.op("tensor", lambda e, o=brr[0][:, 0:TC], l=onesrow_f.ap(), r=R(2):
          e.matmul(o, l, r, start=True, stop=True),
          reads=[onesrow_f.tag(), rt(2)], writes=[brr[1]])
    mk.op("tensor", lambda e, o=brm[0][:, 0:TC], l=onesrow_f.ap(), r=R(3):
          e.matmul(o, l, r, start=True, stop=True),
          reads=[onesrow_f.tag(), rt(3)], writes=[brm[1]])
    REP_R, REP_M = rep.ap(0, 0, TC), rep.ap(0, TC, 2 * TC)
    mk.op("vector", lambda e, o=REP_R, i_=brr[0][:, 0:TC]: e.tensor_copy(o, i_),
          reads=[brr[1]], writes=[f"{rep.name}.r"])
    mk.op("vector", lambda e, o=REP_M, i_=brm[0][:, 0:TC]: e.tensor_copy(o, i_),
          reads=[brm[1]], writes=[f"{rep.name}.m"])
    for c in range(NCH):
        mk.op("gpsimd", lambda e, o=u.ap(c), a=x.ap(c), b=REP_M: e.tensor_add(o, a, b),
              reads=[x.tag(c), f"{rep.name}.m"], writes=[u.tag(c)])
        if scale_col is None:
            mk.op("vector", lambda e, o=xn.ap(c), a=u.ap(c), b=REP_R: e.tensor_mul(o, a, b),
                  reads=[u.tag(c), f"{rep.name}.r"], writes=[xn.tag(c)])
        else:
            mk.op("vector", lambda e, o=u.ap(c), a=u.ap(c), b=REP_R: e.tensor_mul(o, a, b),
                  reads=[u.tag(c), f"{rep.name}.r"], writes=[u.tag(c)])
            mk.op("scalar", lambda e, o=xn.ap(c), i_=u.ap(c), s=scale_col[c], b=shift_col[c]:
                  e.activation(o, i_, AF.Identity, bias=b, scale=s),
                  reads=[u.tag(c), *sc_tags], writes=[xn.tag(c)])



def emit_ffn0(mk, xin_h, xin_tag, xout_h, xout_tag, wg_h, wu_h, wd_h,
              TC=256, name="f", load_weights=True, wts=None):
    DM, T, HID = 1024, 4096, 3072
    NCH, NH = DM // 128, HID // 128
    NT = T // TC

    if wts is None:
        wg = T2(mk, name + "wg", DM, HID, BF16)
        wu = T2(mk, name + "wu", DM, HID, BF16)
        wd = T2(mk, name + "wd", HID, DM, BF16)
    else:
        wg, wu, wd = wts
    x = T2(mk, name + "x", DM, TC, F32)
    xb = T2(mk, name + "xb", DM, TC, BF16)
    xn = T2(mk, name + "xn", DM, TC, BF16)
    u = T2(mk, name + "u", DM, TC, F32)
    hs = T2(mk, name + "hs", 256, TC, BF16)
    hh = T2(mk, name + "hh", HID, TC, BF16)
    rep = T2(mk, name + "rep", 128, 2 * TC, F32)
    rows = T2(mk, name + "rows", 1, 6 * TC, F32)
    onescol = T2(mk, name + "onc", 128, 1, BF16)
    onesrow_f = T2(mk, name + "onr", 1, 128, F32)

    pb = [mk.psum(f"{name}pb{i}", [128, 512], F32) for i in range(8)]

    c127, t127 = mk.constcol(1.0 / 127.0)
    chalf127, thalf = mk.constcol(0.5 / 127.0)

    mk.op("gpsimd", lambda e: e.memset(onescol.h[:, :], 1.0), writes=[onescol.tag()])
    mk.op("gpsimd", lambda e: e.memset(onesrow_f.h[:, :], 1.0), writes=[onesrow_f.tag()])

    if load_weights:
        for w, hbm in ((wg, wg_h), (wu, wu_h), (wd, wd_h)):
            mk.dma("gpsimd", w.ap3(), fm_hbm_view(hbm), reads=[], writes=w.tags())

    for it in range(NT):
        t0 = it * TC
        mk.dma("sync", x.ap3(), fm_hbm_view(xin_h)[:, :, t0:t0 + TC],
               reads=[xin_tag], writes=x.tags())
        emit_ln(mk, x, xb, xn, u, rows, rep, onescol, onesrow_f,
                (pb[4], f"{name}b4"), (pb[5], f"{name}b5"),
                (pb[6], f"{name}b6"), (pb[7], f"{name}b7"), TC)
        for m in range(NH):
            gb, ub = 4 + (m % 2) * 2, 5 + (m % 2) * 2
            pg, pu = pb[gb], pb[ub]
            gt, ut = f"{name}b{gb}", f"{name}b{ub}"
            for k in range(NCH):
                mk.op("tensor", lambda e, o=pg[:, 0:TC], l=wg.ap(k, m * 128, (m + 1) * 128),
                      r=xn.ap(k), s=(k == 0), p=(k == NCH - 1): e.matmul(o, l, r, start=s, stop=p),
                      reads=[wg.tag(k), xn.tag(k)], writes=[gt])
            for k in range(NCH):
                mk.op("tensor", lambda e, o=pu[:, 0:TC], l=wu.ap(k, m * 128, (m + 1) * 128),
                      r=xn.ap(k), s=(k == 0), p=(k == NCH - 1): e.matmul(o, l, r, start=s, stop=p),
                      reads=[wu.tag(k), xn.tag(k)], writes=[ut])
            mk.op("scalar", lambda e, o=hs.ap(m % 2), i_=pg[:, 0:TC], s=c127:
                  e.activation(o, i_, AF.Silu, scale=s),
                  reads=[gt, t127], writes=[hs.tag(m % 2)])
            mk.op("vector", lambda e, o=hh.ap(m), a=pu[:, 0:TC], b=hs.ap(m % 2), s=1.0 / 127.0:
                  e.scalar_tensor_tensor(o, a, s, b, ALU.mult, ALU.mult),
                  reads=[ut, hs.tag(m % 2)], writes=[hh.tag(m)])
        for m2 in range(NCH):
            db = m2 % 2
            pd = pb[db]
            dt = f"{name}b{db}"
            for k2 in range(NH):
                mk.op("tensor", lambda e, o=pd[:, 0:TC], l=wd.ap(k2, m2 * 128, (m2 + 1) * 128),
                      r=hh.ap(k2), s=(k2 == 0), p=(k2 == NH - 1): e.matmul(o, l, r, start=s, stop=p),
                      reads=[wd.tag(k2), hh.tag(k2)], writes=[dt])
            mk.op("vector", lambda e, o=x.ap(m2), a=pd[:, 0:TC], b=x.ap(m2), s=0.5 / 127.0:
                  e.scalar_tensor_tensor(o, a, s, b, ALU.mult, ALU.add),
                  reads=[dt, x.tag(m2)], writes=[x.tag(m2)])
        mk.dma("scalar", fm_hbm_view(xout_h)[:, :, t0:t0 + TC], x.ap3(),
               reads=x.tags(), writes=[f"{xout_tag}.{it}"])
    return [f"{xout_tag}.{it}" for it in range(NT)]


def build_ffn0_program():
    nc = bass.Bass("TRN2", target_bir_lowering=False)
    mk = MK(nc)
    xin = mk.dram("xt", [1024, 4096], F32, kind="ExternalInput")
    wg = mk.dram("wgT", [1024, 3072], I32, kind="ExternalInput")
    wu = mk.dram("wuT", [1024, 3072], I32, kind="ExternalInput")
    wd = mk.dram("wdT", [3072, 1024], I32, kind="ExternalInput")
    xout = mk.dram("xo", [1024, 4096], F32, kind="ExternalOutput")
    out_tags = emit_ffn0(mk, xin, "hbm.xt", xout, "hbm.xo", wg, wu, wd)
    mk.barrier_read(out_tags)
    mk.build()
    return nc



def emit_moe(mk, x_h, x_tag, y2_h, y2_tag, xout_h, xout_tag,
             gw_h, w1_h, w2_h, w3_h, ln2s_h, ln2b_h, idf_h, name="m"):
    DM, T, HID, NE = 1024, 4096, 2048, 8
    NCH, NHC = DM // 128, HID // 128
    QT, TC = 1024, 512
    NQ = T // QT

    acc = T2(mk, name + "acc", DM, QT, F32)
    xn2 = T2(mk, name + "xn2", DM, QT, BF16)
    hh = T2(mk, name + "hh", HID, QT, BF16)
    w1s = T2(mk, name + "w1s", DM, 512, BF16)
    w2s = T2(mk, name + "w2s", DM, 512, BF16)
    w3s = T2(mk, name + "w3s", HID, 256, BF16)
    wrep = T2(mk, name + "wrep", 128, QT, F32)
    y2c = T2(mk, name + "y2c", DM, TC, F32)
    u = T2(mk, name + "u", DM, TC, F32)
    xb = T2(mk, name + "xb", DM, TC, BF16)
    xnc = T2(mk, name + "xnc", DM, TC, BF16)
    rep = T2(mk, name + "rep", 128, 2 * TC, F32)
    rows = T2(mk, name + "rows", 1, 6 * TC, F32)
    hs = T2(mk, name + "hs", 256, TC, BF16)
    hu = T2(mk, name + "hu", 256, TC, BF16)
    gsb = T2(mk, name + "gsb", 8, QT, F32)
    gst = T2(mk, name + "gst", 128, 8 * 8, F32)
    gsc = T2(mk, name + "gsc", 128, 8, F32)
    gsc2 = T2(mk, name + "gsc2", 128, 8, F32)
    grows = T2(mk, name + "grows", 128, 8, F32)
    wtok = T2(mk, name + "wtok", 128, 8 * 8, F32)
    wrow = T2(mk, name + "wrow", 8, QT, F32)
    wrowe = T2(mk, name + "wrowe", 1, QT, F32)
    gw_s = T2(mk, name + "gw", DM, 8, BF16)
    ln2s = T2(mk, name + "ln2s", 128, 8, F32)
    ln2b = T2(mk, name + "ln2b", 128, 8, F32)
    idf = T2(mk, name + "idf", 128, 128, F32)
    onescol = T2(mk, name + "onc", 128, 1, BF16)
    onesrow_f = T2(mk, name + "onr", 1, 128, F32)

    pb = [mk.psum(f"{name}pb{i}", [128, 512], F32) for i in range(8)]
    B = lambda i: (pb[i], f"{name}b{i}")

    mk.op("gpsimd", lambda e: e.memset(onescol.h[:, :], 1.0), writes=[onescol.tag()])
    mk.op("gpsimd", lambda e: e.memset(onesrow_f.h[:, :], 1.0), writes=[onesrow_f.tag()])
    mk.dma("gpsimd", gw_s.ap3(), fm_hbm_view(gw_h), writes=gw_s.tags())
    mk.dma("sync", ln2s.h[:, :], ln2s_h, writes=[ln2s.tag()])
    mk.dma("sync", ln2b.h[:, :], ln2b_h, writes=[ln2b.tag()])
    mk.dma("sync", idf.h[:, :], idf_h, writes=[idf.tag()])
    sc_cols = [ln2s.h[:, c:c+1] for c in range(NCH)]
    sh_cols = [ln2b.h[:, c:c+1] for c in range(NCH)]
    out_tags = []

    for q in range(NQ):
        t0 = q * QT
        for half in range(QT // TC):
            h0 = half * TC
            mk.dma("sync", y2c.ap3(), fm_hbm_view(y2_h)[:, :, t0 + h0:t0 + h0 + TC],
                   reads=[y2_tag], writes=y2c.tags())
            mk.dma("scalar", u.ap3(), fm_hbm_view(x_h)[:, :, t0 + h0:t0 + h0 + TC],
                   reads=[x_tag], writes=u.tags())
            for c in range(NCH):
                mk.op("vector", lambda e, o=acc.ap(c, h0, h0 + TC), a=y2c.ap(c), b=u.ap(c):
                      e.scalar_tensor_tensor(o, a, 0.5, b, ALU.mult, ALU.add),
                      reads=[y2c.tag(c), u.tag(c)], writes=[acc.tag(c)])
            emit_ln(mk, y2c, xb, xnc, u, rows, rep, onescol, onesrow_f,
                    B(4), B(5), B(6), B(7), TC,
                    scale_col=sc_cols, shift_col=sh_cols, sc_tags=[ln2s.tag(), ln2b.tag()])
            for c in range(NCH):
                mk.op("gpsimd", lambda e, o=xn2.ap(c, h0, h0 + TC), a=xnc.ap(c):
                      e.tensor_copy(o, a),
                      reads=[xnc.tag(c)], writes=[xn2.tag(c)])

        for tc in range(QT // TC):
            for k in range(NCH):
                mk.op("tensor", lambda e, o=pb[6][0:8, 0:TC], l=gw_s.ap(k),
                      r=xn2.ap(k, tc * TC, (tc + 1) * TC),
                      s=(k == 0), p=(k == NCH - 1): e.matmul(o, l, r, start=s, stop=p),
                      reads=[gw_s.tag(k), xn2.tag(k)], writes=[f"{name}b6"])
            mk.op("vector", lambda e, o=gsb.ap(0, tc * TC, (tc + 1) * TC), i_=pb[6][0:8, 0:TC]:
                  e.tensor_copy(o, i_), reads=[f"{name}b6"], writes=[gsb.tag(0)])
        for st in range(QT // 128):
            mk.op("tensor", lambda e, o=pb[7][0:128, 0:8],
                  i_=gsb.ap(0, st * 128, (st + 1) * 128, p0=0, p1=8), idn=idf.h[0:8, 0:8]:
                  e.transpose(o, i_, idn),
                  reads=[gsb.tag(0), idf.tag()], writes=[f"{name}b7"])
            mk.op("vector", lambda e, o=gst.ap(0, st * 8, (st + 1) * 8), i_=pb[7][0:128, 0:8]:
                  e.tensor_copy(o, i_), reads=[f"{name}b7"], writes=[gst.tag(0) + f".{st}"])
        G = lambda i: grows.ap(0, i, i + 1)
        gt = lambda i: f"{name}grows.{i}"
        for st in range(QT // 128):
            sap = gst.ap(0, st * 8, (st + 1) * 8)
            stag = gst.tag(0) + f".{st}"
            wt = wtok.ap(0, st * 8, (st + 1) * 8)
            wttag = wtok.tag(0) + f".{st}"
            mk.op("vector", lambda e, o=G(0), i_=sap: e.reduce_max(o, i_, AX.X),
                  reads=[stag], writes=[gt(0)])
            mk.op("vector", lambda e, o=gsc.h[:, :], a=sap, b=G(0):
                  e.tensor_scalar(o, a, b, None, ALU.is_equal),
                  reads=[stag, gt(0)], writes=[gsc.tag(0)])
            mk.op("vector", lambda e, o=gsc2.h[:, :], a=gsc.h[:, :], b=sap:
                  e.scalar_tensor_tensor(o, a, -1e30, b, ALU.mult, ALU.add),
                  reads=[gsc.tag(0), stag], writes=[gsc2.tag(0)])
            mk.op("vector", lambda e, o=G(1), i_=gsc2.h[:, :]: e.reduce_max(o, i_, AX.X),
                  reads=[gsc2.tag(0)], writes=[gt(1)])
            mk.op("vector", lambda e, o=G(2), i_=G(0): e.tensor_scalar(o, i_, -1.0, None, ALU.mult),
                  reads=[gt(0)], writes=[gt(2)])
            mk.op("scalar", lambda e, o=gsc.h[:, :], i_=sap, b=G(2):
                  e.activation(o, i_, AF.Exp, bias=b),
                  reads=[stag, gt(2)], writes=[gsc.tag(0)])
            mk.op("vector", lambda e, o=G(3), a=G(1), b=G(0): e.tensor_sub(o, a, b),
                  reads=[gt(1), gt(0)], writes=[gt(3)])
            mk.op("scalar", lambda e, o=G(4), i_=G(3): e.activation(o, i_, AF.Exp),
                  reads=[gt(3)], writes=[gt(4)])
            mk.op("vector", lambda e, o=G(4), i_=G(4): e.tensor_scalar(o, i_, 1.0, None, ALU.add),
                  reads=[gt(4)], writes=[gt(4)])
            mk.op("vector", lambda e, o=G(5), i_=G(4): e.reciprocal(o, i_),
                  reads=[gt(4)], writes=[gt(5)])
            mk.op("vector", lambda e, o=G(5), i_=G(5): e.tensor_scalar(o, i_, 0.5, None, ALU.mult),
                  reads=[gt(5)], writes=[gt(5)])
            mk.op("vector", lambda e, o=gsc2.h[:, :], a=sap, b=G(1):
                  e.tensor_scalar(o, a, b, None, ALU.is_ge),
                  reads=[stag, gt(1)], writes=[gsc2.tag(0)])
            mk.op("vector", lambda e, o=wt, a=gsc.h[:, :], b=gsc2.h[:, :], s=G(5):
                  e.scalar_tensor_tensor(o, a, s, b, ALU.mult, ALU.mult),
                  reads=[gsc.tag(0), gsc2.tag(0), gt(5)], writes=[wttag])
        for st in range(QT // 128):
            mk.op("tensor", lambda e, o=pb[7][0:8, 0:128],
                  i_=wtok.ap(0, st * 8, (st + 1) * 8), idn=idf.h[:, :]:
                  e.transpose(o, i_, idn),
                  reads=[wtok.tag(0) + f".{st}", idf.tag()], writes=[f"{name}b7"])
            mk.op("vector", lambda e, o=wrow.ap(0, st * 128, (st + 1) * 128, p0=0, p1=8),
                  i_=pb[7][0:8, 0:128]: e.tensor_copy(o, i_),
                  reads=[f"{name}b7"], writes=[wrow.tag(0)])

        for ex in range(NE):
            mk.dma("gpsimd", wrowe.h[0:1, :], wrow.h[ex:ex+1, :],
                   reads=[wrow.tag(0)], writes=[wrowe.tag(0)])
            for tc in range(QT // TC):
                mk.op("tensor", lambda e, o=pb[6][:, 0:TC], l=onesrow_f.ap(),
                      r=wrowe.ap(0, tc * TC, (tc + 1) * TC),
                      : e.matmul(o, l, r, start=True, stop=True),
                      reads=[onesrow_f.tag(), wrowe.tag(0)], writes=[f"{name}b6"])
                mk.op("vector", lambda e, o=wrep.ap(0, tc * TC, (tc + 1) * TC), i_=pb[6][:, 0:TC]:
                      e.tensor_copy(o, i_), reads=[f"{name}b6"], writes=[wrep.tag(0)])
            for ms in range(NHC // 2):
                par = ms % 2
                mk.dma("gpsimd", w1s.h[:, :].rearrange("p (c n) -> p c n", c=8)[:, :, par * 256:(par + 1) * 256],
                       fm_hbm_view(w1_h[ex, :, :])[:, :, ms * 256:(ms + 1) * 256],
                       writes=[f"{name}w1s.{par}"])
                mk.dma("gpsimd", w2s.h[:, :].rearrange("p (c n) -> p c n", c=8)[:, :, par * 256:(par + 1) * 256],
                       fm_hbm_view(w2_h[ex, :, :])[:, :, ms * 256:(ms + 1) * 256],
                       writes=[f"{name}w2s.{par}"])
                for mi in range(2):
                    m = ms * 2 + mi
                    for tc in range(QT // TC):
                        gi = (m * 2 + tc) % 2
                        pg, gtag = pb[0 + gi], f"{name}b{0 + gi}"
                        pu, utag = pb[2 + gi], f"{name}b{2 + gi}"
                        for k in range(NCH):
                            mk.op("tensor", lambda e, o=pg[:, 0:TC],
                                  l=w1s.h[:, (k * 512) + par * 256 + mi * 128: (k * 512) + par * 256 + mi * 128 + 128],
                                  r=xn2.ap(k, tc * TC, (tc + 1) * TC),
                                  s=(k == 0), p=(k == NCH - 1): e.matmul(o, l, r, start=s, stop=p),
                                  reads=[f"{name}w1s.{par}", xn2.tag(k)], writes=[gtag])
                        for k in range(NCH):
                            mk.op("tensor", lambda e, o=pu[:, 0:TC],
                                  l=w2s.h[:, (k * 512) + par * 256 + mi * 128: (k * 512) + par * 256 + mi * 128 + 128],
                                  r=xn2.ap(k, tc * TC, (tc + 1) * TC),
                                  s=(k == 0), p=(k == NCH - 1): e.matmul(o, l, r, start=s, stop=p),
                                  reads=[f"{name}w2s.{par}", xn2.tag(k)], writes=[utag])
                        hpar = (m * 2 + tc) % 2
                        mk.op("scalar", lambda e, o=hs.ap(hpar), i_=pg[:, 0:TC]:
                              e.activation(o, i_, AF.Silu),
                              reads=[gtag], writes=[hs.tag(hpar)])
                        mk.op("vector", lambda e, o=hu.ap(hpar), a=pu[:, 0:TC], b=hs.ap(hpar):
                              e.tensor_mul(o, a, b),
                              reads=[utag, hs.tag(hpar)], writes=[hu.tag(hpar)])
                        mk.op("gpsimd", lambda e, o=hh.ap(m, tc * TC, (tc + 1) * TC),
                              a=hu.ap(hpar), b=wrep.ap(0, tc * TC, (tc + 1) * TC):
                              e.tensor_mul(o, a, b),
                              reads=[hu.tag(hpar), wrep.tag(0)], writes=[hh.tag(m)])
            for m2 in range(NCH):
                par = m2 % 2
                mk.dma("gpsimd", w3s.h[:, :].rearrange("p (c n) -> p c n", c=16)[:, :, par * 128:(par + 1) * 128],
                       fm_hbm_view(w3_h[ex, :, :])[:, :, m2 * 128:(m2 + 1) * 128],
                       writes=[f"{name}w3s.{par}"])
                for tc in range(QT // TC):
                    db = 4 + (m2 * 2 + tc) % 2
                    pd, dtag = pb[db], f"{name}b{db}"
                    for k2 in range(NHC):
                        mk.op("tensor", lambda e, o=pd[:, 0:TC],
                              l=w3s.h[:, k2 * 256 + par * 128: k2 * 256 + par * 128 + 128],
                              r=hh.ap(k2, tc * TC, (tc + 1) * TC),
                              s=(k2 == 0), p=(k2 == NHC - 1): e.matmul(o, l, r, start=s, stop=p),
                              reads=[f"{name}w3s.{par}", hh.tag(k2)], writes=[dtag])
                    mk.op("vector", lambda e, o=acc.ap(m2, tc * TC, (tc + 1) * TC), a=pd[:, 0:TC],
                          b=acc.ap(m2, tc * TC, (tc + 1) * TC): e.tensor_add(o, a, b),
                          reads=[dtag, acc.tag(m2)], writes=[acc.tag(m2)])
        mk.dma("scalar", fm_hbm_view(xout_h)[:, :, t0:t0 + QT], acc.ap3(),
               reads=acc.tags(), writes=[f"{xout_tag}.q{q}"])
        out_tags.append(f"{xout_tag}.q{q}")
    return out_tags


def build_moe_program():
    nc = bass.Bass("TRN2", target_bir_lowering=False)
    mk = MK(nc)
    x = mk.dram("xt", [1024, 4096], F32, kind="ExternalInput")
    y2 = mk.dram("y2t", [1024, 4096], F32, kind="ExternalInput")
    gw = mk.dram("gw", [1024, 8], F32, kind="ExternalInput")
    w1 = mk.dram("w1", [8, 1024, 2048], F32, kind="ExternalInput")
    w2 = mk.dram("w2", [8, 1024, 2048], F32, kind="ExternalInput")
    w3 = mk.dram("w3", [8, 2048, 1024], F32, kind="ExternalInput")
    ln2s = mk.dram("ln2s", [128, 8], F32, kind="ExternalInput")
    ln2b = mk.dram("ln2b", [128, 8], F32, kind="ExternalInput")
    idf = mk.dram("idf", [128, 128], F32, kind="ExternalInput")
    xo = mk.dram("xo", [1024, 4096], F32, kind="ExternalOutput")
    tags = emit_moe(mk, x, "hbm.xt", y2, "hbm.y2t", xo, "hbm.xo",
                    gw, w1, w2, w3, ln2s, ln2b, idf)
    mk.barrier_read(tags)
    mk.build()
    return nc



def emit_pmx_a(mk, x_h, x_tag, o_hbm, o_tag, wq_h, wk_h, wv_h, bq_h, bk_h, bv_h,
               ph_h, idf_h, name="pa"):
    DM, T, HD = 1024, 4096, 4096
    NCH, NQC = DM // 128, HD // 128
    TC = 256
    NT = T // TC

    x = T2(mk, name + "x", DM, TC, F32)
    xb = T2(mk, name + "xb", DM, TC, BF16)
    xn = T2(mk, name + "xn", DM, TC, BF16)
    u = T2(mk, name + "u", DM, TC, F32)
    rep = T2(mk, name + "rep", 128, 2 * TC, F32)
    rows = T2(mk, name + "rows", 1, 6 * TC, F32)
    qq = T2(mk, name + "qq", HD, TC, BF16)
    kk = T2(mk, name + "kk", HD, TC, BF16)
    vv = T2(mk, name + "vv", HD, TC, BF16)
    ws = [T2(mk, name + f"ws{i}", DM, 2 * 512, BF16) for i in range(3)]
    bcols = T2(mk, name + "bcols", 128, 3 * 32, F32)
    phc = T2(mk, name + "phc", 128, 32, F32)
    cph = T2(mk, name + "cph", 128, 32, F32)
    sph = T2(mk, name + "sph", 128, 32, F32)
    ct = T2(mk, name + "ct", 128, 16 * 8, F32)
    ctb = T2(mk, name + "ctb", 128, 16 * 8, BF16)
    pp = T2(mk, name + "pp", 128, 2 * TC, BF16)
    srow = T2(mk, name + "srow", 1, 16 * TC, F32)
    s16 = T2(mk, name + "s16", 16, TC, F32)
    stok = T2(mk, name + "stok", 128, 2 * 16, F32)
    atok = T2(mk, name + "atok", 128, 2 * 16, F32)
    arow = T2(mk, name + "arow", 16, TC, F32)
    arow0 = T2(mk, name + "arow0", 1, 16 * TC, F32)
    repw = T2(mk, name + "repw", 128, 2 * TC, F32)
    oo = T2(mk, name + "oo", HD, TC, BF16)
    tmp = T2(mk, name + "tmp", 128, 2 * TC, BF16)
    idf = T2(mk, name + "idf", 128, 128, F32)
    onescol = T2(mk, name + "onc", 128, 1, BF16)
    onesrow_f = T2(mk, name + "onr", 1, 128, F32)

    pb = [mk.psum(f"{name}pb{i}", [128, 512], F32) for i in range(8)]
    B = lambda i: (pb[i], f"{name}b{i}")

    mk.op("gpsimd", lambda e: e.memset(onescol.h[:, :], 1.0), writes=[onescol.tag()])
    mk.op("gpsimd", lambda e: e.memset(onesrow_f.h[:, :], 1.0), writes=[onesrow_f.tag()])
    mk.dma("sync", idf.h[:, :], idf_h, writes=[idf.tag()])
    mk.dma("sync", bcols.h[:, 0:32], bq_h, writes=[bcols.tag()])
    mk.dma("sync", bcols.h[:, 32:64], bk_h, writes=[bcols.tag()])
    mk.dma("sync", bcols.h[:, 64:96], bv_h, writes=[bcols.tag()])
    mk.dma("sync", phc.h[:, :], ph_h, writes=[phc.tag()])
    chalfpi, thalfpi = mk.constcol(np.pi / 2)
    mk.op("scalar", lambda e, o=cph.h[:, :], i_=phc.h[:, :], b=chalfpi:
          e.activation(o, i_, AF.Sin, bias=b), reads=[phc.tag(), thalfpi], writes=[cph.tag()])
    mk.op("scalar", lambda e, o=sph.h[:, :], i_=phc.h[:, :]:
          e.activation(o, i_, AF.Sin), reads=[phc.tag()], writes=[sph.tag()])
    for h in range(4):
        for g in range(4):
            hg = 4 * h + g
            mk.op("vector", lambda e, o=ct.ap(0, hg * 8, hg * 8 + 8),
                  a=cph.h[:, 8 * h:8 * h + 8], b=cph.h[:, 8 * g:8 * g + 8]:
                  e.tensor_mul(o, a, b), reads=[cph.tag()], writes=[ct.tag()])
            mk.op("vector", lambda e, o=pp.h[:, 0:8],
                  a=sph.h[:, 8 * h:8 * h + 8], b=sph.h[:, 8 * g:8 * g + 8]:
                  e.tensor_mul(o, a, b), reads=[sph.tag()], writes=[pp.tag(0)])
            mk.op("vector", lambda e, o=ct.ap(0, hg * 8, hg * 8 + 8),
                  a=ct.ap(0, hg * 8, hg * 8 + 8), b=pp.h[:, 0:8]:
                  e.tensor_add(o, a, b), reads=[ct.tag(), pp.tag(0)], writes=[ct.tag()])
    mk.op("vector", lambda e, o=ctb.h[:, :], i_=ct.h[:, :]: e.tensor_copy(o, i_),
          reads=[ct.tag()], writes=[ctb.tag()])

    WSL = 512
    for it in range(NT):
        t0 = it * TC
        mk.dma("sync", x.ap3(), fm_hbm_view(x_h)[:, :, t0:t0 + TC],
               reads=[x_tag], writes=x.tags())
        emit_ln(mk, x, xb, xn, u, rows, rep, onescol, onesrow_f,
                B(4), B(5), B(6), B(7), TC)
        for wi, (wh, out_t, bofs) in enumerate(((wq_h, qq, 0), (wk_h, kk, 32), (wv_h, vv, 64))):
            for sl in range(HD // WSL):
                par = sl % 2
                wtag = f"{name}ws{wi}.{par}"
                mk.dma("gpsimd", ws[wi].h[:, :].rearrange("p (c n) -> p c n", c=8)[:, :, par * WSL:(par + 1) * WSL],
                       fm_hbm_view(wh)[:, :, sl * WSL:(sl + 1) * WSL], writes=[wtag])
                for mi in range(WSL // 128):
                    m = sl * (WSL // 128) + mi
                    bk = m % 2
                    pg, gtag = pb[bk], f"{name}b{bk}"
                    for k in range(NCH):
                        mk.op("tensor", lambda e, o=pg[:, 0:TC],
                              l=ws[wi].h[:, k * 1024 + par * WSL + mi * 128: k * 1024 + par * WSL + mi * 128 + 128],
                              r=xn.ap(k), s=(k == 0), p=(k == NCH - 1):
                              e.matmul(o, l, r, start=s, stop=p),
                              reads=[wtag, xn.tag(k)], writes=[gtag])
                    mk.op("scalar", lambda e, o=out_t.ap(m), i_=pg[:, 0:TC],
                          b=bcols.h[:, bofs + m:bofs + m + 1]:
                          e.activation(o, i_, AF.Identity, bias=b),
                          reads=[gtag, bcols.tag()], writes=[out_t.tag(m)])
        for h in range(4):
            for g in range(4):
                hg = 4 * h + g
                bk = 2 + hg % 2
                ps, pstag = pb[bk], f"{name}b{bk}"
                ppt = pp.ap(0, (hg % 2) * TC, (hg % 2) * TC + TC)
                pptag = pp.tag(0) + f".{hg % 2}"
                for k in range(NCH):
                    mk.op("vector", lambda e, o=ppt, a=qq.ap(8 * h + k), b=kk.ap(8 * g + k):
                          e.tensor_mul(o, a, b),
                          reads=[qq.tag(8 * h + k), kk.tag(8 * g + k)], writes=[pptag])
                    mk.op("tensor", lambda e, o=ps[0:1, 0:TC], l=ctb.ap(0, hg * 8 + k, hg * 8 + k + 1),
                          r=ppt, s=(k == 0), p=(k == NCH - 1):
                          e.matmul(o, l, r, start=s, stop=p),
                          reads=[ctb.tag(), pptag], writes=[pstag])
                mk.op("vector", lambda e, o=srow.ap(0, hg * TC, (hg + 1) * TC), i_=ps[0:1, 0:TC],
                      s=1.0 / 32.0: e.tensor_scalar(o, i_, s, None, ALU.mult),
                      reads=[pstag], writes=[srow.tag(0)])
        mk.dma("gpsimd", s16.h[:, :],
               srow.h[0:1, :].rearrange("o (hg t) -> o hg t", hg=16),
               reads=[srow.tag(0)], writes=[s16.tag(0)])
        for st in range(TC // 128):
            mk.op("tensor", lambda e, o=pb[6][0:128, 0:16],
                  i_=s16.ap(0, st * 128, (st + 1) * 128), idn=idf.h[0:16, 0:16]:
                  e.transpose(o, i_, idn), reads=[s16.tag(0), idf.tag()], writes=[f"{name}b6"])
            mk.op("vector", lambda e, o=stok.ap(0, st * 16, (st + 1) * 16), i_=pb[6][0:128, 0:16]:
                  e.tensor_copy(o, i_), reads=[f"{name}b6"], writes=[stok.tag(0) + f".{st}"])
        for st in range(TC // 128):
            sap = stok.ap(0, st * 16, (st + 1) * 16)
            stag = stok.tag(0) + f".{st}"
            aap = atok.ap(0, st * 16, (st + 1) * 16)
            atag = atok.tag(0) + f".{st}"
            m4 = rows.ap(0, 5 * TC + st * 8, 5 * TC + st * 8 + 4)
            mx = repw.ap(0, 0, 4); mxt = f"{name}repw.mx"
            sm = repw.ap(0, 4, 8); smt = f"{name}repw.sm"
            mk.op("vector", lambda e, o=mx.rearrange("p (h o) -> p h o", h=4),
                  i_=sap.rearrange("p (h g) -> p h g", h=4):
                  e.reduce_max(o, i_, AX.X), reads=[stag], writes=[mxt])
            for g in range(4):
                mk.op("vector", lambda e, o=aap[:, g:16:4], a=sap[:, g:16:4], b=mx:
                      e.tensor_sub(o, a, b), reads=[stag, mxt], writes=[atag])
            mk.op("scalar", lambda e, o=aap, i_=aap: e.activation(o, i_, AF.Exp),
                  reads=[atag], writes=[atag])
            mk.op("vector", lambda e, o=sm.rearrange("p (h o) -> p h o", h=4),
                  i_=aap.rearrange("p (h g) -> p h g", h=4):
                  e.reduce_sum(o, i_, AX.X), reads=[atag], writes=[smt])
            mk.op("vector", lambda e, o=sm, i_=sm: e.reciprocal(o, i_),
                  reads=[smt], writes=[smt])
            for g in range(4):
                mk.op("vector", lambda e, o=aap[:, g:16:4], a=aap[:, g:16:4], b=sm:
                      e.tensor_mul(o, a, b), reads=[atag, smt], writes=[atag])
        for st in range(TC // 128):
            mk.op("tensor", lambda e, o=pb[7][0:16, 0:128],
                  i_=atok.ap(0, st * 16, (st + 1) * 16), idn=idf.h[:, :]:
                  e.transpose(o, i_, idn),
                  reads=[atok.tag(0) + f".{st}", idf.tag()], writes=[f"{name}b7"])
            mk.op("vector", lambda e, o=arow.ap(0, st * 128, (st + 1) * 128, p0=0, p1=16),
                  i_=pb[7][0:16, 0:128]: e.tensor_copy(o, i_),
                  reads=[f"{name}b7"], writes=[arow.tag(0)])
        mk.dma("gpsimd", arow0.h[0:1, :].rearrange("o (hg t) -> o hg t", hg=16),
               arow.h[:, :], reads=[arow.tag(0)], writes=[arow0.tag(0)])
        for h in range(4):
            for g in range(4):
                hg = 4 * h + g
                par = hg % 2
                rbk = 2 + par
                mk.op("tensor", lambda e, o=pb[rbk][:, 0:TC], l=onesrow_f.ap(),
                      r=arow0.ap(0, hg * TC, (hg + 1) * TC):
                      e.matmul(o, l, r, start=True, stop=True),
                      reads=[onesrow_f.tag(), arow0.tag(0)], writes=[f"{name}b{rbk}"])
                mk.op("vector", lambda e, o=repw.ap(0, par * TC, (par + 1) * TC), i_=pb[rbk][:, 0:TC]:
                      e.tensor_copy(o, i_), reads=[f"{name}b{rbk}"], writes=[f"{name}repw.{par}"])
                for dch in range(NCH):
                    oc = 8 * h + dch
                    if g == 0:
                        mk.op("gpsimd", lambda e, o=oo.ap(oc), a=vv.ap(8 * g + dch),
                              b=repw.ap(0, par * TC, (par + 1) * TC): e.tensor_mul(o, a, b),
                              reads=[vv.tag(8 * g + dch), f"{name}repw.{par}"], writes=[oo.tag(oc)])
                    else:
                        tpar = dch % 2
                        mk.op("gpsimd", lambda e, o=tmp.ap(0, tpar * TC, (tpar + 1) * TC),
                              a=vv.ap(8 * g + dch), b=repw.ap(0, par * TC, (par + 1) * TC):
                              e.tensor_mul(o, a, b),
                              reads=[vv.tag(8 * g + dch), f"{name}repw.{par}"],
                              writes=[tmp.tag(0) + f".{tpar}"])
                        mk.op("vector", lambda e, o=oo.ap(oc), a=oo.ap(oc),
                              b=tmp.ap(0, tpar * TC, (tpar + 1) * TC): e.tensor_add(o, a, b),
                              reads=[oo.tag(oc), tmp.tag(0) + f".{tpar}"], writes=[oo.tag(oc)])
        mk.dma("scalar",
               o_hbm.rearrange("h (c p) s -> p h c s", p=128)[:, :, :, t0:t0 + TC],
               oo.h[:, :].rearrange("p (h c t) -> p h c t", h=4, c=8),
               reads=oo.tags(), writes=[f"{o_tag}.{it}"])
    return [f"{o_tag}.{it}" for it in range(NT)]



def emit_pmx_b(mk, x_h, x_tag, o_hbm, o_tag, xout_h, xout_tag, wo_h, bo_h, name="pb"):
    DM, T = 1024, 4096
    NCH = DM // 128
    wo = T2(mk, name + "wo", 4096, DM, BF16)
    ow = T2(mk, name + "ow", DM, 2048, BF16)
    xc = T2(mk, name + "xc", DM, 512, F32)
    bo = T2(mk, name + "bo", 128, 8, F32)
    pb = [mk.psum(f"{name}pb{i}", [128, 512], F32) for i in range(8)]

    mk.dma("gpsimd", wo.ap3(), fm_hbm_view(wo_h), writes=wo.tags())
    mk.dma("sync", bo.h[:, :], bo_h, writes=[bo.tag()])

    for h in range(4):
        for qw in range(2):
            q0 = h * 1024 + qw * 512
            s0 = qw * 2048
            mk.dma("sync", ow.ap3(), o_hbm[h, :, :].rearrange("(c p) s -> p c s", p=128)[:, :, s0:s0 + 2048],
                   reads=[o_tag], writes=ow.tags())
            mk.dma("scalar", xc.ap3(), fm_hbm_view(x_h)[:, :, q0:q0 + 512],
                   reads=[x_tag], writes=xc.tags())
            for n in range(NCH):
                bk = n % 2
                ps, pstag = pb[bk], f"{name}b{bk}"
                first = True
                for r in range(4):
                    for dch in range(NCH):
                        jc = 8 * r + dch
                        mk.op("tensor", lambda e, o=ps[:, 0:512],
                              l=wo.ap(jc, n * 128, (n + 1) * 128),
                              r=ow.h[:, dch * 2048 + r: (dch + 1) * 2048: 4],
                              s=first, p=(r == 3 and dch == NCH - 1):
                              e.matmul(o, l, r, start=s, stop=p),
                              reads=[wo.tag(jc), ow.tag(dch)], writes=[pstag])
                        first = False
                mk.op("vector", lambda e, o=xc.ap(n), a=ps[:, 0:512], b=xc.ap(n),
                      s=bo.h[:, n:n + 1]:
                      e.scalar_tensor_tensor(o, a, s, b, ALU.add, ALU.add),
                      reads=[pstag, xc.tag(n), bo.tag()], writes=[xc.tag(n)])
            mk.dma("scalar", fm_hbm_view(xout_h)[:, :, q0:q0 + 512], xc.ap3(),
                   reads=xc.tags(), writes=[f"{xout_tag}.{h}.{qw}"])
    return [f"{xout_tag}.{h}.{qw}" for h in range(4) for qw in range(2)]


def build_pmx_a_program():
    nc = bass.Bass("TRN2", target_bir_lowering=False)
    mk = MK(nc)
    x = mk.dram("xt", [1024, 4096], F32, kind="ExternalInput")
    wq = mk.dram("wq", [1024, 4096], F32, kind="ExternalInput")
    wk = mk.dram("wk", [1024, 4096], F32, kind="ExternalInput")
    wv = mk.dram("wv", [1024, 4096], F32, kind="ExternalInput")
    bq = mk.dram("bq", [128, 32], F32, kind="ExternalInput")
    bk = mk.dram("bk", [128, 32], F32, kind="ExternalInput")
    bv = mk.dram("bv", [128, 32], F32, kind="ExternalInput")
    ph = mk.dram("ph", [128, 32], F32, kind="ExternalInput")
    idf = mk.dram("idf", [128, 128], F32, kind="ExternalInput")
    oh = mk.dram("oh", [4, 1024, 4096], BF16, kind="ExternalOutput")
    tags = emit_pmx_a(mk, x, "hbm.xt", oh, "hbm.oh", wq, wk, wv, bq, bk, bv, ph, idf)
    mk.barrier_read(tags)
    mk.build()
    return nc


def build_pmx_b_program():
    nc = bass.Bass("TRN2", target_bir_lowering=False)
    mk = MK(nc)
    x = mk.dram("xt", [1024, 4096], F32, kind="ExternalInput")
    oh = mk.dram("oh", [4, 1024, 4096], BF16, kind="ExternalInput")
    wo = mk.dram("wo", [4096, 1024], F32, kind="ExternalInput")
    bo = mk.dram("bo", [128, 8], F32, kind="ExternalInput")
    xo = mk.dram("xo", [1024, 4096], F32, kind="ExternalOutput")
    tags = emit_pmx_b(mk, x, "hbm.xt", oh, "hbm.oh", xo, "hbm.xo", wo, bo)
    mk.barrier_read(tags)
    mk.build()
    return nc



def emit_ma1(mk, x_h, x_tag, y_h, y_tag, q_h, q_tag, k_h, k_tag, vt_h, vt_tag,
             wq_h, wk_h, wv_h, ln1s_h, ln1b_h, idb_h, name="n1"):
    DM, T = 1024, 4096
    NCH = DM // 128
    TC = 512
    NT = T // TC

    x = T2(mk, name + "x", DM, TC, F32)
    xb = T2(mk, name + "xb", DM, TC, BF16)
    yb = T2(mk, name + "yb", DM, TC, F32)
    h1 = T2(mk, name + "h1", DM, TC, BF16)
    u = T2(mk, name + "u", DM, TC, F32)
    u2 = T2(mk, name + "u2", DM, TC, F32)
    rep = T2(mk, name + "rep", 128, 2 * TC, F32)
    rep2 = T2(mk, name + "rep2", 128, 2 * TC, F32)
    rows = T2(mk, name + "rows", 1, 6 * TC, F32)
    rows2 = T2(mk, name + "rows2", 1, 6 * TC, F32)
    qq = T2(mk, name + "qq", DM, TC, BF16)
    kk = T2(mk, name + "kk", DM, TC, BF16)
    vvf = T2(mk, name + "vvf", DM, TC, BF16)
    vtk = T2(mk, name + "vtk", TC, DM, BF16)
    wq = T2(mk, name + "wq", DM, DM, BF16)
    wk = T2(mk, name + "wk", DM, DM, BF16)
    wv = T2(mk, name + "wv", DM, DM, BF16)
    ln1s = T2(mk, name + "ln1s", 128, 8, F32)
    ln1b = T2(mk, name + "ln1b", 128, 8, F32)
    idb = T2(mk, name + "idb", 128, 128, BF16)
    onescol = T2(mk, name + "onc", 128, 1, BF16)
    onesrow_f = T2(mk, name + "onr", 1, 128, F32)

    pb = [mk.psum(f"{name}pb{i}", [128, 512], F32) for i in range(6)]
    pbt = mk.psum(f"{name}pbt", [128, 512], BF16)
    B = lambda i: (pb[i], f"{name}b{i}")

    mk.op("gpsimd", lambda e: e.memset(onescol.h[:, :], 1.0), writes=[onescol.tag()])
    mk.op("gpsimd", lambda e: e.memset(onesrow_f.h[:, :], 1.0), writes=[onesrow_f.tag()])
    mk.dma("sync", idb.h[:, :], idb_h, writes=[idb.tag()])
    mk.dma("sync", ln1s.h[:, :], ln1s_h, writes=[ln1s.tag()])
    mk.dma("sync", ln1b.h[:, :], ln1b_h, writes=[ln1b.tag()])
    for w, hbm in ((wq, wq_h), (wk, wk_h), (wv, wv_h)):
        mk.dma("gpsimd", w.ap3(), fm_hbm_view(hbm), writes=w.tags())
    sc_cols = [ln1s.h[:, c:c+1] for c in range(NCH)]
    sh_cols = [ln1b.h[:, c:c+1] for c in range(NCH)]

    for it in range(NT):
        t0 = it * TC
        mk.dma("sync", x.ap3(), fm_hbm_view(x_h)[:, :, t0:t0 + TC],
               reads=[x_tag], writes=x.tags())
        emit_ln(mk, x, xb, h1, u, rows, rep, onescol, onesrow_f,
                B(2), B(3), B(4), B(5), TC)
        REP_R = rep.ap(0, 0, TC)
        for c in range(NCH):
            mk.op("gpsimd", lambda e, o=yb.ap(c), a=u.ap(c), b=REP_R: e.tensor_mul(o, a, b),
                  reads=[u.tag(c), f"{name}rep.r"], writes=[yb.tag(c)])
        mk.dma("scalar", fm_hbm_view(y_h)[:, :, t0:t0 + TC], yb.ap3(),
               reads=yb.tags(), writes=[f"{y_tag}.{it}"])
        emit_ln(mk, yb, xb, h1, u2, rows2, rep2, onescol, onesrow_f,
                B(2), B(3), B(4), B(5), TC,
                scale_col=sc_cols, shift_col=sh_cols, sc_tags=[ln1s.tag(), ln1b.tag()])
        for w, out_t in ((wq, qq), (wk, kk), (wv, vvf)):
            for m in range(NCH):
                bk = m % 2
                pg, gtag = pb[bk], f"{name}b{bk}"
                for k in range(NCH):
                    mk.op("tensor", lambda e, o=pg[:, 0:TC], l=w.ap(k, m * 128, (m + 1) * 128),
                          r=h1.ap(k), s=(k == 0), p=(k == NCH - 1):
                          e.matmul(o, l, r, start=s, stop=p),
                          reads=[w.tag(k), h1.tag(k)], writes=[gtag])
                mk.op("scalar", lambda e, o=out_t.ap(m), i_=pg[:, 0:TC]:
                      e.activation(o, i_, AF.Identity),
                      reads=[gtag], writes=[out_t.tag(m)])
        mk.dma("scalar", fm_hbm_view(q_h)[:, :, t0:t0 + TC], qq.ap3(),
               reads=qq.tags(), writes=[f"{q_tag}.{it}"])
        mk.dma("scalar", fm_hbm_view(k_h)[:, :, t0:t0 + TC], kk.ap3(),
               reads=kk.tags(), writes=[f"{k_tag}.{it}"])
        for c in range(NCH):
            for tt in range(TC // 128):
                mk.op("tensor", lambda e, o=pbt[:, 0:128],
                      i_=vvf.ap(c, tt * 128, (tt + 1) * 128), idn=idb.h[:, :]:
                      e.transpose(o, i_, idn),
                      reads=[vvf.tag(c), idb.tag()], writes=[f"{name}bt"])
                mk.op("vector", lambda e, o=vtk.ap(tt, c * 128, (c + 1) * 128), i_=pbt[:, 0:128]:
                      e.tensor_copy(o, i_), reads=[f"{name}bt"], writes=[vtk.tag(tt)])
        mk.dma("scalar", vt_h[t0:t0 + TC, :].rearrange("(tc p) c -> p tc c", p=128),
               vtk.ap3(), reads=vtk.tags(), writes=[f"{vt_tag}.{it}"])
    return ([f"{y_tag}.{it}" for it in range(NT)] + [f"{q_tag}.{it}" for it in range(NT)]
            + [f"{k_tag}.{it}" for it in range(NT)] + [f"{vt_tag}.{it}" for it in range(NT)])



def emit_ma2(mk, q_h, q_tag, k_h, k_tag, vt_h, vt_tag, ctx_h, ctx_tag, cm_h, name="n2"):
    DM, T, NH, HK = 1024, 4096, 16, 64
    RT = 512
    NR = T // RT
    NSC = T // 128

    kT = T2(mk, name + "kT", 64, T, BF16)
    qT = T2(mk, name + "qT", 64, T, BF16)
    vtk = T2(mk, name + "vtk", T, DM, BF16)
    ksq = T2(mk, name + "ksq", 64, T, BF16)
    vstat = T2(mk, name + "vstat", 128, 2 * 65, BF16)
    at = T2(mk, name + "at", 128, NSC * RT, BF16)
    nrow = T2(mk, name + "nrow", 1, 3 * T, F32)
    brow = T2(mk, name + "brow", 128, 4, F32)
    rrow = T2(mk, name + "rrow", 1, RT, F32)
    rrep = T2(mk, name + "rrep", 64, RT, F32)
    ctxt = T2(mk, name + "ctxt", 64, RT, BF16)
    cmask = T2(mk, name + "cmask", 128, 4 * RT, BF16)
    onescol = T2(mk, name + "onc", 128, 1, BF16)
    onesrow_f = T2(mk, name + "onr", 1, 128, F32)

    pb = [mk.psum(f"{name}pb{i}", [128, 512], F32) for i in range(8)]

    mk.op("gpsimd", lambda e: e.memset(onescol.h[:, :], 1.0), writes=[onescol.tag()])
    mk.op("gpsimd", lambda e: e.memset(onesrow_f.h[:, :], 1.0), writes=[onesrow_f.tag()])
    mk.dma("sync", cmask.h[:, :], cm_h, writes=[cmask.tag(0)])

    mk.dma("sync", vtk.ap3(), fm_hbm_view(vt_h), writes=vtk.tags())
    out_tags = []
    for h in range(NH):
        mk.dma("sync", kT.h[:, :], k_h[64 * h:64 * h + 64, :], writes=kT.tags())
        mk.dma("sync", qT.h[:, :], q_h[64 * h:64 * h + 64, :], writes=qT.tags())
        mk.op("scalar", lambda e, o=ksq.h[:, :], i_=kT.h[:, :]: e.activation(o, i_, AF.Square),
              reads=kT.tags(), writes=ksq.tags())
        for tc in range(T // 512):
            mk.op("tensor", lambda e, o=pb[6][0:1, 0:512], l=onescol.ap(p1=64),
                  r=ksq.h[:, tc * 512:(tc + 1) * 512], s=True, p=True:
                  e.matmul(o, l, r, start=s, stop=p),
                  reads=ksq.tags() + [onescol.tag()], writes=[f"{name}b6"])
            mk.op("vector", lambda e, o=nrow.ap(0, tc * 512, (tc + 1) * 512), i_=pb[6][0:1, 0:512]:
                  e.tensor_copy(o, i_), reads=[f"{name}b6"], writes=[nrow.tag(0) + ".k"])
        mk.op("scalar", lambda e, o=ksq.h[:, :], i_=qT.h[:, :]: e.activation(o, i_, AF.Square),
              reads=qT.tags(), writes=ksq.tags())
        for tc in range(T // 512):
            mk.op("tensor", lambda e, o=pb[7][0:1, 0:512], l=onescol.ap(p1=64),
                  r=ksq.h[:, tc * 512:(tc + 1) * 512], s=True, p=True:
                  e.matmul(o, l, r, start=s, stop=p),
                  reads=ksq.tags() + [onescol.tag()], writes=[f"{name}b7"])
            mk.op("vector", lambda e, o=nrow.ap(0, T + tc * 512, T + (tc + 1) * 512),
                  i_=pb[7][0:1, 0:512]:
                  e.tensor_copy(o, i_), reads=[f"{name}b7"], writes=[nrow.tag(0) + ".q"])
        NR1, NR2, NR3 = nrow.ap(0, 0, T), nrow.ap(0, T, 2 * T), nrow.ap(0, 2 * T, 2 * T + 4)
        mk.op("vector", lambda e, o=nrow.ap(0, 2 * T, 2 * T + 1), i_=NR1:
              e.reduce_max(o, i_, AX.X), reads=[nrow.tag(0) + ".k"], writes=[nrow.tag(0) + ".mk"])
        mk.op("vector", lambda e, o=nrow.ap(0, 2 * T + 1, 2 * T + 2), i_=NR2:
              e.reduce_max(o, i_, AX.X), reads=[nrow.tag(0) + ".q"], writes=[nrow.tag(0) + ".mq"])
        mk.op("vector", lambda e, o=nrow.ap(0, 2 * T + 2, 2 * T + 3),
              a=nrow.ap(0, 2 * T, 2 * T + 1), b=nrow.ap(0, 2 * T + 1, 2 * T + 2):
              e.tensor_mul(o, a, b),
              reads=[nrow.tag(0) + ".mk", nrow.tag(0) + ".mq"], writes=[nrow.tag(0) + ".p"])
        mk.op("scalar", lambda e, o=nrow.ap(0, 2 * T + 3, 2 * T + 4),
              i_=nrow.ap(0, 2 * T + 2, 2 * T + 3):
              e.activation(o, i_, AF.Sqrt), reads=[nrow.tag(0) + ".p"], writes=[nrow.tag(0) + ".sq"])
        mk.op("vector", lambda e, o=nrow.ap(0, 2 * T + 2, 2 * T + 3),
              i_=nrow.ap(0, 2 * T + 3, 2 * T + 4):
              e.tensor_scalar(o, i_, -1.0 / 8.0, None, ALU.mult),
              reads=[nrow.tag(0) + ".sq"], writes=[nrow.tag(0) + ".nb"])
        mk.op("tensor", lambda e, o=pb[6][0:128, 0:1], l=onesrow_f.ap(),
              r=nrow.ap(0, 2 * T + 2, 2 * T + 3):
              e.matmul(o, l, r, start=True, stop=True),
              reads=[onesrow_f.tag(), nrow.tag(0) + ".nb"], writes=[f"{name}b6"])
        mk.op("vector", lambda e, o=brow.h[:, 0:1], i_=pb[6][0:128, 0:1]: e.tensor_copy(o, i_),
              reads=[f"{name}b6"], writes=[brow.tag(0)])

        for R in range(NR):
            t0 = R * RT
            ncs = 4 * R + 4
            for C in range(ncs):
                bk = C % 2
                ps, pstag = pb[bk], f"{name}b{bk}"
                mk.op("tensor", lambda e, o=ps[:, 0:RT], l=kT.h[:, C * 128:(C + 1) * 128],
                      r=qT.h[:, t0:t0 + RT]: e.matmul(o, l, r, start=True, stop=True),
                      reads=kT.tags() + qT.tags(), writes=[pstag])
                mk.op("scalar", lambda e, o=at.ap(0, C * RT, (C + 1) * RT), i_=ps[:, 0:RT],
                      b=brow.h[:, 0:1], s=1.0 / 8.0:
                      e.activation(o, i_, AF.Exp, bias=b, scale=s),
                      reads=[pstag, brow.tag(0)], writes=[at.tag(0) + f".{C}"])
            for j in range(4):
                C = 4 * R + j
                mk.op("vector", lambda e, o=at.ap(0, C * RT, (C + 1) * RT),
                      a=at.ap(0, C * RT, (C + 1) * RT), b=cmask.ap(0, j * RT, (j + 1) * RT):
                      e.tensor_mul(o, a, b),
                      reads=[at.tag(0) + f".{C}", cmask.tag(0)], writes=[at.tag(0) + f".{C}"])
            for C in range(ncs):
                par = C % 2
                vst = vstat.ap(0, par * 65, par * 65 + 65)
                vsttag = f"{name}vstat.{par}"
                mk.op("vector", lambda e, o=vstat.ap(0, par * 65, par * 65 + 64),
                      i_=vtk.ap(C, 64 * h, 64 * h + 64): e.tensor_copy(o, i_),
                      reads=[vtk.tag(C)], writes=[vsttag])
                mk.op("gpsimd", lambda e, o=vstat.ap(0, par * 65 + 64, par * 65 + 65):
                      e.memset(o, 1.0), writes=[vsttag])
                mk.op("tensor", lambda e, o=pb[4][0:65, 0:RT], l=vst,
                      r=at.ap(0, C * RT, (C + 1) * RT), s=(C == 0), p=(C == ncs - 1):
                      e.matmul(o, l, r, start=s, stop=p),
                      reads=[vsttag, at.tag(0) + f".{C}"], writes=[f"{name}b4"])
            mk.op("vector", lambda e, o=rrow.ap(0), i_=pb[4][64:65, 0:RT]:
                  e.reciprocal(o, i_), reads=[f"{name}b4"], writes=[rrow.tag(0)])
            mk.op("tensor", lambda e, o=pb[5][0:64, 0:RT], l=onesrow_f.ap(ce=64),
                  r=rrow.ap(0): e.matmul(o, l, r, start=True, stop=True),
                  reads=[onesrow_f.tag(), rrow.tag(0)], writes=[f"{name}b5"])
            mk.op("vector", lambda e, o=rrep.h[:, :], i_=pb[5][0:64, 0:RT]: e.tensor_copy(o, i_),
                  reads=[f"{name}b5"], writes=[rrep.tag(0)])
            mk.op("vector", lambda e, o=ctxt.h[:, :], a=pb[4][0:64, 0:RT], b=rrep.h[:, :]:
                  e.tensor_mul(o, a, b),
                  reads=[f"{name}b4", rrep.tag(0)], writes=[ctxt.tag(0)])
            mk.dma("scalar", ctx_h[64 * h:64 * h + 64, t0:t0 + RT], ctxt.h[:, :],
                   reads=[ctxt.tag(0)], writes=[f"{ctx_tag}.{h}.{R}"])
            out_tags.append(f"{ctx_tag}.{h}.{R}")
    return out_tags



def emit_ma3(mk, y_h, y_tag, ctx_h, ctx_tag, y2_h, y2_tag, wo_h, bo_h, name="n3"):
    DM, T = 1024, 4096
    NCH = DM // 128
    TC = 512
    wo = T2(mk, name + "wo", DM, DM, BF16)
    cx = T2(mk, name + "cx", DM, TC, BF16)
    yc = T2(mk, name + "yc", DM, TC, F32)
    bo = T2(mk, name + "bo", 128, 8, F32)
    pb = [mk.psum(f"{name}pb{i}", [128, 512], F32) for i in range(4)]
    mk.dma("gpsimd", wo.ap3(), fm_hbm_view(wo_h), writes=wo.tags())
    mk.dma("sync", bo.h[:, :], bo_h, writes=[bo.tag()])
    tags = []
    for it in range(T // TC):
        t0 = it * TC
        mk.dma("sync", cx.ap3(), fm_hbm_view(ctx_h)[:, :, t0:t0 + TC],
               reads=[ctx_tag], writes=cx.tags())
        mk.dma("scalar", yc.ap3(), fm_hbm_view(y_h)[:, :, t0:t0 + TC],
               reads=[y_tag], writes=yc.tags())
        for m in range(NCH):
            bk = m % 2
            ps, pstag = pb[bk], f"{name}b{bk}"
            for k in range(NCH):
                mk.op("tensor", lambda e, o=ps[:, 0:TC], l=wo.ap(k, m * 128, (m + 1) * 128),
                      r=cx.ap(k), s=(k == 0), p=(k == NCH - 1):
                      e.matmul(o, l, r, start=s, stop=p),
                      reads=[wo.tag(k), cx.tag(k)], writes=[pstag])
            mk.op("vector", lambda e, o=yc.ap(m), a=ps[:, 0:TC], b=yc.ap(m), s=bo.h[:, m:m+1]:
                  e.scalar_tensor_tensor(o, a, s, b, ALU.add, ALU.add),
                  reads=[pstag, yc.tag(m), bo.tag()], writes=[yc.tag(m)])
        mk.dma("scalar", fm_hbm_view(y2_h)[:, :, t0:t0 + TC], yc.ap3(),
               reads=yc.tags(), writes=[f"{y2_tag}.{it}"])
        tags.append(f"{y2_tag}.{it}")
    return tags


def build_ma1_program():
    nc = bass.Bass("TRN2", target_bir_lowering=False)
    mk = MK(nc)
    x = mk.dram("xt", [1024, 4096], F32, kind="ExternalInput")
    wq = mk.dram("wq", [1024, 1024], F32, kind="ExternalInput")
    wk = mk.dram("wk", [1024, 1024], F32, kind="ExternalInput")
    wv = mk.dram("wv", [1024, 1024], F32, kind="ExternalInput")
    ln1s = mk.dram("ln1s", [128, 8], F32, kind="ExternalInput")
    ln1b = mk.dram("ln1b", [128, 8], F32, kind="ExternalInput")
    idb = mk.dram("idb", [128, 128], BF16, kind="ExternalInput")
    y = mk.dram("y", [1024, 4096], F32, kind="ExternalOutput")
    q = mk.dram("q", [1024, 4096], BF16, kind="ExternalOutput")
    k = mk.dram("k", [1024, 4096], BF16, kind="ExternalOutput")
    vt = mk.dram("vt", [4096, 1024], BF16, kind="ExternalOutput")
    tags = emit_ma1(mk, x, "hbm.xt", y, "hbm.y", q, "hbm.q", k, "hbm.k", vt, "hbm.vt",
                    wq, wk, wv, ln1s, ln1b, idb)
    mk.barrier_read(tags)
    mk.build()
    return nc


def build_ma2_program():
    nc = bass.Bass("TRN2", target_bir_lowering=False)
    mk = MK(nc)
    q = mk.dram("q", [1024, 4096], BF16, kind="ExternalInput")
    k = mk.dram("k", [1024, 4096], BF16, kind="ExternalInput")
    vt = mk.dram("vt", [4096, 1024], BF16, kind="ExternalInput")
    cm = mk.dram("cm", [128, 2048], BF16, kind="ExternalInput")
    ctx = mk.dram("ctx", [1024, 4096], BF16, kind="ExternalOutput")
    tags = emit_ma2(mk, q, "hbm.q", k, "hbm.k", vt, "hbm.vt", ctx, "hbm.ctx", cm)
    mk.barrier_read(tags)
    mk.build()
    return nc


def build_ma3_program():
    nc = bass.Bass("TRN2", target_bir_lowering=False)
    mk = MK(nc)
    y = mk.dram("y", [1024, 4096], F32, kind="ExternalInput")
    ctx = mk.dram("ctx", [1024, 4096], BF16, kind="ExternalInput")
    wo = mk.dram("wo", [1024, 1024], F32, kind="ExternalInput")
    bo = mk.dram("bo", [128, 8], F32, kind="ExternalInput")
    y2 = mk.dram("y2", [1024, 4096], F32, kind="ExternalOutput")
    tags = emit_ma3(mk, y, "hbm.y", ctx, "hbm.ctx", y2, "hbm.y2", wo, bo)
    mk.barrier_read(tags)
    mk.build()
    return nc



def emit_rw1(mk, x_h, x_tag, outs, prm, name="r1"):
    DM, T = 1024, 4096
    NCH = DM // 128
    TC = 512
    NT = T // TC

    x = T2(mk, name + "x", DM, TC, F32)
    xb = T2(mk, name + "xb", DM, TC, BF16)
    xn = T2(mk, name + "xn", DM, TC, BF16)
    u = T2(mk, name + "u", DM, TC, F32)
    rep = T2(mk, name + "rep", 128, 2 * TC, F32)
    rows = T2(mk, name + "rows", 1, 6 * TC, F32)
    xle = T2(mk, name + "xle", DM, TC + 2, BF16)
    dx = T2(mk, name + "dx", DM, TC, F32)
    xmix = T2(mk, name + "xmix", DM, TC, BF16)
    mixt = T2(mk, name + "mixt", 256, TC, BF16)
    wdec = T2(mk, name + "wdec", 64, TC, BF16)
    qout = T2(mk, name + "qout", DM, TC, BF16)
    cpre = T2(mk, name + "cpre", 16, T, F32)
    czer = T2(mk, name + "czer", 16, 512, F32)
    negd = T2(mk, name + "negd", 128, 8 * 16, BF16)
    combc = T2(mk, name + "combc", 128, 8, F32)
    hrow = T2(mk, name + "hrow", 1, 16, F32)
    maa = T2(mk, name + "maa", 128, 48, F32)
    mw1 = T2(mk, name + "mw1", DM, 160, BF16)
    mw2 = T2(mk, name + "mw2", 128, 5 * 1024, BF16)
    dw1 = T2(mk, name + "dw1", DM, 64, BF16)
    dw2 = T2(mk, name + "dw2", 64, DM, BF16)
    dcol = T2(mk, name + "dcol", 128, 8, F32)
    wr = T2(mk, name + "wr", DM, DM, BF16)
    wk = T2(mk, name + "wk", DM, DM, BF16)
    wv = T2(mk, name + "wv", DM, DM, BF16)
    wg = T2(mk, name + "wg", DM, DM, BF16)
    onescol = T2(mk, name + "onc", 128, 1, BF16)
    onesrow_f = T2(mk, name + "onr", 1, 128, F32)

    pb = [mk.psum(f"{name}pb{i}", [128, 512], F32) for i in range(6)]
    B = lambda i: (pb[i], f"{name}b{i}")

    mk.op("gpsimd", lambda e: e.memset(onescol.h[:, :], 1.0), writes=[onescol.tag()])
    mk.op("gpsimd", lambda e: e.memset(onesrow_f.h[:, :], 1.0), writes=[onesrow_f.tag()])
    mk.op("gpsimd", lambda e: e.memset(czer.h[:, :], 0.0), writes=[czer.tag(0)])
    mk.dma("sync", maa.h[:, :], prm["maa_cols"], writes=[maa.tag()])
    mk.dma("gpsimd", mw1.ap3(), fm_hbm_view(prm["maa_w1"]), writes=mw1.tags())
    MIXSL0 = [(0, 0), (0, 32), (0, 64), (1, 0), (1, 32)]
    for f in range(5):
        mofs = MIXSL0[f][1]
        mk.dma("gpsimd", mw2.h[mofs:mofs + 32, f * 1024:(f + 1) * 1024],
               prm["maa_w2"][f, :, :], writes=[mw2.tag(0)])
    mk.dma("gpsimd", dw1.ap3(), fm_hbm_view(prm["dw1"]), writes=dw1.tags())
    mk.dma("gpsimd", dw2.h[:, :], prm["dw2"], writes=[dw2.tag(0)])
    mk.dma("sync", dcol.h[:, :], prm["decay_cols"], writes=[dcol.tag()])
    for w, key in ((wr, "wr"), (wk, "wk"), (wv, "wv"), (wg, "wg")):
        mk.dma("gpsimd", w.ap3(), fm_hbm_view(prm[key]), writes=w.tags())
    mk.dma("gpsimd", hrow.h[0:1, :], prm["hdb"].rearrange("a b -> b a"), writes=[hrow.tag(0)])
    mk.op("tensor", lambda e, o=pb[0][:, 0:16], l=onesrow_f.ap(), r=hrow.ap(0):
          e.matmul(o, l, r, start=True, stop=True),
          reads=[onesrow_f.tag(), hrow.tag(0)], writes=[f"{name}b0"])
    for c in range(NCH):
        for j in range(2):
            mk.op("vector", lambda e, o=combc.h[64 * j:64 * j + 64, c:c + 1],
                  i_=pb[0][64 * j:64 * j + 64, 2 * c + j:2 * c + j + 1]: e.tensor_copy(o, i_),
                  reads=[f"{name}b0"], writes=[combc.tag()])
    mk.op("vector", lambda e, o=combc.h[:, :], a=combc.h[:, :], b=dcol.h[:, :]:
          e.tensor_add(o, a, b), reads=[combc.tag(), dcol.tag()], writes=[combc.tag()])
    mk.op("gpsimd", lambda e: e.memset(negd.h[:, :], 0.0), writes=[negd.tag()])
    for c in range(NCH):
        for j in range(2):
            mk.op("gpsimd", lambda e, o=negd.h[64 * j:64 * j + 64, c * 16 + 2 * c + j: c * 16 + 2 * c + j + 1]:
                  e.memset(o, -1.0 / 64.0), writes=[negd.tag()])

    MAAC = lambda i, c: maa.h[:, i * 8 + c:i * 8 + c + 1]

    for it in range(NT):
        t0 = it * TC
        mk.dma("sync", x.ap3(), fm_hbm_view(x_h)[:, :, t0:t0 + TC],
               reads=[x_tag], writes=x.tags())
        emit_ln(mk, x, xb, xn, u, rows, rep, onescol, onesrow_f,
                B(2), B(3), B(4), B(5), TC)
        mk.dma("scalar", fm_hbm_view(outs["xln"])[:, :, t0:t0 + TC], xn.ap3(),
               reads=xn.tags(), writes=[f"hbm.{name}xln.{it}"])

    for it in range(NT):
        t0 = it * TC
        lo = max(0, t0 - 1)
        hi = min(T, t0 + TC + 1)
        ofs = 1 if t0 == 0 else 0
        if t0 == 0:
            mk.op("gpsimd", lambda e: e.memset(xle.ap3(cs=0, ce=1), 0.0), writes=xle.tags())
        if hi == T:
            mk.op("gpsimd", lambda e: e.memset(xle.ap3(cs=TC + 1, ce=TC + 2), 0.0), writes=xle.tags())
        mk.dma("sync", xle.ap3(cs=ofs, ce=ofs + (hi - lo)),
               fm_hbm_view(outs["xln"])[:, :, lo:hi],
               reads=[f"hbm.{name}xln.{it2}" for it2 in range(NT)], writes=xle.tags())
        for c in range(NCH):
            mk.op("vector", lambda e, o=dx.ap(c), a=xle.ap(c, 0, TC), b=xle.ap(c, 2, TC + 2):
                  e.tensor_add(o, a, b), reads=xle.tags(), writes=[dx.tag(c)])
            mk.op("vector", lambda e, o=dx.ap(c), a=dx.ap(c), b=xle.ap(c, 1, TC + 1):
                  e.scalar_tensor_tensor(o, a, 0.5, b, ALU.mult, ALU.subtract),
                  reads=[dx.tag(c)] + xle.tags(), writes=[dx.tag(c)])
        for c in range(NCH):
            mk.op("vector", lambda e, o=xmix.ap(c), a=dx.ap(c), s=MAAC(0, c), b=xle.ap(c, 1, TC + 1):
                  e.scalar_tensor_tensor(o, a, s, b, ALU.mult, ALU.add),
                  reads=[dx.tag(c), maa.tag()] + xle.tags(), writes=[xmix.tag(c)])
        for part, (m0, mw) in enumerate(((0, 96), (96, 64))):
            bk = part
            for k in range(NCH):
                mk.op("tensor", lambda e, o=pb[bk][0:mw, 0:TC], l=mw1.ap(k, m0, m0 + mw),
                      r=xmix.ap(k), s=(k == 0), p=(k == NCH - 1):
                      e.matmul(o, l, r, start=s, stop=p),
                      reads=[mw1.tag(k), xmix.tag(k)], writes=[f"{name}b{bk}"])
        mk.op("scalar", lambda e, o=mixt.ap(0, 0, TC, p0=0, p1=96), i_=pb[0][0:96, 0:TC]:
              e.activation(o, i_, AF.Tanh), reads=[f"{name}b0"], writes=[mixt.tag(0)])
        mk.op("scalar", lambda e, o=mixt.ap(1, 0, TC, p0=0, p1=64), i_=pb[1][0:64, 0:TC]:
              e.activation(o, i_, AF.Tanh), reads=[f"{name}b1"], writes=[mixt.tag(1)])
        MIXSL = [(0, 0), (0, 32), (0, 64), (1, 0), (1, 32)]
        for f, (proj_w, out_is_g, maa_i) in enumerate(
                ((None, False, 1), (wk, False, 2), (wv, False, 3), (wr, False, 4), (wg, True, 5))):
            mch, mofs = MIXSL[f]
            for m in range(NCH):
                bk = 2 + m % 2
                mk.op("tensor", lambda e, o=pb[bk][:, 0:TC],
                      l=mw2.h[mofs:mofs + 32, f * 1024 + m * 128: f * 1024 + (m + 1) * 128],
                      r=mixt.ap(mch, 0, TC, p0=mofs, p1=mofs + 32),
                      s=True, p=True: e.matmul(o, l, r, start=s, stop=p),
                      reads=[mw2.tag(0), mixt.tag(mch)], writes=[f"{name}b{bk}"])
                mk.op("vector", lambda e, o=u.ap(m), a=pb[bk][:, 0:TC], s=MAAC(maa_i, m), b=dx.ap(m):
                      e.scalar_tensor_tensor(o, a, s, b, ALU.add, ALU.mult),
                      reads=[f"{name}b{bk}", maa.tag(), dx.tag(m)], writes=[u.tag(m)])
                mk.op("gpsimd", lambda e, o=xmix.ap(m), a=u.ap(m), b=xle.ap(m, 1, TC + 1):
                      e.tensor_add(o, a, b),
                      reads=[u.tag(m)] + xle.tags(), writes=[xmix.tag(m)])
            if f == 0:
                for k in range(NCH):
                    mk.op("tensor", lambda e, o=pb[4][0:64, 0:TC], l=dw1.ap(k),
                          r=xmix.ap(k), s=(k == 0), p=(k == NCH - 1):
                          e.matmul(o, l, r, start=s, stop=p),
                          reads=[dw1.tag(k), xmix.tag(k)], writes=[f"{name}b4"])
                mk.op("scalar", lambda e, o=wdec.h[:, :], i_=pb[4][0:64, 0:TC]:
                      e.activation(o, i_, AF.Tanh), reads=[f"{name}b4"], writes=[wdec.tag(0)])
                for m in range(NCH):
                    mk.op("tensor", lambda e, o=pb[5][:, 0:TC], l=dw2.h[:, m * 128:(m + 1) * 128],
                          r=wdec.h[:, :], s=True, p=True: e.matmul(o, l, r, start=s, stop=p),
                          reads=[dw2.tag(0), wdec.tag(0)], writes=[f"{name}b5"])
                    mk.op("scalar", lambda e, o=qout.ap(m), i_=pb[5][:, 0:TC], b=combc.h[:, m:m+1]:
                          e.activation(o, i_, AF.Exp, bias=b),
                          reads=[f"{name}b5", combc.tag()], writes=[qout.tag(m)])
                for m in range(NCH):
                    mk.op("tensor", lambda e, o=pb[4][0:16, 0:TC],
                          l=negd.h[:, m * 16:(m + 1) * 16], r=qout.ap(m),
                          s=(m == 0), p=(m == NCH - 1): e.matmul(o, l, r, start=s, stop=p),
                          reads=[negd.tag(), qout.tag(m)], writes=[f"{name}b4"])
                mk.op("vector", lambda e, o=cpre.ap(0, t0, t0 + TC), i_=pb[4][0:16, 0:TC]:
                      e.tensor_copy(o, i_), reads=[f"{name}b4"], writes=[cpre.tag(0) + f".{it}"])
            else:
                for m in range(NCH):
                    bk2 = 4 + m % 2
                    for k in range(NCH):
                        mk.op("tensor", lambda e, o=pb[bk2][:, 0:TC], l=proj_w.ap(k, m * 128, (m + 1) * 128),
                              r=xmix.ap(k), s=(k == 0), p=(k == NCH - 1):
                              e.matmul(o, l, r, start=s, stop=p),
                              reads=[proj_w.tag(k), xmix.tag(k)], writes=[f"{name}b{bk2}"])
                    mk.op("scalar", lambda e, o=qout.ap(m), i_=pb[bk2][:, 0:TC],
                          fn=(AF.Silu if out_is_g else AF.Identity):
                          e.activation(o, i_, fn),
                          reads=[f"{name}b{bk2}"], writes=[qout.tag(m)])
                okey = {1: "kT", 2: "vT", 3: "rT", 4: "gT"}[f]
                mk.dma("scalar", fm_hbm_view(outs[okey])[:, :, t0:t0 + TC], qout.ap3(),
                       reads=qout.tags(), writes=[f"hbm.{name}{okey}.{it}"])
    mk.dma("gpsimd", combc.h[0:16, 0:1], prm["slow"], writes=[f"{name}slowcol"])
    mk.op("scalar", lambda e, o=combc.h[0:16, 1:2], i_=combc.h[0:16, 0:1]:
          e.activation(o, i_, AF.Sigmoid), reads=[f"{name}slowcol"], writes=[f"{name}slowsig"])
    for sc_i in range(NT):
        init = 0.0 if sc_i == 0 else cpre.h[0:16, sc_i * TC - 1: sc_i * TC]
        mk.op("vector", lambda e, o=cpre.ap(0, sc_i * TC, (sc_i + 1) * TC),
              a=cpre.ap(0, sc_i * TC, (sc_i + 1) * TC), b=czer.h[0:16, :], ini=init:
              e.tensor_tensor_scan(o, a, b, ini, ALU.add, ALU.add),
              reads=[cpre.tag(0) + f".{it}" for it in range(NT)] + [czer.tag(0)]
                    + ([cpre.tag(0)] if sc_i else []),
              writes=[cpre.tag(0)])
    mk.dma("scalar", outs["C"], cpre.h[:, :], reads=[cpre.tag(0)], writes=["hbm.rwC"])
    mk.op("vector", lambda e, o=cpre.h[:, :], a=cpre.h[:, :], s=combc.h[0:16, 1:2]:
          e.tensor_scalar(o, a, s, None, ALU.mult),
          reads=[cpre.tag(0), f"{name}slowsig"], writes=[cpre.tag(0)])
    mk.dma("scalar", outs["Cs"], cpre.h[:, :], reads=[cpre.tag(0)], writes=["hbm.rwCs"])
    return ([f"hbm.{name}{k}.{it}" for k in ("kT", "rT", "gT", "vT") for it in range(NT)]
            + ["hbm.rwC", "hbm.rwCs"])



def emit_rw2(mk, rT_h, kT_h, vT_h, gT_h, C_h, Cs_h, z_h, dmix_h, gam_h, bet_h,
             idf_h, idb_h, name="r2"):
    DM, T, NH = 1024, 4096, 16
    RT = 512
    NR = T // RT
    NSC = T // 128

    rT = T2(mk, name + "rT", 64, T, BF16)
    kT = T2(mk, name + "kT", 64, T, BF16)
    gT = T2(mk, name + "gT", 64, T, BF16)
    vfh = T2(mk, name + "vfh", 64, T, BF16)
    vtk = T2(mk, name + "vtk", T, 64, BF16)
    cc = T2(mk, name + "cc", 16, T, F32)
    cs = T2(mk, name + "cs", 16, T, F32)
    cct = T2(mk, name + "cct", T, 16, F32)
    cst = T2(mk, name + "cst", T, 16, F32)
    c0 = T2(mk, name + "c0", 1, 512, F32)
    cs0 = T2(mk, name + "cs0", 1, 512, F32)
    crep = T2(mk, name + "crep", 128, T, F32)
    csrep = T2(mk, name + "csrep", 128, T, F32)
    bcols = T2(mk, name + "bcols", 128, 4 * 32, F32)
    arep = T2(mk, name + "arep", 128, 2 * RT, F32)
    scol = T2(mk, name + "scol", 128, 8, F32)
    tt_ = T2(mk, name + "tt", 128, 2 * RT, BF16)
    dmt = T2(mk, name + "dmt", 128, 2 * RT, BF16)
    ddt = T2(mk, name + "ddt", 128, RT, F32)
    vstat = T2(mk, name + "vstat", 128, 2 * 128, BF16)
    ycmb = T2(mk, name + "ycmb", 128, RT, F32)
    yh = T2(mk, name + "yh", 64, T, F32)
    ybf = T2(mk, name + "ybf", 64, T, BF16)
    ysq = T2(mk, name + "ysq", 64, T, BF16)
    grow = T2(mk, name + "grow", 1, 6 * 512, F32)
    murep = T2(mk, name + "murep", 64, 512, F32)
    rsrep = T2(mk, name + "rsrep", 64, 512, F32)
    dmix = T2(mk, name + "dmix", 128, 8, F32)
    gam = T2(mk, name + "gam", 128, 8, F32)
    bet = T2(mk, name + "bet", 128, 8, F32)
    idf = T2(mk, name + "idf", 128, 128, F32)
    idb = T2(mk, name + "idb", 128, 128, BF16)
    zt = T2(mk, name + "zt", 64, T, BF16)
    ysl = T2(mk, name + "ysl", 64, RT, F32)
    ddt2 = T2(mk, name + "ddt2", 128, RT, F32)
    hcols = T2(mk, name + "hcols", 64, 4, F32)
    onescol = T2(mk, name + "onc", 128, 1, BF16)
    onesrow_f = T2(mk, name + "onr", 1, 128, F32)

    pb = [mk.psum(f"{name}pb{i}", [128, 512], F32) for i in range(7)]
    pbt = mk.psum(f"{name}pbt", [128, 1024], BF16)
    mk.op("gpsimd", lambda e: e.memset(onescol.h[:, :], 1.0), writes=[onescol.tag()])
    mk.op("gpsimd", lambda e: e.memset(onesrow_f.h[:, :], 1.0), writes=[onesrow_f.tag()])
    mk.dma("sync", cc.h[:, :], C_h, writes=[cc.tag(0)])
    mk.dma("sync", cs.h[:, :], Cs_h, writes=[cs.tag(0)])
    mk.dma("sync", dmix.h[:, :], dmix_h, writes=[dmix.tag()])
    mk.op("scalar", lambda e, o=dmix.h[:, :], i_=dmix.h[:, :]: e.activation(o, i_, AF.Sigmoid),
          reads=[dmix.tag()], writes=[dmix.tag()])
    mk.dma("sync", gam.h[:, :], gam_h, writes=[gam.tag()])
    mk.dma("sync", bet.h[:, :], bet_h, writes=[bet.tag()])
    mk.dma("sync", idf.h[:, :], idf_h, writes=[idf.tag()])
    mk.dma("sync", idb.h[:, :], idb_h, writes=[idb.tag()])
    for src, dst in ((cc, cct), (cs, cst)):
        for st in range(NSC):
            mk.op("tensor", lambda e, o=pb[6][0:128, 0:16],
                  i_=src.ap(0, st * 128, (st + 1) * 128), idn=idf.h[0:16, 0:16]:
                  e.transpose(o, i_, idn), reads=[src.tag(0), idf.tag()], writes=[f"{name}b6"])
            mk.op("vector", lambda e, o=dst.ap(st), i_=pb[6][0:128, 0:16]:
                  e.tensor_copy(o, i_), reads=[f"{name}b6"], writes=[dst.tag(st)])
    cct_tags = [cct.tag(st) for st in range(NSC)]
    cst_tags = [cst.tag(st) for st in range(NSC)]

    GNEPS = 1e-5 * 64.0
    ceps, teps = mk.constcol(GNEPS, parts=1)
    zt_tags = []
    for h in range(NH):
        hofs = 64 * (h % 2)
        for ci, t2src in ((0, dmix), (1, gam), (2, bet)):
            mk.op("vector", lambda e, o=hcols.h[:, ci:ci + 1],
                  i_=t2src.h[hofs:hofs + 64, h // 2:h // 2 + 1]: e.tensor_copy(o, i_),
                  reads=[t2src.tag()], writes=[hcols.tag()])
        mk.dma("sync", rT.h[:, :], rT_h[64 * h:64 * h + 64, :], writes=rT.tags())
        mk.dma("sync", kT.h[:, :], kT_h[64 * h:64 * h + 64, :], writes=kT.tags())
        mk.dma("sync", gT.h[:, :], gT_h[64 * h:64 * h + 64, :], writes=gT.tags())
        mk.dma("sync", vfh.h[:, :], vT_h[64 * h:64 * h + 64, :], writes=vfh.tags())
        for st in range(NSC // 2):
            for half in range(2):
                st2 = 2 * st + half
                mk.op("tensor", lambda e, o=pbt[:, half * 64:half * 64 + 64],
                      i_=vfh.h[:, st2 * 128:(st2 + 1) * 128], idn=idb.h[0:64, 0:64]:
                      e.transpose(o, i_, idn),
                      reads=vfh.tags() + [idb.tag()], writes=[f"{name}bt"])
            mk.op("vector", lambda e, o=vtk.ap(2 * st), i_=pbt[:, 0:64]:
                  e.tensor_copy(o, i_), reads=[f"{name}bt"], writes=[vtk.tag(2 * st)])
            mk.op("vector", lambda e, o=vtk.ap(2 * st + 1), i_=pbt[:, 64:128]:
                  e.tensor_copy(o, i_), reads=[f"{name}bt"], writes=[vtk.tag(2 * st + 1)])
        for tc in range(T // 512):
            mk.dma("gpsimd", c0.h[0:1, :], C_h[h:h + 1, tc * 512:(tc + 1) * 512],
                   writes=[c0.tag(0)])
            mk.op("tensor", lambda e, o=pb[6][:, 0:512], l=onesrow_f.ap(),
                  r=c0.ap(0): e.matmul(o, l, r, start=True, stop=True),
                  reads=[onesrow_f.tag(), c0.tag(0)], writes=[f"{name}b6"])
            mk.op("vector", lambda e, o=crep.ap(0, tc * 512, (tc + 1) * 512), i_=pb[6][:, 0:512]:
                  e.tensor_copy(o, i_), reads=[f"{name}b6"], writes=[crep.tag(0)])
            mk.dma("gpsimd", cs0.h[0:1, :], Cs_h[h:h + 1, tc * 512:(tc + 1) * 512],
                   writes=[cs0.tag(0)])
            mk.op("tensor", lambda e, o=pb[6][:, 0:512], l=onesrow_f.ap(),
                  r=cs0.ap(0): e.matmul(o, l, r, start=True, stop=True),
                  reads=[onesrow_f.tag(), cs0.tag(0)], writes=[f"{name}b6"])
            mk.op("vector", lambda e, o=csrep.ap(0, tc * 512, (tc + 1) * 512), i_=pb[6][:, 0:512]:
                  e.tensor_copy(o, i_), reads=[f"{name}b6"], writes=[csrep.tag(0)])

        for R in range(NR):
            t0 = R * RT
            RLO, RUP, RLOS, RUPS, NLO, NUP, NLOS, NUPS = range(8)
            sct = lambda i: f"{name}scol.{i}"
            for (idx, nidx, reptile, rtag, pos) in (
                    (RLO, NLO, crep, crep.tag(0), t0), (RUP, NUP, crep, crep.tag(0), t0 + RT - 1),
                    (RLOS, NLOS, csrep, csrep.tag(0), t0), (RUPS, NUPS, csrep, csrep.tag(0), t0 + RT - 1)):
                mk.op("vector", lambda e, o=scol.h[:, idx:idx + 1], i_=reptile.ap(0, pos, pos + 1):
                      e.tensor_copy(o, i_), reads=[rtag], writes=[sct(idx)])
                mk.op("vector", lambda e, o=scol.h[:, nidx:nidx + 1], i_=scol.h[:, idx:idx + 1]:
                      e.tensor_scalar(o, i_, -1.0, None, ALU.mult),
                      reads=[sct(idx)], writes=[sct(nidx)])
            for (ofs, t2, t2tags, refidx, sc) in (
                    (0, cct, cct_tags, RLO, -1.0), (32, cst, cst_tags, RLOS, -1.0),
                    (64, cct, cct_tags, RUP, 1.0), (96, cst, cst_tags, RUPS, 1.0)):
                bt = f"{name}bcols.{ofs}"
                clampop = ALU.max if sc < 0 else ALU.min
                mk.op("vector", lambda e, o=bcols.h[:, ofs:ofs + 32], a=t2.h[:, h::16],
                      s=scol.h[:, refidx:refidx + 1], cl=clampop:
                      e.tensor_scalar(o, a, s, 0.0, ALU.subtract, cl),
                      reads=t2tags + [sct(refidx)], writes=[bt])
                mk.op("scalar", lambda e, o=bcols.h[:, ofs:ofs + 32], i_=bcols.h[:, ofs:ofs + 32],
                      sc=sc: e.activation(o, i_, AF.Exp, scale=sc),
                      reads=[bt], writes=[bt])
            mk.op("scalar", lambda e, o=arep.ap(0, 0, RT, p0=0, p1=64),
                  i_=crep.ap(0, t0, t0 + RT, p0=0, p1=64), b=scol.h[0:64, NLO:NLO + 1]:
                  e.activation(o, i_, AF.Exp, bias=b),
                  reads=[crep.tag(0), sct(NLO)], writes=[f"{name}arep.lo"])
            mk.op("scalar", lambda e, o=arep.ap(0, 0, RT, p0=64, p1=128),
                  i_=csrep.ap(0, t0, t0 + RT, p0=64, p1=128), b=scol.h[64:128, NLOS:NLOS + 1]:
                  e.activation(o, i_, AF.Exp, bias=b),
                  reads=[csrep.tag(0), sct(NLOS)], writes=[f"{name}arep.lo"])
            mk.op("scalar", lambda e, o=arep.ap(0, RT, 2 * RT, p0=0, p1=64),
                  i_=crep.ap(0, t0, t0 + RT, p0=0, p1=64), b=scol.h[0:64, RUP:RUP + 1]:
                  e.activation(o, i_, AF.Exp, bias=b, scale=-1.0),
                  reads=[crep.tag(0), sct(RUP)], writes=[f"{name}arep.up"])
            mk.op("scalar", lambda e, o=arep.ap(0, RT, 2 * RT, p0=64, p1=128),
                  i_=csrep.ap(0, t0, t0 + RT, p0=64, p1=128), b=scol.h[64:128, RUPS:RUPS + 1]:
                  e.activation(o, i_, AF.Exp, bias=b, scale=-1.0),
                  reads=[csrep.tag(0), sct(RUPS)], writes=[f"{name}arep.up"])

            nlo = nup = nnear = 0
            tot_lo, tot_up = 4 * R, NSC - 4 * R - 4
            for C in range(NSC):
                kind = "lo" if C < 4 * R else ("near" if C < 4 * R + 4 else "up")
                par = C % 2
                ttile = tt_.ap(0, par * RT, (par + 1) * RT)
                tttag = f"{name}tt.{par}"
                ps, pstag = pb[par], f"{name}b{par}"
                mk.op("tensor", lambda e, o=ps[:, 0:RT], l=kT.h[:, C * 128:(C + 1) * 128],
                      r=rT.h[:, t0:t0 + RT]: e.matmul(o, l, r, start=True, stop=True),
                      reads=kT.tags() + rT.tags(), writes=[pstag])
                mk.op("scalar", lambda e, o=ttile, i_=ps[:, 0:RT]:
                      e.activation(o, i_, AF.Identity, scale=1.0 / 8.0),
                      reads=[pstag], writes=[tttag])
                if kind in ("lo", "up"):
                    boff = 0 if kind == "lo" else 64
                    ybank, ytag = (2, f"{name}b2") if kind == "lo" else (3, f"{name}b3")
                    vs = vstat.ap(0, par * 128, (par + 1) * 128)
                    vstag = f"{name}vstat.{par}"
                    mk.op("vector", lambda e, o=vstat.ap(0, par * 128, par * 128 + 64),
                          a=vtk.ap(C), s=bcols.h[:, boff + C:boff + C + 1]:
                          e.tensor_scalar(o, a, s, None, ALU.mult),
                          reads=[vtk.tag(C), f"{name}bcols.{boff}"], writes=[vstag])
                    mk.op("vector", lambda e, o=vstat.ap(0, par * 128 + 64, par * 128 + 128),
                          a=vtk.ap(C), s=bcols.h[:, boff + 32 + C:boff + 32 + C + 1]:
                          e.tensor_scalar(o, a, s, None, ALU.mult),
                          reads=[vtk.tag(C), f"{name}bcols.{boff + 32}"], writes=[vstag])
                    cnt = nlo if kind == "lo" else nup
                    tot = tot_lo if kind == "lo" else tot_up
                    mk.op("tensor", lambda e, o=pb[ybank][:, 0:RT], l=vs, r=ttile,
                          s=(cnt == 0), p=(cnt == tot - 1): e.matmul(o, l, r, start=s, stop=p),
                          reads=[vstag, tttag], writes=[ytag])
                    if kind == "lo":
                        nlo += 1
                    else:
                        nup += 1
                else:
                    for (mu, reptile, rtags2, t2, t2tags, bank, btag) in (
                            ("f", crep, [crep.tag(0)], cct, cct_tags, 4, f"{name}b4"),
                            ("s", csrep, [csrep.tag(0)], cst, cst_tags, 5, f"{name}b5")):
                        dmf = dmt.ap(0, (0 if mu == "f" else 1) * RT,
                                     ((0 if mu == "f" else 1) + 1) * RT)
                        dmftag = f"{name}dmt.{mu}"
                        mk.op("vector", lambda e, o=ddt.h[:, :], a=reptile.ap(0, t0, t0 + RT),
                              s=t2.h[:, h::16][:, C:C + 1]:
                              e.tensor_scalar(o, a, s, None, ALU.subtract),
                              reads=rtags2 + t2tags, writes=[f"{name}ddt"])
                        mk.op("gpsimd", lambda e, o=ddt2.h[:, :], a=ddt.h[:, :]:
                              e.tensor_scalar(o, a, -1.0, None, ALU.mult),
                              reads=[f"{name}ddt"], writes=[f"{name}ddt2"])
                        mk.op("vector", lambda e, o=ddt.h[:, :], a=ddt.h[:, :], b=ddt2.h[:, :]:
                              e.tensor_max(o, a, b),
                              reads=[f"{name}ddt", f"{name}ddt2"], writes=[f"{name}ddt"])
                        mk.op("scalar", lambda e, o=dmf, i_=ddt.h[:, :]:
                              e.activation(o, i_, AF.Exp, scale=-1.0),
                              reads=[f"{name}ddt"], writes=[dmftag])
                        mk.op("gpsimd", lambda e, o=dmf, a=dmf, b=ttile: e.tensor_mul(o, a, b),
                              reads=[dmftag, tttag], writes=[dmftag])
                        vs2 = vstat.ap(0, par * 128, par * 128 + 64)
                        vstag = f"{name}vstat.{par}"
                        if mu == "f":
                            mk.op("vector", lambda e, o=vs2, i_=vtk.ap(C):
                                  e.tensor_copy(o, i_), reads=[vtk.tag(C)], writes=[vstag])
                        mk.op("tensor", lambda e, o=pb[bank][0:64, 0:RT], l=vs2, r=dmf,
                              s=(nnear == 0), p=(nnear == 3): e.matmul(o, l, r, start=s, stop=p),
                              reads=[vstag, dmftag], writes=[btag])
                    nnear += 1
            ct = f"{name}ycmb"
            if tot_lo > 0:
                mk.op("vector", lambda e, o=ycmb.h[:, :], a=pb[2][:, 0:RT], b=arep.ap(0, 0, RT):
                      e.tensor_mul(o, a, b),
                      reads=[f"{name}b2", f"{name}arep.lo"], writes=[ct])
                if tot_up > 0:
                    mk.op("vector", lambda e, o=ddt.h[:, :], a=pb[3][:, 0:RT], b=arep.ap(0, RT, 2 * RT):
                          e.tensor_mul(o, a, b),
                          reads=[f"{name}b3", f"{name}arep.up"], writes=[f"{name}ddt"])
                    mk.op("vector", lambda e, o=ycmb.h[:, :], a=ycmb.h[:, :], b=ddt.h[:, :]:
                          e.tensor_add(o, a, b), reads=[ct, f"{name}ddt"], writes=[ct])
            else:
                mk.op("vector", lambda e, o=ycmb.h[:, :], a=pb[3][:, 0:RT], b=arep.ap(0, RT, 2 * RT):
                      e.tensor_mul(o, a, b),
                      reads=[f"{name}b3", f"{name}arep.up"], writes=[ct])
            mk.op("vector", lambda e, o=ycmb.h[0:64, :], a=ycmb.h[0:64, :], b=pb[4][0:64, 0:RT]:
                  e.tensor_add(o, a, b), reads=[ct, f"{name}b4"], writes=[ct])
            mk.op("vector", lambda e, o=ycmb.h[64:128, :], a=ycmb.h[64:128, :], b=pb[5][0:64, 0:RT]:
                  e.tensor_add(o, a, b), reads=[ct, f"{name}b5"], writes=[ct])
            mk.op("vector", lambda e, o=ysl.h[:, :], i_=ycmb.h[64:128, :]: e.tensor_copy(o, i_),
                  reads=[ct], writes=[ysl.tag(0)])
            mk.op("vector", lambda e, o=ddt.h[0:64, :], a=ycmb.h[0:64, :], b=ysl.h[:, :]:
                  e.tensor_sub(o, a, b), reads=[ct, ysl.tag(0)], writes=[f"{name}ddt"])
            mk.op("vector", lambda e, o=yh.ap(0, t0, t0 + RT), a=ddt.h[0:64, :],
                  s=hcols.h[:, 0:1], b=ysl.h[:, :]:
                  e.scalar_tensor_tensor(o, a, s, b, ALU.mult, ALU.add),
                  reads=[f"{name}ddt", ysl.tag(0), hcols.tag()], writes=[yh.tag(0)])
        mk.op("scalar", lambda e, o=ybf.h[:, :], i_=yh.h[:, :]: e.activation(o, i_, AF.Identity),
              reads=[yh.tag(0)], writes=[ybf.tag(0)])
        mk.op("scalar", lambda e, o=ysq.h[:, :], i_=ybf.h[:, :]: e.activation(o, i_, AF.Square),
              reads=[ybf.tag(0)], writes=[ysq.tag(0)])
        gcol = lambda t2: hcols.h[:, 1:2] if t2 is gam else hcols.h[:, 2:3]
        for tc in range(T // 512):
            sl = slice(tc * 512, (tc + 1) * 512)
            MU = grow.ap(0, 0, 512); SQ = grow.ap(0, 512, 1024)
            V_ = grow.ap(0, 1024, 1536); RS = grow.ap(0, 1536, 2048)
            MN = grow.ap(0, 2048, 2560)
            mk.op("tensor", lambda e, o=pb[6][0:1, 0:512], l=onescol.ap(p1=64),
                  r=ybf.h[:, sl]: e.matmul(o, l, r, start=True, stop=True),
                  reads=[ybf.tag(0), onescol.tag()], writes=[f"{name}b6"])
            mk.op("vector", lambda e, o=MU, i_=pb[6][0:1, 0:512]:
                  e.tensor_scalar(o, i_, 1.0 / 64.0, None, ALU.mult),
                  reads=[f"{name}b6"], writes=[f"{name}grow.mu"])
            mk.op("tensor", lambda e, o=pb[6][0:1, 0:512], l=onescol.ap(p1=64),
                  r=ysq.h[:, sl]: e.matmul(o, l, r, start=True, stop=True),
                  reads=[ysq.tag(0), onescol.tag(), f"{name}grow.mu"], writes=[f"{name}b6"])
            mk.op("vector", lambda e, o=SQ, i_=pb[6][0:1, 0:512]:
                  e.tensor_scalar(o, i_, 1.0 / 64.0, None, ALU.mult),
                  reads=[f"{name}b6"], writes=[f"{name}grow.sq"])
            mk.op("vector", lambda e, o=V_, a=MU, b=MU: e.tensor_mul(o, a, b),
                  reads=[f"{name}grow.mu"], writes=[f"{name}grow.v"])
            mk.op("vector", lambda e, o=V_, a=SQ, b=V_: e.tensor_sub(o, a, b),
                  reads=[f"{name}grow.sq", f"{name}grow.v"], writes=[f"{name}grow.v"])
            mk.op("scalar", lambda e, o=RS, i_=V_, b=ceps: e.activation(o, i_, AF.Sqrt, bias=b),
                  reads=[f"{name}grow.v", teps], writes=[f"{name}grow.rs"])
            mk.op("vector", lambda e, o=RS, i_=RS: e.reciprocal(o, i_),
                  reads=[f"{name}grow.rs"], writes=[f"{name}grow.rs"])
            mk.op("vector", lambda e, o=MN, a=MU: e.tensor_scalar(o, a, -1.0, None, ALU.mult),
                  reads=[f"{name}grow.mu"], writes=[f"{name}grow.mn"])
            mk.op("tensor", lambda e, o=pb[6][0:64, 0:512], l=onesrow_f.ap(ce=64), r=RS:
                  e.matmul(o, l, r, start=True, stop=True),
                  reads=[onesrow_f.tag(), f"{name}grow.rs"], writes=[f"{name}b6"])
            mk.op("vector", lambda e, o=rsrep.h[:, :], i_=pb[6][0:64, 0:512]:
                  e.tensor_copy(o, i_), reads=[f"{name}b6"], writes=[rsrep.tag(0)])
            mk.op("tensor", lambda e, o=pb[6][0:64, 0:512], l=onesrow_f.ap(ce=64), r=MN:
                  e.matmul(o, l, r, start=True, stop=True),
                  reads=[onesrow_f.tag(), f"{name}grow.mn", rsrep.tag(0)], writes=[f"{name}b6"])
            mk.op("vector", lambda e, o=murep.h[:, :], i_=pb[6][0:64, 0:512]:
                  e.tensor_copy(o, i_), reads=[f"{name}b6"], writes=[murep.tag(0)])
            mk.op("vector", lambda e, o=yh.h[:, sl], a=yh.h[:, sl], b=murep.h[:, :]:
                  e.tensor_add(o, a, b), reads=[yh.tag(0), murep.tag(0)], writes=[yh.tag(0)])
            mk.op("gpsimd", lambda e, o=yh.h[:, sl], a=yh.h[:, sl], b=rsrep.h[:, :]:
                  e.tensor_mul(o, a, b), reads=[yh.tag(0), rsrep.tag(0)], writes=[yh.tag(0)])
            mk.op("scalar", lambda e, o=ysq.h[:, sl], i_=yh.h[:, sl], s=gcol(gam), b=gcol(bet):
                  e.activation(o, i_, AF.Identity, scale=s, bias=b),
                  reads=[yh.tag(0), hcols.tag()], writes=[ysq.tag(0)])
            mk.op("vector", lambda e, o=zt.h[:, sl], a=ysq.h[:, sl], b=gT.h[:, sl]:
                  e.tensor_mul(o, a, b), reads=[ysq.tag(0)] + gT.tags(), writes=[zt.tag(0)])
        mk.dma("scalar", z_h[64 * h:64 * h + 64, :], zt.h[:, :],
               reads=[zt.tag(0)], writes=[f"hbm.{name}z.{h}"])
        zt_tags.append(f"hbm.{name}z.{h}")
    return zt_tags


def emit_wo_residual(mk, x_h, x_tag, z_h, z_tag, xout_h, xout_tag, wo_h,
                     scale=None, name="r3"):
    DM, T = 1024, 4096
    NCH = DM // 128
    TC = 512
    wo = T2(mk, name + "wo", DM, DM, BF16)
    cx = T2(mk, name + "cx", DM, TC, BF16)
    yc = T2(mk, name + "yc", DM, TC, F32)
    pb = [mk.psum(f"{name}pb{i}", [128, 512], F32) for i in range(4)]
    mk.dma("gpsimd", wo.ap3(), fm_hbm_view(wo_h), writes=wo.tags())
    tags = []
    for it in range(T // TC):
        t0 = it * TC
        mk.dma("sync", cx.ap3(), fm_hbm_view(z_h)[:, :, t0:t0 + TC],
               reads=[z_tag], writes=cx.tags())
        mk.dma("scalar", yc.ap3(), fm_hbm_view(x_h)[:, :, t0:t0 + TC],
               reads=[x_tag], writes=yc.tags())
        for m in range(NCH):
            bk = m % 2
            ps, pstag = pb[bk], f"{name}b{bk}"
            for k in range(NCH):
                mk.op("tensor", lambda e, o=ps[:, 0:TC], l=wo.ap(k, m * 128, (m + 1) * 128),
                      r=cx.ap(k), s=(k == 0), p=(k == NCH - 1):
                      e.matmul(o, l, r, start=s, stop=p),
                      reads=[wo.tag(k), cx.tag(k)], writes=[pstag])
            mk.op("vector", lambda e, o=yc.ap(m), a=ps[:, 0:TC], b=yc.ap(m):
                  e.tensor_add(o, a, b),
                  reads=[pstag, yc.tag(m)], writes=[yc.tag(m)])
        mk.dma("scalar", fm_hbm_view(xout_h)[:, :, t0:t0 + TC], yc.ap3(),
               reads=yc.tags(), writes=[f"{xout_tag}.{it}"])
        tags.append(f"{xout_tag}.{it}")
    return tags


def build_rw1_program():
    nc = bass.Bass("TRN2", target_bir_lowering=False)
    mk = MK(nc)
    x = mk.dram("xt", [1024, 4096], F32, kind="ExternalInput")
    prm = {
        "maa_cols": mk.dram("maa_cols", [128, 48], F32, kind="ExternalInput"),
        "maa_w1": mk.dram("maa_w1", [1024, 160], F32, kind="ExternalInput"),
        "maa_w2": mk.dram("maa_w2", [5, 32, 1024], F32, kind="ExternalInput"),
        "dw1": mk.dram("dw1", [1024, 64], F32, kind="ExternalInput"),
        "dw2": mk.dram("dw2", [64, 1024], F32, kind="ExternalInput"),
        "decay_cols": mk.dram("decay_cols", [128, 8], F32, kind="ExternalInput"),
        "hdb": mk.dram("hdb", [16, 1], F32, kind="ExternalInput"),
        "slow": mk.dram("slow", [16, 1], F32, kind="ExternalInput"),
        "wr": mk.dram("wr", [1024, 1024], F32, kind="ExternalInput"),
        "wk": mk.dram("wk", [1024, 1024], F32, kind="ExternalInput"),
        "wv": mk.dram("wv", [1024, 1024], F32, kind="ExternalInput"),
        "wg": mk.dram("wg", [1024, 1024], F32, kind="ExternalInput"),
    }
    outs = {
        "xln": mk.dram("xln", [1024, 4096], BF16, kind="ExternalOutput"),
        "rT": mk.dram("rT", [1024, 4096], BF16, kind="ExternalOutput"),
        "kT": mk.dram("kT", [1024, 4096], BF16, kind="ExternalOutput"),
        "gT": mk.dram("gT", [1024, 4096], BF16, kind="ExternalOutput"),
        "vT": mk.dram("vT", [1024, 4096], BF16, kind="ExternalOutput"),
        "C": mk.dram("C", [16, 4096], F32, kind="ExternalOutput"),
        "Cs": mk.dram("Cs", [16, 4096], F32, kind="ExternalOutput"),
    }
    tags = emit_rw1(mk, x, "hbm.xt", outs, prm)
    mk.barrier_read(tags)
    mk.build()
    return nc


def build_rw2_program():
    nc = bass.Bass("TRN2", target_bir_lowering=False)
    mk = MK(nc)
    rT = mk.dram("rT", [1024, 4096], BF16, kind="ExternalInput")
    kT = mk.dram("kT", [1024, 4096], BF16, kind="ExternalInput")
    vt = mk.dram("vT", [1024, 4096], BF16, kind="ExternalInput")
    gT = mk.dram("gT", [1024, 4096], BF16, kind="ExternalInput")
    C = mk.dram("C", [16, 4096], F32, kind="ExternalInput")
    Cs = mk.dram("Cs", [16, 4096], F32, kind="ExternalInput")
    dmix = mk.dram("dmix", [128, 8], F32, kind="ExternalInput")
    gam = mk.dram("gam", [128, 8], F32, kind="ExternalInput")
    bet = mk.dram("bet", [128, 8], F32, kind="ExternalInput")
    idf = mk.dram("idf", [128, 128], F32, kind="ExternalInput")
    idb = mk.dram("idb", [128, 128], BF16, kind="ExternalInput")
    z = mk.dram("z", [1024, 4096], BF16, kind="ExternalOutput")
    tags = emit_rw2(mk, rT, kT, vt, gT, C, Cs, z, dmix, gam, bet, idf, idb)
    mk.barrier_read(tags)
    mk.build()
    return nc


def build_rw3_program():
    nc = bass.Bass("TRN2", target_bir_lowering=False)
    mk = MK(nc)
    x = mk.dram("xt", [1024, 4096], F32, kind="ExternalInput")
    z = mk.dram("z", [1024, 4096], BF16, kind="ExternalInput")
    wo = mk.dram("wo", [1024, 1024], F32, kind="ExternalInput")
    xo = mk.dram("xo", [1024, 4096], F32, kind="ExternalOutput")
    tags = emit_wo_residual(mk, x, "hbm.xt", z, "hbm.z", xo, "hbm.xo", wo)
    mk.barrier_read(tags)
    mk.build()
    return nc
from concourse.bass_utils import run_bass_kernel_spmd
import ml_dtypes

_PROGRAMS = {}

def _prog(name, builder):
    if name not in _PROGRAMS:
        _PROGRAMS[name] = builder()
    return _PROGRAMS[name]


def _colstack(v):
    v = np.asarray(v, np.float32).reshape(-1)
    return np.ascontiguousarray(v.reshape(-1, 128).T)


def _cmask_np():
    cm = np.zeros((128, 2048), dtype=np.float32)
    for j in range(4):
        for srow in range(128):
            if 128 * j + srow < 512:
                cm[srow, j * 512 + 128 * j + srow:(j + 1) * 512] = 1
    return cm.astype(ml_dtypes.bfloat16)


def _run(nc, shared, per_core):
    in_maps = []
    for c in range(8):
        m = dict(shared)
        m.update({k: v[c] for k, v in per_core.items()})
        in_maps.append(m)
    res = run_bass_kernel_spmd(nc, in_maps, core_ids=list(range(8)))
    return res.results


def kernel(**inp):
    x0 = inp["L0_attn_x"]
    xs = [np.ascontiguousarray(x0[c].T) for c in range(8)]
    idf = np.eye(128, dtype=np.float32)
    idb = idf.astype(ml_dtypes.bfloat16)

    tp = lambda a: np.ascontiguousarray(np.asarray(a).T)

    ffn_shared = {"wgT": tp(inp["L0_ffn_w_gate"]), "wuT": tp(inp["L0_ffn_w_up"]),
                  "wdT": tp(inp["L0_ffn_w_down"])}
    nc_ffn = _prog("ffn0", build_ffn0_program)

    def ffn0(xs_):
        r = _run(nc_ffn, ffn_shared, {"xt": xs_})
        return [np.asarray(r[c]["xo"]) for c in range(8)]

    maa_cols = np.concatenate(
        [_colstack(inp[f"L0_attn_time_maa_{n}"]) for n in "xwkvrg"], axis=1)
    rw1_shared = {
        "maa_cols": maa_cols,
        "maa_w1": np.asarray(inp["L0_attn_time_maa_w1"]),
        "maa_w2": np.asarray(inp["L0_attn_time_maa_w2"]),
        "dw1": np.asarray(inp["L0_attn_time_decay_w1"]),
        "dw2": np.asarray(inp["L0_attn_time_decay_w2"]),
        "decay_cols": _colstack(inp["L0_attn_time_decay"]),
        "hdb": np.asarray(inp["L0_attn_head_decay_bias"], np.float32).reshape(16, 1),
        "slow": np.asarray(inp["L0_attn_slow_scale"], np.float32).reshape(16, 1),
        "wr": np.asarray(inp["L0_attn_W_r"]), "wk": np.asarray(inp["L0_attn_W_k"]),
        "wv": np.asarray(inp["L0_attn_W_v"]), "wg": np.asarray(inp["L0_attn_W_g"]),
    }
    rw2_shared = {"dmix": _colstack(inp["L0_attn_decay_mix"]),
                  "gam": _colstack(inp["L0_attn_ln_gamma"]),
                  "bet": _colstack(inp["L0_attn_ln_beta"]), "idf": idf, "idb": idb}
    rw3_shared = {"wo": np.asarray(inp["L0_attn_W_o"])}

    def rwkv(xs_):
        r1 = _run(_prog("rw1", build_rw1_program), rw1_shared, {"xt": xs_})
        per2 = {k: [np.asarray(r1[c][k]) for c in range(8)]
                for k in ("rT", "kT", "vT", "gT", "C", "Cs")}
        r2 = _run(_prog("rw2", build_rw2_program), rw2_shared, per2)
        r3 = _run(_prog("rw3", build_rw3_program), rw3_shared,
                  {"xt": xs_, "z": [np.asarray(r2[c]["z"]) for c in range(8)]})
        return [np.asarray(r3[c]["xo"]) for c in range(8)]

    ma1_shared = {"wq": np.asarray(inp["L1_ffn_wq"]), "wk": np.asarray(inp["L1_ffn_wk"]),
                  "wv": np.asarray(inp["L1_ffn_wv"]),
                  "ln1s": _colstack(inp["L1_ffn_ln1_scale"]),
                  "ln1b": _colstack(inp["L1_ffn_ln1_shift"]), "idb": idb}
    ma2_shared = {"cm": _cmask_np()}
    ma3_shared = {"wo": np.asarray(inp["L1_ffn_wo"]), "bo": _colstack(inp["L1_ffn_bo"])}
    moe_shared = {"gw": np.asarray(inp["L1_ffn_gate_w"]),
                  "w1": np.asarray(inp["L1_ffn_w1"]), "w2": np.asarray(inp["L1_ffn_w2"]),
                  "w3": np.asarray(inp["L1_ffn_w3"]),
                  "ln2s": _colstack(inp["L1_ffn_ln2_scale"]),
                  "ln2b": _colstack(inp["L1_ffn_ln2_shift"]), "idf": idf}

    def mini(xs_):
        r1 = _run(_prog("ma1", build_ma1_program), ma1_shared, {"xt": xs_})
        q = [np.asarray(r1[c]["q"]) for c in range(8)]
        k = [np.asarray(r1[c]["k"]) for c in range(8)]
        vt = [np.asarray(r1[c]["vt"]) for c in range(8)]
        y = [np.asarray(r1[c]["y"]) for c in range(8)]
        r2 = _run(_prog("ma2", build_ma2_program), ma2_shared, {"q": q, "k": k, "vt": vt})
        ctx = [np.asarray(r2[c]["ctx"]) for c in range(8)]
        r3 = _run(_prog("ma3", build_ma3_program), ma3_shared, {"y": y, "ctx": ctx})
        y2 = [np.asarray(r3[c]["y2"]) for c in range(8)]
        r4 = _run(_prog("moe", build_moe_program), moe_shared, {"xt": xs_, "y2t": y2})
        return [np.asarray(r4[c]["xo"]) for c in range(8)]

    pa_shared = {"wq": np.asarray(inp["L1_attn_Wq"]), "wk": np.asarray(inp["L1_attn_Wk"]),
                 "wv": np.asarray(inp["L1_attn_Wv"]),
                 "bq": _colstack(inp["L1_attn_bq"]), "bk": _colstack(inp["L1_attn_bk"]),
                 "bv": _colstack(inp["L1_attn_bv"]),
                 "ph": _colstack(np.asarray(inp["L1_attn_phase"]).reshape(-1)), "idf": idf}
    pb_shared = {"wo": np.asarray(inp["L1_attn_Wo"]), "bo": _colstack(inp["L1_attn_bo"])}

    def pmx(xs_):
        ra = _run(_prog("pmx_a", build_pmx_a_program), pa_shared, {"xt": xs_})
        oh = [np.asarray(ra[c]["oh"]) for c in range(8)]
        rb = _run(_prog("pmx_b", build_pmx_b_program), pb_shared, {"xt": xs_, "oh": oh})
        return [np.asarray(rb[c]["xo"]) for c in range(8)]

    x = ffn0(xs)
    x = rwkv(x)
    x = ffn0(x)
    x = mini(x)
    x = pmx(x)
    x = mini(x)

    out = np.stack([x[c].T for c in range(8)], axis=0).astype(np.float32)
    return out


# version 07
# speedup vs baseline: 1.0117x; 1.0117x over previous
import contextlib
import numpy as np
import concourse.bass as bass
import concourse.mybir as mybir

F32 = mybir.dt.float32
BF16 = mybir.dt.bfloat16
I32 = mybir.dt.int32
AF = mybir.ActivationFunctionType
ALU = mybir.AluOpType
AX = mybir.AxisListType

ENGINES = ["tensor", "vector", "scalar", "gpsimd", "sync"]
DMA_QUEUES = ["gpsimd", "sync", "scalar"]


DMA_SLOTS = 8
SEM_SETS = 3

class Op:
    __slots__ = ("eng", "fn", "reads", "writes", "idx", "dma", "deps",
                 "awaited", "sem_val", "vclock", "slot", "epoch")
    def __init__(self, eng, fn, reads, writes, idx, dma, epoch):
        self.eng = eng; self.fn = fn; self.reads = reads; self.writes = writes
        self.idx = idx; self.dma = dma
        self.deps = set(); self.awaited = False; self.sem_val = None
        self.vclock = None; self.slot = None; self.epoch = epoch


class MK:
    def __init__(self, nc):
        self.nc = nc
        self.stack = contextlib.ExitStack()
        self._scopes = []
        self.ops = []
        self.last_write = {}
        self.readers = {}
        self.sbuf_bytes = 0
        self._consts = {}
        self._const_tile = None
        self._const_used = 0
        self.epoch = 0
        self._last_op_eng = {}
        self._last_dma_q = {}
        self._const_tile = self.stack.enter_context(
            self.nc.sbuf_tensor("mk_consts", [128, 64], F32))

    def push_scope(self):
        self._scopes.append(contextlib.ExitStack())

    def pop_scope(self):
        self.fence()
        self._scopes.pop().close()

    def fence(self):
        deps = list(self._last_op_eng.values())
        for lst in self._last_dma_q.values():
            deps.extend(lst)
        for e in ENGINES:
            self.op(e, lambda eng: eng.nop(), reads=[], writes=[], extra_deps=deps)
        self._last_op_eng = {}
        self._last_dma_q = {}
        self.epoch += 1

    def sbuf(self, name, shape, dtype):
        st = self._scopes[-1] if self._scopes else self.stack
        h = st.enter_context(self.nc.sbuf_tensor(name, list(shape), dtype))
        bpe = 2 if dtype == BF16 else 4
        self.sbuf_bytes += int(np.prod(shape[1:])) * bpe
        return h

    def psum(self, name, shape, dtype=F32):
        st = self._scopes[-1] if self._scopes else self.stack
        return st.enter_context(self.nc.psum_tensor(name, list(shape), dtype))

    def sem(self, name):
        return self.stack.enter_context(self.nc.semaphore(name))

    def dram(self, name, shape, dtype, kind="Internal"):
        return self.nc.dram_tensor(name, list(shape), dtype, kind=kind).ap()

    def constcol(self, value, parts=128):
        key = float(value)
        if key in self._consts:
            ap128, tag = self._consts[key]
            return ap128[0:parts, 0:1], tag
        i = self._const_used; self._const_used += 1
        assert i < 64
        full = self._const_tile[:, i:i+1]
        tag = f"const.{i}"
        self.op("gpsimd", lambda e, full=full, v=float(value): e.memset(full, v),
                reads=[], writes=[tag])
        self._consts[key] = (full, tag)
        return full[0:parts, 0:1], tag

    def op(self, eng, fn, reads=(), writes=(), dma=False, extra_deps=()):
        assert eng in ENGINES
        idx = len(self.ops)
        o = Op(eng, fn, list(reads), list(writes), idx, dma, self.epoch)
        for d in extra_deps:
            o.deps.add(d)
        for t in o.reads:
            if t in self.last_write:
                o.deps.add(self.last_write[t])
        for t in o.writes:
            if t in self.last_write:
                o.deps.add(self.last_write[t])
            per_eng_last = {}
            for r in self.readers.get(t, ()):
                ro = self.ops[r]
                if ro.dma:
                    o.deps.add(r)
                else:
                    k = ro.eng
                    if k not in per_eng_last or per_eng_last[k] < r:
                        per_eng_last[k] = r
            for r in per_eng_last.values():
                o.deps.add(r)
        for t in o.reads:
            self.readers.setdefault(t, []).append(idx)
        for t in o.writes:
            self.last_write[t] = idx
            self.readers[t] = []
        self.ops.append(o)
        if dma:
            self._last_dma_q.setdefault(eng, []).append(idx)
            if len(self._last_dma_q[eng]) > DMA_SLOTS:
                self._last_dma_q[eng].pop(0)
        else:
            self._last_op_eng[eng] = idx
        return idx

    def dma(self, queue, out, in_, reads=(), writes=(), **kw):
        def fn(e, out=out, in_=in_, kw=kw):
            return e.dma_start(out=out, in_=in_, **kw)
        reads = list(reads); writes = list(writes)
        if type(in_.tensor).__name__ == "DRamTensorHandle":
            reads.append(f"hbm.{in_.tensor.name}")
        if type(out.tensor).__name__ == "DRamTensorHandle":
            writes.append(f"hbm.{out.tensor.name}")
        return self.op(queue, fn, reads, writes, dma=True)

    def barrier_read(self, tags):
        deps = list(self._last_op_eng.values())
        for lst in self._last_dma_q.values():
            deps.extend(lst)
        self.op("gpsimd", lambda e: e.nop(), reads=list(tags), writes=[], extra_deps=deps)

    def build(self, verbose=True):
        ops = self.ops
        n_eng = len(ENGINES)
        eidx = {e: i for i, e in enumerate(ENGINES)}

        def needs_edge(dep, o):
            if dep.dma:
                return True
            if dep.eng == o.eng == "tensor":
                return False
            return True

        for o in ops:
            for d in o.deps:
                dep = ops[d]
                if needs_edge(dep, o):
                    dep.awaited = True
        n_epochs = self.epoch + 1
        eng_await_ctr = {}
        q_issue_ctr = {}
        slot_ctr = {}
        for o in ops:
            si = o.epoch % SEM_SETS
            if o.dma:
                qk = (si, o.eng)
                q_issue_ctr[qk] = q_issue_ctr.get(qk, 0)
                o.slot = q_issue_ctr[qk] % DMA_SLOTS
                q_issue_ctr[qk] += 1
                sk = (si, o.eng, o.slot)
                slot_ctr[sk] = slot_ctr.get(sk, 0) + 16
                o.sem_val = slot_ctr[sk]
            elif o.awaited:
                ek = (si, o.eng)
                eng_await_ctr[ek] = eng_await_ctr.get(ek, 0) + 1
                o.sem_val = eng_await_ctr[ek]

        qidx = {q: i for i, q in enumerate(DMA_QUEUES)}
        NCH = n_eng + len(DMA_QUEUES) * DMA_SLOTS
        dchan = lambda q, sl: n_eng + qidx[q] * DMA_SLOTS + sl
        qidx = {q: i for i, q in enumerate(DMA_QUEUES)}
        NCH = n_eng + len(DMA_QUEUES) * DMA_SLOTS
        dchan = lambda q, sl: n_eng + qidx[q] * DMA_SLOTS + sl
        eng_run = {e: {} for e in ENGINES}
        waits = {}
        n_waits = 0
        for o in ops:
            run = eng_run[o.eng]
            w = []
            def know(dep):
                if dep.vclock is not None:
                    for kk, vv in dep.vclock.items():
                        if run.get(kk, 0) < vv:
                            run[kk] = vv
            for d in sorted(o.deps):
                dep = ops[d]
                if dep.dma:
                    key = (dep.epoch, dchan(dep.eng, dep.slot))
                    val = dep.sem_val
                elif not needs_edge(dep, o):
                    know(dep)
                    continue
                else:
                    key = (dep.epoch, eidx[dep.eng])
                    val = dep.sem_val
                if run.get(key, 0) < val:
                    assert dep.epoch == o.epoch, (
                        f"cross-epoch wait not pruned: op {o.idx} (ep {o.epoch}) "
                        f"dep {d} (ep {dep.epoch})")
                    w.append((key[0], key[1], val))
                    run[key] = val
                    know(dep)
            if o.dma and o.sem_val > 16:
                key = (o.epoch, dchan(o.eng, o.slot))
                if run.get(key, 0) < o.sem_val - 16:
                    w.append((o.epoch, key[1], o.sem_val - 16))
                    run[key] = o.sem_val - 16
            waits[o.idx] = w
            n_waits += len(w)
            o.vclock = dict(run)

        self.stats = dict(
            n_ops=len(ops), awaited=sum(1 for o in ops if o.awaited),
            dmas=sum(1 for o in ops if o.dma), waits=n_waits, epochs=self.epoch + 1,
            per_eng={e: sum(1 for o in ops if o.eng == e) for e in ENGINES},
            sbuf_kb=self.sbuf_bytes / 1024)
        if verbose:
            print("MK stats:", self.stats)

        esems = {(si, e): self.sem(f"sem{si}_{e}") for si in range(SEM_SETS) for e in ENGINES}
        dsem_pool = {(si, q, sl): self.sem(f"dsem{si}_{q}{sl}")
                     for si in range(SEM_SETS) for q in DMA_QUEUES for sl in range(DMA_SLOTS)}

        def chan_sem(ep, ch):
            si = ep % SEM_SETS
            if ch < n_eng:
                return esems[(si, ENGINES[ch])]
            r = ch - n_eng
            return dsem_pool[(si, DMA_QUEUES[r // DMA_SLOTS], r % DMA_SLOTS)]

        by_eng = {e: [o for o in ops if o.eng == e] for e in ENGINES}

        def emit(engine_obj, ename):
            for o in by_eng[ename]:
                for ep, ch, val in waits[o.idx]:
                    engine_obj.wait_ge(chan_sem(ep, ch), val)
                inst = o.fn(engine_obj)
                if o.dma:
                    inst.then_inc(chan_sem(o.epoch, dchan(o.eng, o.slot)), 16)
                elif o.awaited:
                    inst.then_inc(chan_sem(o.epoch, eidx[o.eng]), 1)

        block = self.stack.enter_context(self.nc.Block())
        @block.tensor
        def _(t): emit(t, "tensor")
        @block.vector
        def _(v): emit(v, "vector")
        @block.scalar
        def _(s): emit(s, "scalar")
        @block.gpsimd
        def _(g): emit(g, "gpsimd")
        @block.sync
        def _(s): emit(s, "sync")
        self.stack.close()
        return self.nc


class T2:
    def __init__(self, mk, name, rows, cols, dtype):
        assert rows % 128 == 0 or rows < 128
        self.name = name; self.rows = rows; self.cols = cols; self.dtype = dtype
        self.nch = max(1, rows // 128)
        self.parts = min(rows, 128)
        self.h = mk.sbuf(name, [self.parts, self.nch * cols], dtype)
    def ap(self, chunk=0, cs=None, ce=None, p0=0, p1=None):
        cs = 0 if cs is None else cs
        ce = self.cols if ce is None else ce
        p1 = self.parts if p1 is None else p1
        return self.h[p0:p1, chunk * self.cols + cs: chunk * self.cols + ce]
    def ap3(self, cs=0, ce=None):
        ce = self.cols if ce is None else ce
        v = self.h[:, :].rearrange("p (c t) -> p c t", c=self.nch)
        return v[:, :, cs:ce]
    def tag(self, chunk=0):
        return f"{self.name}.{chunk}"
    def tags(self):
        return [self.tag(c) for c in range(self.nch)]


def fm_hbm_view(hbm_ap, p=128):
    return hbm_ap.rearrange("(c p) t -> p c t", p=p)



def emit_ln(mk, x, xb, xn, u, rows, rep, onescol, onesrow_f, bsum, bsq, brr, brm,
            TC, scale_col=None, shift_col=None, sc_tags=()):
    NCH = x.nch
    ceps, teps = mk.constcol(1e-5, parts=1)
    mk.op("scalar", lambda e, a=xb.h[:, :], b=x.h[:, :]: e.activation(a, b, AF.Copy),
          reads=x.tags(), writes=xb.tags())
    mk.op("scalar", lambda e, a=xn.h[:, :], b=xb.h[:, :]: e.activation(a, b, AF.Square),
          reads=xb.tags(), writes=xn.tags())
    for k in range(NCH):
        mk.op("tensor", lambda e, l=onescol.ap(), r=xb.ap(k), o=bsum[0][0:1, 0:TC],
              s=(k == 0), p=(k == NCH - 1): e.matmul(o, l, r, start=s, stop=p),
              reads=[onescol.tag(), xb.tag(k)], writes=[bsum[1]])
    for k in range(NCH):
        mk.op("tensor", lambda e, l=onescol.ap(), r=xn.ap(k), o=bsq[0][0:1, 0:TC],
              s=(k == 0), p=(k == NCH - 1): e.matmul(o, l, r, start=s, stop=p),
              reads=[onescol.tag(), xn.tag(k)], writes=[bsq[1]])
    R = lambda i: rows.ap(0, i * TC, (i + 1) * TC)
    rt = lambda i: f"{rows.name}.r{i}"
    mk.op("vector", lambda e, o=R(0), i_=bsum[0][0:1, 0:TC]:
          e.tensor_scalar(o, i_, 1.0 / 1024, None, ALU.mult),
          reads=[bsum[1]], writes=[rt(0)])
    mk.op("vector", lambda e, o=R(1), i_=bsq[0][0:1, 0:TC]:
          e.tensor_scalar(o, i_, 1.0 / 1024, None, ALU.mult),
          reads=[bsq[1]], writes=[rt(1)])
    mk.op("vector", lambda e, o=R(4), a=R(0): e.tensor_mul(o, a, a),
          reads=[rt(0)], writes=[rt(4)])
    mk.op("vector", lambda e, o=R(1), a=R(1), b=R(4): e.tensor_sub(o, a, b),
          reads=[rt(1), rt(4)], writes=[rt(1)])
    mk.op("scalar", lambda e, o=R(5), i_=R(1), b=ceps: e.activation(o, i_, AF.Sqrt, bias=b),
          reads=[rt(1), teps], writes=[rt(5)])
    mk.op("vector", lambda e, o=R(2), i_=R(5): e.reciprocal(o, i_),
          reads=[rt(5)], writes=[rt(2)])
    mk.op("vector", lambda e, o=R(3), a=R(0): e.tensor_scalar(o, a, -1.0, None, ALU.mult),
          reads=[rt(0)], writes=[rt(3)])
    mk.op("tensor", lambda e, o=brr[0][:, 0:TC], l=onesrow_f.ap(), r=R(2):
          e.matmul(o, l, r, start=True, stop=True),
          reads=[onesrow_f.tag(), rt(2)], writes=[brr[1]])
    mk.op("tensor", lambda e, o=brm[0][:, 0:TC], l=onesrow_f.ap(), r=R(3):
          e.matmul(o, l, r, start=True, stop=True),
          reads=[onesrow_f.tag(), rt(3)], writes=[brm[1]])
    REP_R, REP_M = rep.ap(0, 0, TC), rep.ap(0, TC, 2 * TC)
    mk.op("vector", lambda e, o=REP_R, i_=brr[0][:, 0:TC]: e.tensor_copy(o, i_),
          reads=[brr[1]], writes=[f"{rep.name}.r"])
    mk.op("vector", lambda e, o=REP_M, i_=brm[0][:, 0:TC]: e.tensor_copy(o, i_),
          reads=[brm[1]], writes=[f"{rep.name}.m"])
    for c in range(NCH):
        mk.op("gpsimd", lambda e, o=u.ap(c), a=x.ap(c), b=REP_M: e.tensor_add(o, a, b),
              reads=[x.tag(c), f"{rep.name}.m"], writes=[u.tag(c)])
        if scale_col is None:
            mk.op("vector", lambda e, o=xn.ap(c), a=u.ap(c), b=REP_R: e.tensor_mul(o, a, b),
                  reads=[u.tag(c), f"{rep.name}.r"], writes=[xn.tag(c)])
        else:
            mk.op("vector", lambda e, o=u.ap(c), a=u.ap(c), b=REP_R: e.tensor_mul(o, a, b),
                  reads=[u.tag(c), f"{rep.name}.r"], writes=[u.tag(c)])
            mk.op("scalar", lambda e, o=xn.ap(c), i_=u.ap(c), s=scale_col[c], b=shift_col[c]:
                  e.activation(o, i_, AF.Identity, bias=b, scale=s),
                  reads=[u.tag(c), *sc_tags], writes=[xn.tag(c)])



def emit_ffn0(mk, xin_h, xin_tag, xout_h, xout_tag, wg_h, wu_h, wd_h,
              TC=256, name="f", load_weights=True, wts=None):
    DM, T, HID = 1024, 4096, 3072
    NCH, NH = DM // 128, HID // 128
    NT = T // TC

    if wts is None:
        wg = T2(mk, name + "wg", DM, HID, BF16)
        wu = T2(mk, name + "wu", DM, HID, BF16)
        wd = T2(mk, name + "wd", HID, DM, BF16)
    else:
        wg, wu, wd = wts
    x = T2(mk, name + "x", DM, TC, F32)
    xb = T2(mk, name + "xb", DM, TC, BF16)
    xn = T2(mk, name + "xn", DM, TC, BF16)
    u = T2(mk, name + "u", DM, TC, F32)
    hs = T2(mk, name + "hs", 256, TC, BF16)
    hh = T2(mk, name + "hh", HID, TC, BF16)
    rep = T2(mk, name + "rep", 128, 2 * TC, F32)
    rows = T2(mk, name + "rows", 1, 6 * TC, F32)
    onescol = T2(mk, name + "onc", 128, 1, BF16)
    onesrow_f = T2(mk, name + "onr", 1, 128, F32)

    pb = [mk.psum(f"{name}pb{i}", [128, 512], F32) for i in range(8)]

    c127, t127 = mk.constcol(1.0 / 127.0)
    chalf127, thalf = mk.constcol(0.5 / 127.0)

    mk.op("gpsimd", lambda e: e.memset(onescol.h[:, :], 1.0), writes=[onescol.tag()])
    mk.op("gpsimd", lambda e: e.memset(onesrow_f.h[:, :], 1.0), writes=[onesrow_f.tag()])

    if load_weights:
        for w, hbm in ((wg, wg_h), (wu, wu_h), (wd, wd_h)):
            mk.dma("gpsimd", w.ap3(), fm_hbm_view(hbm), reads=[], writes=w.tags())

    for it in range(NT):
        t0 = it * TC
        mk.dma("sync", x.ap3(), fm_hbm_view(xin_h)[:, :, t0:t0 + TC],
               reads=[xin_tag], writes=x.tags())
        emit_ln(mk, x, xb, xn, u, rows, rep, onescol, onesrow_f,
                (pb[4], f"{name}b4"), (pb[5], f"{name}b5"),
                (pb[6], f"{name}b6"), (pb[7], f"{name}b7"), TC)
        for m in range(NH):
            gb, ub = 4 + (m % 2) * 2, 5 + (m % 2) * 2
            pg, pu = pb[gb], pb[ub]
            gt, ut = f"{name}b{gb}", f"{name}b{ub}"
            for k in range(NCH):
                mk.op("tensor", lambda e, o=pg[:, 0:TC], l=wg.ap(k, m * 128, (m + 1) * 128),
                      r=xn.ap(k), s=(k == 0), p=(k == NCH - 1): e.matmul(o, l, r, start=s, stop=p),
                      reads=[wg.tag(k), xn.tag(k)], writes=[gt])
            for k in range(NCH):
                mk.op("tensor", lambda e, o=pu[:, 0:TC], l=wu.ap(k, m * 128, (m + 1) * 128),
                      r=xn.ap(k), s=(k == 0), p=(k == NCH - 1): e.matmul(o, l, r, start=s, stop=p),
                      reads=[wu.tag(k), xn.tag(k)], writes=[ut])
            mk.op("scalar", lambda e, o=hs.ap(m % 2), i_=pg[:, 0:TC], s=c127:
                  e.activation(o, i_, AF.Silu, scale=s),
                  reads=[gt, t127], writes=[hs.tag(m % 2)])
            mk.op("vector", lambda e, o=hh.ap(m), a=pu[:, 0:TC], b=hs.ap(m % 2), s=1.0 / 127.0:
                  e.scalar_tensor_tensor(o, a, s, b, ALU.mult, ALU.mult),
                  reads=[ut, hs.tag(m % 2)], writes=[hh.tag(m)])
        for m2 in range(NCH):
            db = m2 % 2
            pd = pb[db]
            dt = f"{name}b{db}"
            for k2 in range(NH):
                mk.op("tensor", lambda e, o=pd[:, 0:TC], l=wd.ap(k2, m2 * 128, (m2 + 1) * 128),
                      r=hh.ap(k2), s=(k2 == 0), p=(k2 == NH - 1): e.matmul(o, l, r, start=s, stop=p),
                      reads=[wd.tag(k2), hh.tag(k2)], writes=[dt])
            mk.op("vector", lambda e, o=x.ap(m2), a=pd[:, 0:TC], b=x.ap(m2), s=0.5 / 127.0:
                  e.scalar_tensor_tensor(o, a, s, b, ALU.mult, ALU.add),
                  reads=[dt, x.tag(m2)], writes=[x.tag(m2)])
        mk.dma("scalar", fm_hbm_view(xout_h)[:, :, t0:t0 + TC], x.ap3(),
               reads=x.tags(), writes=[f"{xout_tag}.{it}"])
    return [f"{xout_tag}.{it}" for it in range(NT)]


def build_ffn0_program():
    nc = bass.Bass("TRN2", target_bir_lowering=False)
    mk = MK(nc)
    xin = mk.dram("xt", [1024, 4096], F32, kind="ExternalInput")
    wg = mk.dram("wgT", [1024, 3072], I32, kind="ExternalInput")
    wu = mk.dram("wuT", [1024, 3072], I32, kind="ExternalInput")
    wd = mk.dram("wdT", [3072, 1024], I32, kind="ExternalInput")
    xout = mk.dram("xo", [1024, 4096], F32, kind="ExternalOutput")
    out_tags = emit_ffn0(mk, xin, "hbm.xt", xout, "hbm.xo", wg, wu, wd)
    mk.barrier_read(out_tags)
    mk.build()
    return nc



def emit_moe(mk, x_h, x_tag, y2_h, y2_tag, xout_h, xout_tag,
             gw_h, w1_h, w2_h, w3_h, ln2s_h, ln2b_h, idf_h, name="m"):
    DM, T, HID, NE = 1024, 4096, 2048, 8
    NCH, NHC = DM // 128, HID // 128
    QT, TC = 1024, 512
    NQ = T // QT

    acc = T2(mk, name + "acc", DM, QT, F32)
    xn2 = T2(mk, name + "xn2", DM, QT, BF16)
    hh = T2(mk, name + "hh", HID, QT, BF16)
    w1s = T2(mk, name + "w1s", DM, 512, BF16)
    w2s = T2(mk, name + "w2s", DM, 512, BF16)
    w3s = T2(mk, name + "w3s", HID, 256, BF16)
    wrep = T2(mk, name + "wrep", 128, QT, F32)
    y2c = T2(mk, name + "y2c", DM, TC, F32)
    u = T2(mk, name + "u", DM, TC, F32)
    xb = T2(mk, name + "xb", DM, TC, BF16)
    xnc = T2(mk, name + "xnc", DM, TC, BF16)
    rep = T2(mk, name + "rep", 128, 2 * TC, F32)
    rows = T2(mk, name + "rows", 1, 6 * TC, F32)
    hs = T2(mk, name + "hs", 256, TC, BF16)
    hu = T2(mk, name + "hu", 256, TC, BF16)
    gsb = T2(mk, name + "gsb", 8, QT, F32)
    gst = T2(mk, name + "gst", 128, 8 * 8, F32)
    gsc = T2(mk, name + "gsc", 128, 8, F32)
    gsc2 = T2(mk, name + "gsc2", 128, 8, F32)
    grows = T2(mk, name + "grows", 128, 8, F32)
    wtok = T2(mk, name + "wtok", 128, 8 * 8, F32)
    wrow = T2(mk, name + "wrow", 8, QT, F32)
    wrowe = T2(mk, name + "wrowe", 1, QT, F32)
    gw_s = T2(mk, name + "gw", DM, 8, BF16)
    ln2s = T2(mk, name + "ln2s", 128, 8, F32)
    ln2b = T2(mk, name + "ln2b", 128, 8, F32)
    idf = T2(mk, name + "idf", 128, 128, F32)
    onescol = T2(mk, name + "onc", 128, 1, BF16)
    onesrow_f = T2(mk, name + "onr", 1, 128, F32)

    pb = [mk.psum(f"{name}pb{i}", [128, 512], F32) for i in range(8)]
    B = lambda i: (pb[i], f"{name}b{i}")

    mk.op("gpsimd", lambda e: e.memset(onescol.h[:, :], 1.0), writes=[onescol.tag()])
    mk.op("gpsimd", lambda e: e.memset(onesrow_f.h[:, :], 1.0), writes=[onesrow_f.tag()])
    mk.dma("gpsimd", gw_s.ap3(), fm_hbm_view(gw_h), writes=gw_s.tags())
    mk.dma("sync", ln2s.h[:, :], ln2s_h, writes=[ln2s.tag()])
    mk.dma("sync", ln2b.h[:, :], ln2b_h, writes=[ln2b.tag()])
    mk.dma("sync", idf.h[:, :], idf_h, writes=[idf.tag()])
    sc_cols = [ln2s.h[:, c:c+1] for c in range(NCH)]
    sh_cols = [ln2b.h[:, c:c+1] for c in range(NCH)]
    out_tags = []

    for q in range(NQ):
        t0 = q * QT
        for half in range(QT // TC):
            h0 = half * TC
            mk.dma("sync", y2c.ap3(), fm_hbm_view(y2_h)[:, :, t0 + h0:t0 + h0 + TC],
                   reads=[y2_tag], writes=y2c.tags())
            mk.dma("scalar", u.ap3(), fm_hbm_view(x_h)[:, :, t0 + h0:t0 + h0 + TC],
                   reads=[x_tag], writes=u.tags())
            for c in range(NCH):
                mk.op("vector", lambda e, o=acc.ap(c, h0, h0 + TC), a=y2c.ap(c), b=u.ap(c):
                      e.scalar_tensor_tensor(o, a, 0.5, b, ALU.mult, ALU.add),
                      reads=[y2c.tag(c), u.tag(c)], writes=[acc.tag(c)])
            emit_ln(mk, y2c, xb, xnc, u, rows, rep, onescol, onesrow_f,
                    B(4), B(5), B(6), B(7), TC,
                    scale_col=sc_cols, shift_col=sh_cols, sc_tags=[ln2s.tag(), ln2b.tag()])
            for c in range(NCH):
                mk.op("gpsimd", lambda e, o=xn2.ap(c, h0, h0 + TC), a=xnc.ap(c):
                      e.tensor_copy(o, a),
                      reads=[xnc.tag(c)], writes=[xn2.tag(c)])

        for tc in range(QT // TC):
            for k in range(NCH):
                mk.op("tensor", lambda e, o=pb[6][0:8, 0:TC], l=gw_s.ap(k),
                      r=xn2.ap(k, tc * TC, (tc + 1) * TC),
                      s=(k == 0), p=(k == NCH - 1): e.matmul(o, l, r, start=s, stop=p),
                      reads=[gw_s.tag(k), xn2.tag(k)], writes=[f"{name}b6"])
            mk.op("vector", lambda e, o=gsb.ap(0, tc * TC, (tc + 1) * TC), i_=pb[6][0:8, 0:TC]:
                  e.tensor_copy(o, i_), reads=[f"{name}b6"], writes=[gsb.tag(0)])
        for st in range(QT // 128):
            mk.op("tensor", lambda e, o=pb[7][0:128, 0:8],
                  i_=gsb.ap(0, st * 128, (st + 1) * 128, p0=0, p1=8), idn=idf.h[0:8, 0:8]:
                  e.transpose(o, i_, idn),
                  reads=[gsb.tag(0), idf.tag()], writes=[f"{name}b7"])
            mk.op("vector", lambda e, o=gst.ap(0, st * 8, (st + 1) * 8), i_=pb[7][0:128, 0:8]:
                  e.tensor_copy(o, i_), reads=[f"{name}b7"], writes=[gst.tag(0) + f".{st}"])
        G = lambda i: grows.ap(0, i, i + 1)
        gt = lambda i: f"{name}grows.{i}"
        for st in range(QT // 128):
            sap = gst.ap(0, st * 8, (st + 1) * 8)
            stag = gst.tag(0) + f".{st}"
            wt = wtok.ap(0, st * 8, (st + 1) * 8)
            wttag = wtok.tag(0) + f".{st}"
            mk.op("vector", lambda e, o=G(0), i_=sap: e.reduce_max(o, i_, AX.X),
                  reads=[stag], writes=[gt(0)])
            mk.op("vector", lambda e, o=gsc.h[:, :], a=sap, b=G(0):
                  e.tensor_scalar(o, a, b, None, ALU.is_equal),
                  reads=[stag, gt(0)], writes=[gsc.tag(0)])
            mk.op("vector", lambda e, o=gsc2.h[:, :], a=gsc.h[:, :], b=sap:
                  e.scalar_tensor_tensor(o, a, -1e30, b, ALU.mult, ALU.add),
                  reads=[gsc.tag(0), stag], writes=[gsc2.tag(0)])
            mk.op("vector", lambda e, o=G(1), i_=gsc2.h[:, :]: e.reduce_max(o, i_, AX.X),
                  reads=[gsc2.tag(0)], writes=[gt(1)])
            mk.op("vector", lambda e, o=G(2), i_=G(0): e.tensor_scalar(o, i_, -1.0, None, ALU.mult),
                  reads=[gt(0)], writes=[gt(2)])
            mk.op("scalar", lambda e, o=gsc.h[:, :], i_=sap, b=G(2):
                  e.activation(o, i_, AF.Exp, bias=b),
                  reads=[stag, gt(2)], writes=[gsc.tag(0)])
            mk.op("vector", lambda e, o=G(3), a=G(1), b=G(0): e.tensor_sub(o, a, b),
                  reads=[gt(1), gt(0)], writes=[gt(3)])
            mk.op("scalar", lambda e, o=G(4), i_=G(3): e.activation(o, i_, AF.Exp),
                  reads=[gt(3)], writes=[gt(4)])
            mk.op("vector", lambda e, o=G(4), i_=G(4): e.tensor_scalar(o, i_, 1.0, None, ALU.add),
                  reads=[gt(4)], writes=[gt(4)])
            mk.op("vector", lambda e, o=G(5), i_=G(4): e.reciprocal(o, i_),
                  reads=[gt(4)], writes=[gt(5)])
            mk.op("vector", lambda e, o=G(5), i_=G(5): e.tensor_scalar(o, i_, 0.5, None, ALU.mult),
                  reads=[gt(5)], writes=[gt(5)])
            mk.op("vector", lambda e, o=gsc2.h[:, :], a=sap, b=G(1):
                  e.tensor_scalar(o, a, b, None, ALU.is_ge),
                  reads=[stag, gt(1)], writes=[gsc2.tag(0)])
            mk.op("vector", lambda e, o=wt, a=gsc.h[:, :], b=gsc2.h[:, :], s=G(5):
                  e.scalar_tensor_tensor(o, a, s, b, ALU.mult, ALU.mult),
                  reads=[gsc.tag(0), gsc2.tag(0), gt(5)], writes=[wttag])
        for st in range(QT // 128):
            mk.op("tensor", lambda e, o=pb[7][0:8, 0:128],
                  i_=wtok.ap(0, st * 8, (st + 1) * 8), idn=idf.h[:, :]:
                  e.transpose(o, i_, idn),
                  reads=[wtok.tag(0) + f".{st}", idf.tag()], writes=[f"{name}b7"])
            mk.op("vector", lambda e, o=wrow.ap(0, st * 128, (st + 1) * 128, p0=0, p1=8),
                  i_=pb[7][0:8, 0:128]: e.tensor_copy(o, i_),
                  reads=[f"{name}b7"], writes=[wrow.tag(0)])

        for ex in range(NE):
            mk.dma("gpsimd", wrowe.h[0:1, :], wrow.h[ex:ex+1, :],
                   reads=[wrow.tag(0)], writes=[wrowe.tag(0)])
            for tc in range(QT // TC):
                mk.op("tensor", lambda e, o=pb[6][:, 0:TC], l=onesrow_f.ap(),
                      r=wrowe.ap(0, tc * TC, (tc + 1) * TC),
                      : e.matmul(o, l, r, start=True, stop=True),
                      reads=[onesrow_f.tag(), wrowe.tag(0)], writes=[f"{name}b6"])
                mk.op("vector", lambda e, o=wrep.ap(0, tc * TC, (tc + 1) * TC), i_=pb[6][:, 0:TC]:
                      e.tensor_copy(o, i_), reads=[f"{name}b6"], writes=[wrep.tag(0)])
            for ms in range(NHC // 2):
                par = ms % 2
                mk.dma("gpsimd", w1s.h[:, :].rearrange("p (c n) -> p c n", c=8)[:, :, par * 256:(par + 1) * 256],
                       fm_hbm_view(w1_h[ex, :, :])[:, :, ms * 256:(ms + 1) * 256],
                       writes=[f"{name}w1s.{par}"])
                mk.dma("gpsimd", w2s.h[:, :].rearrange("p (c n) -> p c n", c=8)[:, :, par * 256:(par + 1) * 256],
                       fm_hbm_view(w2_h[ex, :, :])[:, :, ms * 256:(ms + 1) * 256],
                       writes=[f"{name}w2s.{par}"])
                for mi in range(2):
                    m = ms * 2 + mi
                    for tc in range(QT // TC):
                        gi = (m * 2 + tc) % 2
                        pg, gtag = pb[0 + gi], f"{name}b{0 + gi}"
                        pu, utag = pb[2 + gi], f"{name}b{2 + gi}"
                        for k in range(NCH):
                            mk.op("tensor", lambda e, o=pg[:, 0:TC],
                                  l=w1s.h[:, (k * 512) + par * 256 + mi * 128: (k * 512) + par * 256 + mi * 128 + 128],
                                  r=xn2.ap(k, tc * TC, (tc + 1) * TC),
                                  s=(k == 0), p=(k == NCH - 1): e.matmul(o, l, r, start=s, stop=p),
                                  reads=[f"{name}w1s.{par}", xn2.tag(k)], writes=[gtag])
                        for k in range(NCH):
                            mk.op("tensor", lambda e, o=pu[:, 0:TC],
                                  l=w2s.h[:, (k * 512) + par * 256 + mi * 128: (k * 512) + par * 256 + mi * 128 + 128],
                                  r=xn2.ap(k, tc * TC, (tc + 1) * TC),
                                  s=(k == 0), p=(k == NCH - 1): e.matmul(o, l, r, start=s, stop=p),
                                  reads=[f"{name}w2s.{par}", xn2.tag(k)], writes=[utag])
                        hpar = (m * 2 + tc) % 2
                        mk.op("scalar", lambda e, o=hs.ap(hpar), i_=pg[:, 0:TC]:
                              e.activation(o, i_, AF.Silu),
                              reads=[gtag], writes=[hs.tag(hpar)])
                        mk.op("vector", lambda e, o=hu.ap(hpar), a=pu[:, 0:TC], b=hs.ap(hpar):
                              e.tensor_mul(o, a, b),
                              reads=[utag, hs.tag(hpar)], writes=[hu.tag(hpar)])
                        mk.op("gpsimd", lambda e, o=hh.ap(m, tc * TC, (tc + 1) * TC),
                              a=hu.ap(hpar), b=wrep.ap(0, tc * TC, (tc + 1) * TC):
                              e.tensor_mul(o, a, b),
                              reads=[hu.tag(hpar), wrep.tag(0)], writes=[hh.tag(m)])
            for m2 in range(NCH):
                par = m2 % 2
                mk.dma("gpsimd", w3s.h[:, :].rearrange("p (c n) -> p c n", c=16)[:, :, par * 128:(par + 1) * 128],
                       fm_hbm_view(w3_h[ex, :, :])[:, :, m2 * 128:(m2 + 1) * 128],
                       writes=[f"{name}w3s.{par}"])
                for tc in range(QT // TC):
                    db = 4 + (m2 * 2 + tc) % 2
                    pd, dtag = pb[db], f"{name}b{db}"
                    for k2 in range(NHC):
                        mk.op("tensor", lambda e, o=pd[:, 0:TC],
                              l=w3s.h[:, k2 * 256 + par * 128: k2 * 256 + par * 128 + 128],
                              r=hh.ap(k2, tc * TC, (tc + 1) * TC),
                              s=(k2 == 0), p=(k2 == NHC - 1): e.matmul(o, l, r, start=s, stop=p),
                              reads=[f"{name}w3s.{par}", hh.tag(k2)], writes=[dtag])
                    mk.op("vector", lambda e, o=acc.ap(m2, tc * TC, (tc + 1) * TC), a=pd[:, 0:TC],
                          b=acc.ap(m2, tc * TC, (tc + 1) * TC): e.tensor_add(o, a, b),
                          reads=[dtag, acc.tag(m2)], writes=[acc.tag(m2)])
        mk.dma("scalar", fm_hbm_view(xout_h)[:, :, t0:t0 + QT], acc.ap3(),
               reads=acc.tags(), writes=[f"{xout_tag}.q{q}"])
        out_tags.append(f"{xout_tag}.q{q}")
    return out_tags


def build_moe_program():
    nc = bass.Bass("TRN2", target_bir_lowering=False)
    mk = MK(nc)
    x = mk.dram("xt", [1024, 4096], F32, kind="ExternalInput")
    y2 = mk.dram("y2t", [1024, 4096], F32, kind="ExternalInput")
    gw = mk.dram("gw", [1024, 8], F32, kind="ExternalInput")
    w1 = mk.dram("w1", [8, 1024, 2048], F32, kind="ExternalInput")
    w2 = mk.dram("w2", [8, 1024, 2048], F32, kind="ExternalInput")
    w3 = mk.dram("w3", [8, 2048, 1024], F32, kind="ExternalInput")
    ln2s = mk.dram("ln2s", [128, 8], F32, kind="ExternalInput")
    ln2b = mk.dram("ln2b", [128, 8], F32, kind="ExternalInput")
    idf = mk.dram("idf", [128, 128], F32, kind="ExternalInput")
    xo = mk.dram("xo", [1024, 4096], F32, kind="ExternalOutput")
    tags = emit_moe(mk, x, "hbm.xt", y2, "hbm.y2t", xo, "hbm.xo",
                    gw, w1, w2, w3, ln2s, ln2b, idf)
    mk.barrier_read(tags)
    mk.build()
    return nc



def emit_pmx_a(mk, x_h, x_tag, o_hbm, o_tag, wq_h, wk_h, wv_h, bq_h, bk_h, bv_h,
               ph_h, idf_h, name="pa"):
    DM, T, HD = 1024, 4096, 4096
    NCH, NQC = DM // 128, HD // 128
    TC = 256
    NT = T // TC

    x = T2(mk, name + "x", DM, TC, F32)
    xb = T2(mk, name + "xb", DM, TC, BF16)
    xn = T2(mk, name + "xn", DM, TC, BF16)
    u = T2(mk, name + "u", DM, TC, F32)
    rep = T2(mk, name + "rep", 128, 2 * TC, F32)
    rows = T2(mk, name + "rows", 1, 6 * TC, F32)
    qq = T2(mk, name + "qq", HD, TC, BF16)
    kk = T2(mk, name + "kk", HD, TC, BF16)
    vv = T2(mk, name + "vv", HD, TC, BF16)
    ws = [T2(mk, name + f"ws{i}", DM, 2 * 512, BF16) for i in range(3)]
    bcols = T2(mk, name + "bcols", 128, 3 * 32, F32)
    phc = T2(mk, name + "phc", 128, 32, F32)
    cph = T2(mk, name + "cph", 128, 32, F32)
    sph = T2(mk, name + "sph", 128, 32, F32)
    ct = T2(mk, name + "ct", 128, 16 * 8, F32)
    ctb = T2(mk, name + "ctb", 128, 16 * 8, BF16)
    pp = T2(mk, name + "pp", 128, 2 * TC, BF16)
    srow = T2(mk, name + "srow", 1, 16 * TC, F32)
    s16 = T2(mk, name + "s16", 16, TC, F32)
    stok = T2(mk, name + "stok", 128, 2 * 16, F32)
    atok = T2(mk, name + "atok", 128, 2 * 16, F32)
    arow = T2(mk, name + "arow", 16, TC, F32)
    arow0 = T2(mk, name + "arow0", 1, 16 * TC, F32)
    repw = T2(mk, name + "repw", 128, 2 * TC, F32)
    oo = T2(mk, name + "oo", HD, TC, BF16)
    tmp = T2(mk, name + "tmp", 128, 2 * TC, BF16)
    idf = T2(mk, name + "idf", 128, 128, F32)
    onescol = T2(mk, name + "onc", 128, 1, BF16)
    onesrow_f = T2(mk, name + "onr", 1, 128, F32)

    pb = [mk.psum(f"{name}pb{i}", [128, 512], F32) for i in range(8)]
    B = lambda i: (pb[i], f"{name}b{i}")

    mk.op("gpsimd", lambda e: e.memset(onescol.h[:, :], 1.0), writes=[onescol.tag()])
    mk.op("gpsimd", lambda e: e.memset(onesrow_f.h[:, :], 1.0), writes=[onesrow_f.tag()])
    mk.dma("sync", idf.h[:, :], idf_h, writes=[idf.tag()])
    mk.dma("sync", bcols.h[:, 0:32], bq_h, writes=[bcols.tag()])
    mk.dma("sync", bcols.h[:, 32:64], bk_h, writes=[bcols.tag()])
    mk.dma("sync", bcols.h[:, 64:96], bv_h, writes=[bcols.tag()])
    mk.dma("sync", phc.h[:, :], ph_h, writes=[phc.tag()])
    chalfpi, thalfpi = mk.constcol(np.pi / 2)
    mk.op("scalar", lambda e, o=cph.h[:, :], i_=phc.h[:, :], b=chalfpi:
          e.activation(o, i_, AF.Sin, bias=b), reads=[phc.tag(), thalfpi], writes=[cph.tag()])
    mk.op("scalar", lambda e, o=sph.h[:, :], i_=phc.h[:, :]:
          e.activation(o, i_, AF.Sin), reads=[phc.tag()], writes=[sph.tag()])
    for h in range(4):
        for g in range(4):
            hg = 4 * h + g
            mk.op("vector", lambda e, o=ct.ap(0, hg * 8, hg * 8 + 8),
                  a=cph.h[:, 8 * h:8 * h + 8], b=cph.h[:, 8 * g:8 * g + 8]:
                  e.tensor_mul(o, a, b), reads=[cph.tag()], writes=[ct.tag()])
            mk.op("vector", lambda e, o=pp.h[:, 0:8],
                  a=sph.h[:, 8 * h:8 * h + 8], b=sph.h[:, 8 * g:8 * g + 8]:
                  e.tensor_mul(o, a, b), reads=[sph.tag()], writes=[pp.tag(0)])
            mk.op("vector", lambda e, o=ct.ap(0, hg * 8, hg * 8 + 8),
                  a=ct.ap(0, hg * 8, hg * 8 + 8), b=pp.h[:, 0:8]:
                  e.tensor_add(o, a, b), reads=[ct.tag(), pp.tag(0)], writes=[ct.tag()])
    mk.op("vector", lambda e, o=ctb.h[:, :], i_=ct.h[:, :]: e.tensor_copy(o, i_),
          reads=[ct.tag()], writes=[ctb.tag()])

    WSL = 512
    for it in range(NT):
        t0 = it * TC
        mk.dma("sync", x.ap3(), fm_hbm_view(x_h)[:, :, t0:t0 + TC],
               reads=[x_tag], writes=x.tags())
        emit_ln(mk, x, xb, xn, u, rows, rep, onescol, onesrow_f,
                B(4), B(5), B(6), B(7), TC)
        for wi, (wh, out_t, bofs) in enumerate(((wq_h, qq, 0), (wk_h, kk, 32), (wv_h, vv, 64))):
            for sl in range(HD // WSL):
                par = sl % 2
                wtag = f"{name}ws{wi}.{par}"
                mk.dma("gpsimd", ws[wi].h[:, :].rearrange("p (c n) -> p c n", c=8)[:, :, par * WSL:(par + 1) * WSL],
                       fm_hbm_view(wh)[:, :, sl * WSL:(sl + 1) * WSL], writes=[wtag])
                for mi in range(WSL // 128):
                    m = sl * (WSL // 128) + mi
                    bk = m % 2
                    pg, gtag = pb[bk], f"{name}b{bk}"
                    for k in range(NCH):
                        mk.op("tensor", lambda e, o=pg[:, 0:TC],
                              l=ws[wi].h[:, k * 1024 + par * WSL + mi * 128: k * 1024 + par * WSL + mi * 128 + 128],
                              r=xn.ap(k), s=(k == 0), p=(k == NCH - 1):
                              e.matmul(o, l, r, start=s, stop=p),
                              reads=[wtag, xn.tag(k)], writes=[gtag])
                    mk.op("scalar", lambda e, o=out_t.ap(m), i_=pg[:, 0:TC],
                          b=bcols.h[:, bofs + m:bofs + m + 1]:
                          e.activation(o, i_, AF.Identity, bias=b),
                          reads=[gtag, bcols.tag()], writes=[out_t.tag(m)])
        for h in range(4):
            for g in range(4):
                hg = 4 * h + g
                bk = 2 + hg % 2
                ps, pstag = pb[bk], f"{name}b{bk}"
                ppt = pp.ap(0, (hg % 2) * TC, (hg % 2) * TC + TC)
                pptag = pp.tag(0) + f".{hg % 2}"
                for k in range(NCH):
                    mk.op("vector", lambda e, o=ppt, a=qq.ap(8 * h + k), b=kk.ap(8 * g + k):
                          e.tensor_mul(o, a, b),
                          reads=[qq.tag(8 * h + k), kk.tag(8 * g + k)], writes=[pptag])
                    mk.op("tensor", lambda e, o=ps[0:1, 0:TC], l=ctb.ap(0, hg * 8 + k, hg * 8 + k + 1),
                          r=ppt, s=(k == 0), p=(k == NCH - 1):
                          e.matmul(o, l, r, start=s, stop=p),
                          reads=[ctb.tag(), pptag], writes=[pstag])
                mk.op("vector", lambda e, o=srow.ap(0, hg * TC, (hg + 1) * TC), i_=ps[0:1, 0:TC],
                      s=1.0 / 32.0: e.tensor_scalar(o, i_, s, None, ALU.mult),
                      reads=[pstag], writes=[srow.tag(0)])
        mk.dma("gpsimd", s16.h[:, :],
               srow.h[0:1, :].rearrange("o (hg t) -> o hg t", hg=16),
               reads=[srow.tag(0)], writes=[s16.tag(0)])
        for st in range(TC // 128):
            mk.op("tensor", lambda e, o=pb[6][0:128, 0:16],
                  i_=s16.ap(0, st * 128, (st + 1) * 128), idn=idf.h[0:16, 0:16]:
                  e.transpose(o, i_, idn), reads=[s16.tag(0), idf.tag()], writes=[f"{name}b6"])
            mk.op("vector", lambda e, o=stok.ap(0, st * 16, (st + 1) * 16), i_=pb[6][0:128, 0:16]:
                  e.tensor_copy(o, i_), reads=[f"{name}b6"], writes=[stok.tag(0) + f".{st}"])
        for st in range(TC // 128):
            sap = stok.ap(0, st * 16, (st + 1) * 16)
            stag = stok.tag(0) + f".{st}"
            aap = atok.ap(0, st * 16, (st + 1) * 16)
            atag = atok.tag(0) + f".{st}"
            m4 = rows.ap(0, 5 * TC + st * 8, 5 * TC + st * 8 + 4)
            mx = repw.ap(0, 0, 4); mxt = f"{name}repw.mx"
            sm = repw.ap(0, 4, 8); smt = f"{name}repw.sm"
            mk.op("vector", lambda e, o=mx.rearrange("p (h o) -> p h o", h=4),
                  i_=sap.rearrange("p (h g) -> p h g", h=4):
                  e.reduce_max(o, i_, AX.X), reads=[stag], writes=[mxt])
            for g in range(4):
                mk.op("vector", lambda e, o=aap[:, g:16:4], a=sap[:, g:16:4], b=mx:
                      e.tensor_sub(o, a, b), reads=[stag, mxt], writes=[atag])
            mk.op("scalar", lambda e, o=aap, i_=aap: e.activation(o, i_, AF.Exp),
                  reads=[atag], writes=[atag])
            mk.op("vector", lambda e, o=sm.rearrange("p (h o) -> p h o", h=4),
                  i_=aap.rearrange("p (h g) -> p h g", h=4):
                  e.reduce_sum(o, i_, AX.X), reads=[atag], writes=[smt])
            mk.op("vector", lambda e, o=sm, i_=sm: e.reciprocal(o, i_),
                  reads=[smt], writes=[smt])
            for g in range(4):
                mk.op("vector", lambda e, o=aap[:, g:16:4], a=aap[:, g:16:4], b=sm:
                      e.tensor_mul(o, a, b), reads=[atag, smt], writes=[atag])
        for st in range(TC // 128):
            mk.op("tensor", lambda e, o=pb[7][0:16, 0:128],
                  i_=atok.ap(0, st * 16, (st + 1) * 16), idn=idf.h[:, :]:
                  e.transpose(o, i_, idn),
                  reads=[atok.tag(0) + f".{st}", idf.tag()], writes=[f"{name}b7"])
            mk.op("vector", lambda e, o=arow.ap(0, st * 128, (st + 1) * 128, p0=0, p1=16),
                  i_=pb[7][0:16, 0:128]: e.tensor_copy(o, i_),
                  reads=[f"{name}b7"], writes=[arow.tag(0)])
        mk.dma("gpsimd", arow0.h[0:1, :].rearrange("o (hg t) -> o hg t", hg=16),
               arow.h[:, :], reads=[arow.tag(0)], writes=[arow0.tag(0)])
        for h in range(4):
            for g in range(4):
                hg = 4 * h + g
                par = hg % 2
                rbk = 2 + par
                mk.op("tensor", lambda e, o=pb[rbk][:, 0:TC], l=onesrow_f.ap(),
                      r=arow0.ap(0, hg * TC, (hg + 1) * TC):
                      e.matmul(o, l, r, start=True, stop=True),
                      reads=[onesrow_f.tag(), arow0.tag(0)], writes=[f"{name}b{rbk}"])
                mk.op("vector", lambda e, o=repw.ap(0, par * TC, (par + 1) * TC), i_=pb[rbk][:, 0:TC]:
                      e.tensor_copy(o, i_), reads=[f"{name}b{rbk}"], writes=[f"{name}repw.{par}"])
                for dch in range(NCH):
                    oc = 8 * h + dch
                    if g == 0:
                        mk.op("gpsimd", lambda e, o=oo.ap(oc), a=vv.ap(8 * g + dch),
                              b=repw.ap(0, par * TC, (par + 1) * TC): e.tensor_mul(o, a, b),
                              reads=[vv.tag(8 * g + dch), f"{name}repw.{par}"], writes=[oo.tag(oc)])
                    else:
                        tpar = dch % 2
                        mk.op("gpsimd", lambda e, o=tmp.ap(0, tpar * TC, (tpar + 1) * TC),
                              a=vv.ap(8 * g + dch), b=repw.ap(0, par * TC, (par + 1) * TC):
                              e.tensor_mul(o, a, b),
                              reads=[vv.tag(8 * g + dch), f"{name}repw.{par}"],
                              writes=[tmp.tag(0) + f".{tpar}"])
                        mk.op("vector", lambda e, o=oo.ap(oc), a=oo.ap(oc),
                              b=tmp.ap(0, tpar * TC, (tpar + 1) * TC): e.tensor_add(o, a, b),
                              reads=[oo.tag(oc), tmp.tag(0) + f".{tpar}"], writes=[oo.tag(oc)])
        mk.dma("scalar",
               o_hbm.rearrange("h (c p) s -> p h c s", p=128)[:, :, :, t0:t0 + TC],
               oo.h[:, :].rearrange("p (h c t) -> p h c t", h=4, c=8),
               reads=oo.tags(), writes=[f"{o_tag}.{it}"])
    return [f"{o_tag}.{it}" for it in range(NT)]



def emit_pmx_b(mk, x_h, x_tag, o_hbm, o_tag, xout_h, xout_tag, wo_h, bo_h, name="pb"):
    DM, T = 1024, 4096
    NCH = DM // 128
    wo = T2(mk, name + "wo", 4096, DM, BF16)
    ow = T2(mk, name + "ow", DM, 2048, BF16)
    xc = T2(mk, name + "xc", DM, 512, F32)
    bo = T2(mk, name + "bo", 128, 8, F32)
    pb = [mk.psum(f"{name}pb{i}", [128, 512], F32) for i in range(8)]

    mk.dma("gpsimd", wo.ap3(), fm_hbm_view(wo_h), writes=wo.tags())
    mk.dma("sync", bo.h[:, :], bo_h, writes=[bo.tag()])

    for h in range(4):
        for qw in range(2):
            q0 = h * 1024 + qw * 512
            s0 = qw * 2048
            mk.dma("sync", ow.ap3(), o_hbm[h, :, :].rearrange("(c p) s -> p c s", p=128)[:, :, s0:s0 + 2048],
                   reads=[o_tag], writes=ow.tags())
            mk.dma("scalar", xc.ap3(), fm_hbm_view(x_h)[:, :, q0:q0 + 512],
                   reads=[x_tag], writes=xc.tags())
            for n in range(NCH):
                bk = n % 2
                ps, pstag = pb[bk], f"{name}b{bk}"
                first = True
                for r in range(4):
                    for dch in range(NCH):
                        jc = 8 * r + dch
                        mk.op("tensor", lambda e, o=ps[:, 0:512],
                              l=wo.ap(jc, n * 128, (n + 1) * 128),
                              r=ow.h[:, dch * 2048 + r: (dch + 1) * 2048: 4],
                              s=first, p=(r == 3 and dch == NCH - 1):
                              e.matmul(o, l, r, start=s, stop=p),
                              reads=[wo.tag(jc), ow.tag(dch)], writes=[pstag])
                        first = False
                mk.op("vector", lambda e, o=xc.ap(n), a=ps[:, 0:512], b=xc.ap(n),
                      s=bo.h[:, n:n + 1]:
                      e.scalar_tensor_tensor(o, a, s, b, ALU.add, ALU.add),
                      reads=[pstag, xc.tag(n), bo.tag()], writes=[xc.tag(n)])
            mk.dma("scalar", fm_hbm_view(xout_h)[:, :, q0:q0 + 512], xc.ap3(),
                   reads=xc.tags(), writes=[f"{xout_tag}.{h}.{qw}"])
    return [f"{xout_tag}.{h}.{qw}" for h in range(4) for qw in range(2)]


def build_pmx_a_program():
    nc = bass.Bass("TRN2", target_bir_lowering=False)
    mk = MK(nc)
    x = mk.dram("xt", [1024, 4096], F32, kind="ExternalInput")
    wq = mk.dram("wq", [1024, 4096], F32, kind="ExternalInput")
    wk = mk.dram("wk", [1024, 4096], F32, kind="ExternalInput")
    wv = mk.dram("wv", [1024, 4096], F32, kind="ExternalInput")
    bq = mk.dram("bq", [128, 32], F32, kind="ExternalInput")
    bk = mk.dram("bk", [128, 32], F32, kind="ExternalInput")
    bv = mk.dram("bv", [128, 32], F32, kind="ExternalInput")
    ph = mk.dram("ph", [128, 32], F32, kind="ExternalInput")
    idf = mk.dram("idf", [128, 128], F32, kind="ExternalInput")
    oh = mk.dram("oh", [4, 1024, 4096], BF16, kind="ExternalOutput")
    tags = emit_pmx_a(mk, x, "hbm.xt", oh, "hbm.oh", wq, wk, wv, bq, bk, bv, ph, idf)
    mk.barrier_read(tags)
    mk.build()
    return nc


def build_pmx_b_program():
    nc = bass.Bass("TRN2", target_bir_lowering=False)
    mk = MK(nc)
    x = mk.dram("xt", [1024, 4096], F32, kind="ExternalInput")
    oh = mk.dram("oh", [4, 1024, 4096], BF16, kind="ExternalInput")
    wo = mk.dram("wo", [4096, 1024], F32, kind="ExternalInput")
    bo = mk.dram("bo", [128, 8], F32, kind="ExternalInput")
    xo = mk.dram("xo", [1024, 4096], F32, kind="ExternalOutput")
    tags = emit_pmx_b(mk, x, "hbm.xt", oh, "hbm.oh", xo, "hbm.xo", wo, bo)
    mk.barrier_read(tags)
    mk.build()
    return nc



def emit_ma1(mk, x_h, x_tag, y_h, y_tag, q_h, q_tag, k_h, k_tag, vt_h, vt_tag,
             wq_h, wk_h, wv_h, ln1s_h, ln1b_h, idb_h, name="n1"):
    DM, T = 1024, 4096
    NCH = DM // 128
    TC = 512
    NT = T // TC

    x = T2(mk, name + "x", DM, TC, F32)
    xb = T2(mk, name + "xb", DM, TC, BF16)
    yb = T2(mk, name + "yb", DM, TC, F32)
    h1 = T2(mk, name + "h1", DM, TC, BF16)
    u = T2(mk, name + "u", DM, TC, F32)
    u2 = T2(mk, name + "u2", DM, TC, F32)
    rep = T2(mk, name + "rep", 128, 2 * TC, F32)
    rep2 = T2(mk, name + "rep2", 128, 2 * TC, F32)
    rows = T2(mk, name + "rows", 1, 6 * TC, F32)
    rows2 = T2(mk, name + "rows2", 1, 6 * TC, F32)
    qq = T2(mk, name + "qq", DM, TC, BF16)
    kk = T2(mk, name + "kk", DM, TC, BF16)
    vvf = T2(mk, name + "vvf", DM, TC, BF16)
    vtk = T2(mk, name + "vtk", TC, DM, BF16)
    wq = T2(mk, name + "wq", DM, DM, BF16)
    wk = T2(mk, name + "wk", DM, DM, BF16)
    wv = T2(mk, name + "wv", DM, DM, BF16)
    ln1s = T2(mk, name + "ln1s", 128, 8, F32)
    ln1b = T2(mk, name + "ln1b", 128, 8, F32)
    idb = T2(mk, name + "idb", 128, 128, BF16)
    onescol = T2(mk, name + "onc", 128, 1, BF16)
    onesrow_f = T2(mk, name + "onr", 1, 128, F32)

    pb = [mk.psum(f"{name}pb{i}", [128, 512], F32) for i in range(6)]
    pbt = mk.psum(f"{name}pbt", [128, 512], BF16)
    B = lambda i: (pb[i], f"{name}b{i}")

    mk.op("gpsimd", lambda e: e.memset(onescol.h[:, :], 1.0), writes=[onescol.tag()])
    mk.op("gpsimd", lambda e: e.memset(onesrow_f.h[:, :], 1.0), writes=[onesrow_f.tag()])
    mk.dma("sync", idb.h[:, :], idb_h, writes=[idb.tag()])
    mk.dma("sync", ln1s.h[:, :], ln1s_h, writes=[ln1s.tag()])
    mk.dma("sync", ln1b.h[:, :], ln1b_h, writes=[ln1b.tag()])
    for w, hbm in ((wq, wq_h), (wk, wk_h), (wv, wv_h)):
        mk.dma("gpsimd", w.ap3(), fm_hbm_view(hbm), writes=w.tags())
    sc_cols = [ln1s.h[:, c:c+1] for c in range(NCH)]
    sh_cols = [ln1b.h[:, c:c+1] for c in range(NCH)]

    for it in range(NT):
        t0 = it * TC
        mk.dma("sync", x.ap3(), fm_hbm_view(x_h)[:, :, t0:t0 + TC],
               reads=[x_tag], writes=x.tags())
        emit_ln(mk, x, xb, h1, u, rows, rep, onescol, onesrow_f,
                B(2), B(3), B(4), B(5), TC)
        REP_R = rep.ap(0, 0, TC)
        for c in range(NCH):
            mk.op("gpsimd", lambda e, o=yb.ap(c), a=u.ap(c), b=REP_R: e.tensor_mul(o, a, b),
                  reads=[u.tag(c), f"{name}rep.r"], writes=[yb.tag(c)])
        mk.dma("scalar", fm_hbm_view(y_h)[:, :, t0:t0 + TC], yb.ap3(),
               reads=yb.tags(), writes=[f"{y_tag}.{it}"])
        emit_ln(mk, yb, xb, h1, u2, rows2, rep2, onescol, onesrow_f,
                B(2), B(3), B(4), B(5), TC,
                scale_col=sc_cols, shift_col=sh_cols, sc_tags=[ln1s.tag(), ln1b.tag()])
        for w, out_t in ((wq, qq), (wk, kk), (wv, vvf)):
            for m in range(NCH):
                bk = m % 2
                pg, gtag = pb[bk], f"{name}b{bk}"
                for k in range(NCH):
                    mk.op("tensor", lambda e, o=pg[:, 0:TC], l=w.ap(k, m * 128, (m + 1) * 128),
                          r=h1.ap(k), s=(k == 0), p=(k == NCH - 1):
                          e.matmul(o, l, r, start=s, stop=p),
                          reads=[w.tag(k), h1.tag(k)], writes=[gtag])
                mk.op("scalar", lambda e, o=out_t.ap(m), i_=pg[:, 0:TC]:
                      e.activation(o, i_, AF.Identity),
                      reads=[gtag], writes=[out_t.tag(m)])
        mk.dma("scalar", fm_hbm_view(q_h)[:, :, t0:t0 + TC], qq.ap3(),
               reads=qq.tags(), writes=[f"{q_tag}.{it}"])
        mk.dma("scalar", fm_hbm_view(k_h)[:, :, t0:t0 + TC], kk.ap3(),
               reads=kk.tags(), writes=[f"{k_tag}.{it}"])
        for c in range(NCH):
            for tt in range(TC // 128):
                mk.op("tensor", lambda e, o=pbt[:, 0:128],
                      i_=vvf.ap(c, tt * 128, (tt + 1) * 128), idn=idb.h[:, :]:
                      e.transpose(o, i_, idn),
                      reads=[vvf.tag(c), idb.tag()], writes=[f"{name}bt"])
                mk.op("vector", lambda e, o=vtk.ap(tt, c * 128, (c + 1) * 128), i_=pbt[:, 0:128]:
                      e.tensor_copy(o, i_), reads=[f"{name}bt"], writes=[vtk.tag(tt)])
        mk.dma("scalar", vt_h[t0:t0 + TC, :].rearrange("(tc p) c -> p tc c", p=128),
               vtk.ap3(), reads=vtk.tags(), writes=[f"{vt_tag}.{it}"])
    return ([f"{y_tag}.{it}" for it in range(NT)] + [f"{q_tag}.{it}" for it in range(NT)]
            + [f"{k_tag}.{it}" for it in range(NT)] + [f"{vt_tag}.{it}" for it in range(NT)])



def emit_ma2(mk, q_h, q_tag, k_h, k_tag, vt_h, vt_tag, ctx_h, ctx_tag, cm_h, name="n2"):
    DM, T, NH, HK = 1024, 4096, 16, 64
    RT = 512
    NR = T // RT
    NSC = T // 128

    kT = T2(mk, name + "kT", 64, T, BF16)
    qT = T2(mk, name + "qT", 64, T, BF16)
    vtk = T2(mk, name + "vtk", T, DM, BF16)
    ksq = T2(mk, name + "ksq", 64, T, BF16)
    vstat = T2(mk, name + "vstat", 128, 2 * 65, BF16)
    at = T2(mk, name + "at", 128, NSC * RT, BF16)
    nrow = T2(mk, name + "nrow", 1, 3 * T, F32)
    brow = T2(mk, name + "brow", 128, 4, F32)
    rrow = T2(mk, name + "rrow", 1, RT, F32)
    rrep = T2(mk, name + "rrep", 64, RT, F32)
    ctxt = T2(mk, name + "ctxt", 64, RT, BF16)
    cmask = T2(mk, name + "cmask", 128, 4 * RT, BF16)
    onescol = T2(mk, name + "onc", 128, 1, BF16)
    onesrow_f = T2(mk, name + "onr", 1, 128, F32)

    pb = [mk.psum(f"{name}pb{i}", [128, 512], F32) for i in range(8)]

    mk.op("gpsimd", lambda e: e.memset(onescol.h[:, :], 1.0), writes=[onescol.tag()])
    mk.op("gpsimd", lambda e: e.memset(onesrow_f.h[:, :], 1.0), writes=[onesrow_f.tag()])
    mk.dma("sync", cmask.h[:, :], cm_h, writes=[cmask.tag(0)])

    mk.dma("sync", vtk.ap3(), fm_hbm_view(vt_h), writes=vtk.tags())
    out_tags = []
    for h in range(NH):
        mk.dma("sync", kT.h[:, :], k_h[64 * h:64 * h + 64, :], writes=kT.tags())
        mk.dma("sync", qT.h[:, :], q_h[64 * h:64 * h + 64, :], writes=qT.tags())
        mk.op("scalar", lambda e, o=ksq.h[:, :], i_=kT.h[:, :]: e.activation(o, i_, AF.Square),
              reads=kT.tags(), writes=ksq.tags())
        for tc in range(T // 512):
            mk.op("tensor", lambda e, o=pb[6][0:1, 0:512], l=onescol.ap(p1=64),
                  r=ksq.h[:, tc * 512:(tc + 1) * 512], s=True, p=True:
                  e.matmul(o, l, r, start=s, stop=p),
                  reads=ksq.tags() + [onescol.tag()], writes=[f"{name}b6"])
            mk.op("vector", lambda e, o=nrow.ap(0, tc * 512, (tc + 1) * 512), i_=pb[6][0:1, 0:512]:
                  e.tensor_copy(o, i_), reads=[f"{name}b6"], writes=[nrow.tag(0) + ".k"])
        mk.op("scalar", lambda e, o=ksq.h[:, :], i_=qT.h[:, :]: e.activation(o, i_, AF.Square),
              reads=qT.tags(), writes=ksq.tags())
        for tc in range(T // 512):
            mk.op("tensor", lambda e, o=pb[7][0:1, 0:512], l=onescol.ap(p1=64),
                  r=ksq.h[:, tc * 512:(tc + 1) * 512], s=True, p=True:
                  e.matmul(o, l, r, start=s, stop=p),
                  reads=ksq.tags() + [onescol.tag()], writes=[f"{name}b7"])
            mk.op("vector", lambda e, o=nrow.ap(0, T + tc * 512, T + (tc + 1) * 512),
                  i_=pb[7][0:1, 0:512]:
                  e.tensor_copy(o, i_), reads=[f"{name}b7"], writes=[nrow.tag(0) + ".q"])
        NR1, NR2, NR3 = nrow.ap(0, 0, T), nrow.ap(0, T, 2 * T), nrow.ap(0, 2 * T, 2 * T + 4)
        mk.op("vector", lambda e, o=nrow.ap(0, 2 * T, 2 * T + 1), i_=NR1:
              e.reduce_max(o, i_, AX.X), reads=[nrow.tag(0) + ".k"], writes=[nrow.tag(0) + ".mk"])
        mk.op("vector", lambda e, o=nrow.ap(0, 2 * T + 1, 2 * T + 2), i_=NR2:
              e.reduce_max(o, i_, AX.X), reads=[nrow.tag(0) + ".q"], writes=[nrow.tag(0) + ".mq"])
        mk.op("vector", lambda e, o=nrow.ap(0, 2 * T + 2, 2 * T + 3),
              a=nrow.ap(0, 2 * T, 2 * T + 1), b=nrow.ap(0, 2 * T + 1, 2 * T + 2):
              e.tensor_mul(o, a, b),
              reads=[nrow.tag(0) + ".mk", nrow.tag(0) + ".mq"], writes=[nrow.tag(0) + ".p"])
        mk.op("scalar", lambda e, o=nrow.ap(0, 2 * T + 3, 2 * T + 4),
              i_=nrow.ap(0, 2 * T + 2, 2 * T + 3):
              e.activation(o, i_, AF.Sqrt), reads=[nrow.tag(0) + ".p"], writes=[nrow.tag(0) + ".sq"])
        mk.op("vector", lambda e, o=nrow.ap(0, 2 * T + 2, 2 * T + 3),
              i_=nrow.ap(0, 2 * T + 3, 2 * T + 4):
              e.tensor_scalar(o, i_, -1.0 / 8.0, None, ALU.mult),
              reads=[nrow.tag(0) + ".sq"], writes=[nrow.tag(0) + ".nb"])
        mk.op("tensor", lambda e, o=pb[6][0:128, 0:1], l=onesrow_f.ap(),
              r=nrow.ap(0, 2 * T + 2, 2 * T + 3):
              e.matmul(o, l, r, start=True, stop=True),
              reads=[onesrow_f.tag(), nrow.tag(0) + ".nb"], writes=[f"{name}b6"])
        mk.op("vector", lambda e, o=brow.h[:, 0:1], i_=pb[6][0:128, 0:1]: e.tensor_copy(o, i_),
              reads=[f"{name}b6"], writes=[brow.tag(0)])

        for R in range(NR):
            t0 = R * RT
            ncs = 4 * R + 4
            for C in range(ncs):
                bk = C % 2
                ps, pstag = pb[bk], f"{name}b{bk}"
                mk.op("tensor", lambda e, o=ps[:, 0:RT], l=kT.h[:, C * 128:(C + 1) * 128],
                      r=qT.h[:, t0:t0 + RT]: e.matmul(o, l, r, start=True, stop=True),
                      reads=kT.tags() + qT.tags(), writes=[pstag])
                mk.op("scalar", lambda e, o=at.ap(0, C * RT, (C + 1) * RT), i_=ps[:, 0:RT],
                      b=brow.h[:, 0:1], s=1.0 / 8.0:
                      e.activation(o, i_, AF.Exp, bias=b, scale=s),
                      reads=[pstag, brow.tag(0)], writes=[at.tag(0) + f".{C}"])
            for j in range(4):
                C = 4 * R + j
                mk.op("vector", lambda e, o=at.ap(0, C * RT, (C + 1) * RT),
                      a=at.ap(0, C * RT, (C + 1) * RT), b=cmask.ap(0, j * RT, (j + 1) * RT):
                      e.tensor_mul(o, a, b),
                      reads=[at.tag(0) + f".{C}", cmask.tag(0)], writes=[at.tag(0) + f".{C}"])
            for C in range(ncs):
                par = C % 2
                vst = vstat.ap(0, par * 65, par * 65 + 65)
                vsttag = f"{name}vstat.{par}"
                mk.op("vector", lambda e, o=vstat.ap(0, par * 65, par * 65 + 64),
                      i_=vtk.ap(C, 64 * h, 64 * h + 64): e.tensor_copy(o, i_),
                      reads=[vtk.tag(C)], writes=[vsttag])
                mk.op("gpsimd", lambda e, o=vstat.ap(0, par * 65 + 64, par * 65 + 65):
                      e.memset(o, 1.0), writes=[vsttag])
                mk.op("tensor", lambda e, o=pb[4][0:65, 0:RT], l=vst,
                      r=at.ap(0, C * RT, (C + 1) * RT), s=(C == 0), p=(C == ncs - 1):
                      e.matmul(o, l, r, start=s, stop=p),
                      reads=[vsttag, at.tag(0) + f".{C}"], writes=[f"{name}b4"])
            mk.op("vector", lambda e, o=rrow.ap(0), i_=pb[4][64:65, 0:RT]:
                  e.reciprocal(o, i_), reads=[f"{name}b4"], writes=[rrow.tag(0)])
            mk.op("tensor", lambda e, o=pb[5][0:64, 0:RT], l=onesrow_f.ap(ce=64),
                  r=rrow.ap(0): e.matmul(o, l, r, start=True, stop=True),
                  reads=[onesrow_f.tag(), rrow.tag(0)], writes=[f"{name}b5"])
            mk.op("vector", lambda e, o=rrep.h[:, :], i_=pb[5][0:64, 0:RT]: e.tensor_copy(o, i_),
                  reads=[f"{name}b5"], writes=[rrep.tag(0)])
            mk.op("vector", lambda e, o=ctxt.h[:, :], a=pb[4][0:64, 0:RT], b=rrep.h[:, :]:
                  e.tensor_mul(o, a, b),
                  reads=[f"{name}b4", rrep.tag(0)], writes=[ctxt.tag(0)])
            mk.dma("scalar", ctx_h[64 * h:64 * h + 64, t0:t0 + RT], ctxt.h[:, :],
                   reads=[ctxt.tag(0)], writes=[f"{ctx_tag}.{h}.{R}"])
            out_tags.append(f"{ctx_tag}.{h}.{R}")
    return out_tags



def emit_ma3(mk, y_h, y_tag, ctx_h, ctx_tag, y2_h, y2_tag, wo_h, bo_h, name="n3"):
    DM, T = 1024, 4096
    NCH = DM // 128
    TC = 512
    wo = T2(mk, name + "wo", DM, DM, BF16)
    cx = T2(mk, name + "cx", DM, TC, BF16)
    yc = T2(mk, name + "yc", DM, TC, F32)
    bo = T2(mk, name + "bo", 128, 8, F32)
    pb = [mk.psum(f"{name}pb{i}", [128, 512], F32) for i in range(4)]
    mk.dma("gpsimd", wo.ap3(), fm_hbm_view(wo_h), writes=wo.tags())
    mk.dma("sync", bo.h[:, :], bo_h, writes=[bo.tag()])
    tags = []
    for it in range(T // TC):
        t0 = it * TC
        mk.dma("sync", cx.ap3(), fm_hbm_view(ctx_h)[:, :, t0:t0 + TC],
               reads=[ctx_tag], writes=cx.tags())
        mk.dma("scalar", yc.ap3(), fm_hbm_view(y_h)[:, :, t0:t0 + TC],
               reads=[y_tag], writes=yc.tags())
        for m in range(NCH):
            bk = m % 2
            ps, pstag = pb[bk], f"{name}b{bk}"
            for k in range(NCH):
                mk.op("tensor", lambda e, o=ps[:, 0:TC], l=wo.ap(k, m * 128, (m + 1) * 128),
                      r=cx.ap(k), s=(k == 0), p=(k == NCH - 1):
                      e.matmul(o, l, r, start=s, stop=p),
                      reads=[wo.tag(k), cx.tag(k)], writes=[pstag])
            mk.op("vector", lambda e, o=yc.ap(m), a=ps[:, 0:TC], b=yc.ap(m), s=bo.h[:, m:m+1]:
                  e.scalar_tensor_tensor(o, a, s, b, ALU.add, ALU.add),
                  reads=[pstag, yc.tag(m), bo.tag()], writes=[yc.tag(m)])
        mk.dma("scalar", fm_hbm_view(y2_h)[:, :, t0:t0 + TC], yc.ap3(),
               reads=yc.tags(), writes=[f"{y2_tag}.{it}"])
        tags.append(f"{y2_tag}.{it}")
    return tags


def build_ma1_program():
    nc = bass.Bass("TRN2", target_bir_lowering=False)
    mk = MK(nc)
    x = mk.dram("xt", [1024, 4096], F32, kind="ExternalInput")
    wq = mk.dram("wq", [1024, 1024], F32, kind="ExternalInput")
    wk = mk.dram("wk", [1024, 1024], F32, kind="ExternalInput")
    wv = mk.dram("wv", [1024, 1024], F32, kind="ExternalInput")
    ln1s = mk.dram("ln1s", [128, 8], F32, kind="ExternalInput")
    ln1b = mk.dram("ln1b", [128, 8], F32, kind="ExternalInput")
    idb = mk.dram("idb", [128, 128], BF16, kind="ExternalInput")
    y = mk.dram("y", [1024, 4096], F32, kind="ExternalOutput")
    q = mk.dram("q", [1024, 4096], BF16, kind="ExternalOutput")
    k = mk.dram("k", [1024, 4096], BF16, kind="ExternalOutput")
    vt = mk.dram("vt", [4096, 1024], BF16, kind="ExternalOutput")
    tags = emit_ma1(mk, x, "hbm.xt", y, "hbm.y", q, "hbm.q", k, "hbm.k", vt, "hbm.vt",
                    wq, wk, wv, ln1s, ln1b, idb)
    mk.barrier_read(tags)
    mk.build()
    return nc


def build_ma2_program():
    nc = bass.Bass("TRN2", target_bir_lowering=False)
    mk = MK(nc)
    q = mk.dram("q", [1024, 4096], BF16, kind="ExternalInput")
    k = mk.dram("k", [1024, 4096], BF16, kind="ExternalInput")
    vt = mk.dram("vt", [4096, 1024], BF16, kind="ExternalInput")
    cm = mk.dram("cm", [128, 2048], BF16, kind="ExternalInput")
    ctx = mk.dram("ctx", [1024, 4096], BF16, kind="ExternalOutput")
    tags = emit_ma2(mk, q, "hbm.q", k, "hbm.k", vt, "hbm.vt", ctx, "hbm.ctx", cm)
    mk.barrier_read(tags)
    mk.build()
    return nc


def build_ma3_program():
    nc = bass.Bass("TRN2", target_bir_lowering=False)
    mk = MK(nc)
    y = mk.dram("y", [1024, 4096], F32, kind="ExternalInput")
    ctx = mk.dram("ctx", [1024, 4096], BF16, kind="ExternalInput")
    wo = mk.dram("wo", [1024, 1024], F32, kind="ExternalInput")
    bo = mk.dram("bo", [128, 8], F32, kind="ExternalInput")
    y2 = mk.dram("y2", [1024, 4096], F32, kind="ExternalOutput")
    tags = emit_ma3(mk, y, "hbm.y", ctx, "hbm.ctx", y2, "hbm.y2", wo, bo)
    mk.barrier_read(tags)
    mk.build()
    return nc



def emit_rw1(mk, x_h, x_tag, outs, prm, name="r1"):
    DM, T = 1024, 4096
    NCH = DM // 128
    TC = 512
    NT = T // TC

    x = T2(mk, name + "x", DM, TC, F32)
    xb = T2(mk, name + "xb", DM, TC, BF16)
    xn = T2(mk, name + "xn", DM, TC, BF16)
    u = T2(mk, name + "u", DM, TC, F32)
    rep = T2(mk, name + "rep", 128, 2 * TC, F32)
    rows = T2(mk, name + "rows", 1, 6 * TC, F32)
    xle = T2(mk, name + "xle", DM, TC + 2, BF16)
    dx = T2(mk, name + "dx", DM, TC, F32)
    xmix = T2(mk, name + "xmix", DM, TC, BF16)
    mixt = T2(mk, name + "mixt", 256, TC, BF16)
    wdec = T2(mk, name + "wdec", 64, TC, BF16)
    qout = T2(mk, name + "qout", DM, TC, BF16)
    cpre = T2(mk, name + "cpre", 16, T, F32)
    czer = T2(mk, name + "czer", 16, 512, F32)
    negd = T2(mk, name + "negd", 128, 8 * 16, BF16)
    combc = T2(mk, name + "combc", 128, 8, F32)
    hrow = T2(mk, name + "hrow", 1, 16, F32)
    maa = T2(mk, name + "maa", 128, 48, F32)
    mw1 = T2(mk, name + "mw1", DM, 160, BF16)
    mw2 = T2(mk, name + "mw2", 128, 5 * 1024, BF16)
    dw1 = T2(mk, name + "dw1", DM, 64, BF16)
    dw2 = T2(mk, name + "dw2", 64, DM, BF16)
    dcol = T2(mk, name + "dcol", 128, 8, F32)
    wr = T2(mk, name + "wr", DM, DM, BF16)
    wk = T2(mk, name + "wk", DM, DM, BF16)
    wv = T2(mk, name + "wv", DM, DM, BF16)
    wg = T2(mk, name + "wg", DM, DM, BF16)
    onescol = T2(mk, name + "onc", 128, 1, BF16)
    onesrow_f = T2(mk, name + "onr", 1, 128, F32)

    pb = [mk.psum(f"{name}pb{i}", [128, 512], F32) for i in range(6)]
    B = lambda i: (pb[i], f"{name}b{i}")

    mk.op("gpsimd", lambda e: e.memset(onescol.h[:, :], 1.0), writes=[onescol.tag()])
    mk.op("gpsimd", lambda e: e.memset(onesrow_f.h[:, :], 1.0), writes=[onesrow_f.tag()])
    mk.op("gpsimd", lambda e: e.memset(czer.h[:, :], 0.0), writes=[czer.tag(0)])
    mk.dma("sync", maa.h[:, :], prm["maa_cols"], writes=[maa.tag()])
    mk.dma("gpsimd", mw1.ap3(), fm_hbm_view(prm["maa_w1"]), writes=mw1.tags())
    MIXSL0 = [(0, 0), (0, 32), (0, 64), (1, 0), (1, 32)]
    for f in range(5):
        mofs = MIXSL0[f][1]
        mk.dma("gpsimd", mw2.h[mofs:mofs + 32, f * 1024:(f + 1) * 1024],
               prm["maa_w2"][f, :, :], writes=[mw2.tag(0)])
    mk.dma("gpsimd", dw1.ap3(), fm_hbm_view(prm["dw1"]), writes=dw1.tags())
    mk.dma("gpsimd", dw2.h[:, :], prm["dw2"], writes=[dw2.tag(0)])
    mk.dma("sync", dcol.h[:, :], prm["decay_cols"], writes=[dcol.tag()])
    for w, key in ((wr, "wr"), (wk, "wk"), (wv, "wv"), (wg, "wg")):
        mk.dma("gpsimd", w.ap3(), fm_hbm_view(prm[key]), writes=w.tags())
    mk.dma("gpsimd", hrow.h[0:1, :], prm["hdb"].rearrange("a b -> b a"), writes=[hrow.tag(0)])
    mk.op("tensor", lambda e, o=pb[0][:, 0:16], l=onesrow_f.ap(), r=hrow.ap(0):
          e.matmul(o, l, r, start=True, stop=True),
          reads=[onesrow_f.tag(), hrow.tag(0)], writes=[f"{name}b0"])
    for c in range(NCH):
        for j in range(2):
            mk.op("vector", lambda e, o=combc.h[64 * j:64 * j + 64, c:c + 1],
                  i_=pb[0][64 * j:64 * j + 64, 2 * c + j:2 * c + j + 1]: e.tensor_copy(o, i_),
                  reads=[f"{name}b0"], writes=[combc.tag()])
    mk.op("vector", lambda e, o=combc.h[:, :], a=combc.h[:, :], b=dcol.h[:, :]:
          e.tensor_add(o, a, b), reads=[combc.tag(), dcol.tag()], writes=[combc.tag()])
    mk.op("gpsimd", lambda e: e.memset(negd.h[:, :], 0.0), writes=[negd.tag()])
    for c in range(NCH):
        for j in range(2):
            mk.op("gpsimd", lambda e, o=negd.h[64 * j:64 * j + 64, c * 16 + 2 * c + j: c * 16 + 2 * c + j + 1]:
                  e.memset(o, -1.0 / 64.0), writes=[negd.tag()])

    MAAC = lambda i, c: maa.h[:, i * 8 + c:i * 8 + c + 1]

    for it in range(NT):
        t0 = it * TC
        mk.dma("sync", x.ap3(), fm_hbm_view(x_h)[:, :, t0:t0 + TC],
               reads=[x_tag], writes=x.tags())
        emit_ln(mk, x, xb, xn, u, rows, rep, onescol, onesrow_f,
                B(2), B(3), B(4), B(5), TC)
        mk.dma("scalar", fm_hbm_view(outs["xln"])[:, :, t0:t0 + TC], xn.ap3(),
               reads=xn.tags(), writes=[f"hbm.{name}xln.{it}"])

    for it in range(NT):
        t0 = it * TC
        lo = max(0, t0 - 1)
        hi = min(T, t0 + TC + 1)
        ofs = 1 if t0 == 0 else 0
        if t0 == 0:
            mk.op("gpsimd", lambda e: e.memset(xle.ap3(cs=0, ce=1), 0.0), writes=xle.tags())
        if hi == T:
            mk.op("gpsimd", lambda e: e.memset(xle.ap3(cs=TC + 1, ce=TC + 2), 0.0), writes=xle.tags())
        mk.dma("sync", xle.ap3(cs=ofs, ce=ofs + (hi - lo)),
               fm_hbm_view(outs["xln"])[:, :, lo:hi],
               reads=[f"hbm.{name}xln.{it2}" for it2 in range(NT)], writes=xle.tags())
        for c in range(NCH):
            mk.op("vector", lambda e, o=dx.ap(c), a=xle.ap(c, 0, TC), b=xle.ap(c, 2, TC + 2):
                  e.tensor_add(o, a, b), reads=xle.tags(), writes=[dx.tag(c)])
            mk.op("vector", lambda e, o=dx.ap(c), a=dx.ap(c), b=xle.ap(c, 1, TC + 1):
                  e.scalar_tensor_tensor(o, a, 0.5, b, ALU.mult, ALU.subtract),
                  reads=[dx.tag(c)] + xle.tags(), writes=[dx.tag(c)])
        for c in range(NCH):
            mk.op("vector", lambda e, o=xmix.ap(c), a=dx.ap(c), s=MAAC(0, c), b=xle.ap(c, 1, TC + 1):
                  e.scalar_tensor_tensor(o, a, s, b, ALU.mult, ALU.add),
                  reads=[dx.tag(c), maa.tag()] + xle.tags(), writes=[xmix.tag(c)])
        for part, (m0, mw) in enumerate(((0, 96), (96, 64))):
            bk = part
            for k in range(NCH):
                mk.op("tensor", lambda e, o=pb[bk][0:mw, 0:TC], l=mw1.ap(k, m0, m0 + mw),
                      r=xmix.ap(k), s=(k == 0), p=(k == NCH - 1):
                      e.matmul(o, l, r, start=s, stop=p),
                      reads=[mw1.tag(k), xmix.tag(k)], writes=[f"{name}b{bk}"])
        mk.op("scalar", lambda e, o=mixt.ap(0, 0, TC, p0=0, p1=96), i_=pb[0][0:96, 0:TC]:
              e.activation(o, i_, AF.Tanh), reads=[f"{name}b0"], writes=[mixt.tag(0)])
        mk.op("scalar", lambda e, o=mixt.ap(1, 0, TC, p0=0, p1=64), i_=pb[1][0:64, 0:TC]:
              e.activation(o, i_, AF.Tanh), reads=[f"{name}b1"], writes=[mixt.tag(1)])
        MIXSL = [(0, 0), (0, 32), (0, 64), (1, 0), (1, 32)]
        for f, (proj_w, out_is_g, maa_i) in enumerate(
                ((None, False, 1), (wk, False, 2), (wv, False, 3), (wr, False, 4), (wg, True, 5))):
            mch, mofs = MIXSL[f]
            for m in range(NCH):
                bk = 2 + m % 2
                mk.op("tensor", lambda e, o=pb[bk][:, 0:TC],
                      l=mw2.h[mofs:mofs + 32, f * 1024 + m * 128: f * 1024 + (m + 1) * 128],
                      r=mixt.ap(mch, 0, TC, p0=mofs, p1=mofs + 32),
                      s=True, p=True: e.matmul(o, l, r, start=s, stop=p),
                      reads=[mw2.tag(0), mixt.tag(mch)], writes=[f"{name}b{bk}"])
                mk.op("vector", lambda e, o=u.ap(m), a=pb[bk][:, 0:TC], s=MAAC(maa_i, m), b=dx.ap(m):
                      e.scalar_tensor_tensor(o, a, s, b, ALU.add, ALU.mult),
                      reads=[f"{name}b{bk}", maa.tag(), dx.tag(m)], writes=[u.tag(m)])
                mk.op("gpsimd", lambda e, o=xmix.ap(m), a=u.ap(m), b=xle.ap(m, 1, TC + 1):
                      e.tensor_add(o, a, b),
                      reads=[u.tag(m)] + xle.tags(), writes=[xmix.tag(m)])
            if f == 0:
                for k in range(NCH):
                    mk.op("tensor", lambda e, o=pb[4][0:64, 0:TC], l=dw1.ap(k),
                          r=xmix.ap(k), s=(k == 0), p=(k == NCH - 1):
                          e.matmul(o, l, r, start=s, stop=p),
                          reads=[dw1.tag(k), xmix.tag(k)], writes=[f"{name}b4"])
                mk.op("scalar", lambda e, o=wdec.h[:, :], i_=pb[4][0:64, 0:TC]:
                      e.activation(o, i_, AF.Tanh), reads=[f"{name}b4"], writes=[wdec.tag(0)])
                for m in range(NCH):
                    mk.op("tensor", lambda e, o=pb[5][:, 0:TC], l=dw2.h[:, m * 128:(m + 1) * 128],
                          r=wdec.h[:, :], s=True, p=True: e.matmul(o, l, r, start=s, stop=p),
                          reads=[dw2.tag(0), wdec.tag(0)], writes=[f"{name}b5"])
                    mk.op("scalar", lambda e, o=qout.ap(m), i_=pb[5][:, 0:TC], b=combc.h[:, m:m+1]:
                          e.activation(o, i_, AF.Exp, bias=b),
                          reads=[f"{name}b5", combc.tag()], writes=[qout.tag(m)])
                for m in range(NCH):
                    mk.op("tensor", lambda e, o=pb[4][0:16, 0:TC],
                          l=negd.h[:, m * 16:(m + 1) * 16], r=qout.ap(m),
                          s=(m == 0), p=(m == NCH - 1): e.matmul(o, l, r, start=s, stop=p),
                          reads=[negd.tag(), qout.tag(m)], writes=[f"{name}b4"])
                mk.op("vector", lambda e, o=cpre.ap(0, t0, t0 + TC), i_=pb[4][0:16, 0:TC]:
                      e.tensor_copy(o, i_), reads=[f"{name}b4"], writes=[cpre.tag(0) + f".{it}"])
            else:
                for m in range(NCH):
                    bk2 = 4 + m % 2
                    for k in range(NCH):
                        mk.op("tensor", lambda e, o=pb[bk2][:, 0:TC], l=proj_w.ap(k, m * 128, (m + 1) * 128),
                              r=xmix.ap(k), s=(k == 0), p=(k == NCH - 1):
                              e.matmul(o, l, r, start=s, stop=p),
                              reads=[proj_w.tag(k), xmix.tag(k)], writes=[f"{name}b{bk2}"])
                    mk.op("scalar", lambda e, o=qout.ap(m), i_=pb[bk2][:, 0:TC],
                          fn=(AF.Silu if out_is_g else AF.Identity):
                          e.activation(o, i_, fn),
                          reads=[f"{name}b{bk2}"], writes=[qout.tag(m)])
                okey = {1: "kT", 2: "vT", 3: "rT", 4: "gT"}[f]
                mk.dma("scalar", fm_hbm_view(outs[okey])[:, :, t0:t0 + TC], qout.ap3(),
                       reads=qout.tags(), writes=[f"hbm.{name}{okey}.{it}"])
    mk.dma("gpsimd", combc.h[0:16, 0:1], prm["slow"], writes=[f"{name}slowcol"])
    mk.op("scalar", lambda e, o=combc.h[0:16, 1:2], i_=combc.h[0:16, 0:1]:
          e.activation(o, i_, AF.Sigmoid), reads=[f"{name}slowcol"], writes=[f"{name}slowsig"])
    for sc_i in range(NT):
        init = 0.0 if sc_i == 0 else cpre.h[0:16, sc_i * TC - 1: sc_i * TC]
        mk.op("vector", lambda e, o=cpre.ap(0, sc_i * TC, (sc_i + 1) * TC),
              a=cpre.ap(0, sc_i * TC, (sc_i + 1) * TC), b=czer.h[0:16, :], ini=init:
              e.tensor_tensor_scan(o, a, b, ini, ALU.add, ALU.add),
              reads=[cpre.tag(0) + f".{it}" for it in range(NT)] + [czer.tag(0)]
                    + ([cpre.tag(0)] if sc_i else []),
              writes=[cpre.tag(0)])
    mk.dma("scalar", outs["C"], cpre.h[:, :], reads=[cpre.tag(0)], writes=["hbm.rwC"])
    mk.op("vector", lambda e, o=cpre.h[:, :], a=cpre.h[:, :], s=combc.h[0:16, 1:2]:
          e.tensor_scalar(o, a, s, None, ALU.mult),
          reads=[cpre.tag(0), f"{name}slowsig"], writes=[cpre.tag(0)])
    mk.dma("scalar", outs["Cs"], cpre.h[:, :], reads=[cpre.tag(0)], writes=["hbm.rwCs"])
    return ([f"hbm.{name}{k}.{it}" for k in ("kT", "rT", "gT", "vT") for it in range(NT)]
            + ["hbm.rwC", "hbm.rwCs"])



def emit_rw2(mk, rT_h, kT_h, vT_h, gT_h, C_h, Cs_h, z_h, dmix_h, gam_h, bet_h,
             idf_h, idb_h, name="r2"):
    DM, T, NH = 1024, 4096, 16
    RT = 512
    NR = T // RT
    NSC = T // 128

    rT = T2(mk, name + "rT", 64, T, BF16)
    kT = T2(mk, name + "kT", 64, T, BF16)
    gT = T2(mk, name + "gT", 64, T, BF16)
    vfh = T2(mk, name + "vfh", 64, T, BF16)
    vtk = T2(mk, name + "vtk", T, 64, BF16)
    cc = T2(mk, name + "cc", 16, T, F32)
    cs = T2(mk, name + "cs", 16, T, F32)
    cct = T2(mk, name + "cct", T, 16, F32)
    cst = T2(mk, name + "cst", T, 16, F32)
    c0 = T2(mk, name + "c0", 1, 512, F32)
    cs0 = T2(mk, name + "cs0", 1, 512, F32)
    crep = T2(mk, name + "crep", 128, T, F32)
    csrep = T2(mk, name + "csrep", 128, T, F32)
    bcols = T2(mk, name + "bcols", 128, 4 * 32, F32)
    arep = T2(mk, name + "arep", 128, 2 * RT, F32)
    scol = T2(mk, name + "scol", 128, 8, F32)
    tt_ = T2(mk, name + "tt", 128, 2 * RT, BF16)
    dmt = T2(mk, name + "dmt", 128, 2 * RT, BF16)
    ddt = T2(mk, name + "ddt", 128, RT, F32)
    vstat = T2(mk, name + "vstat", 128, 2 * 128, BF16)
    ycmb = T2(mk, name + "ycmb", 128, RT, F32)
    yh = T2(mk, name + "yh", 64, T, F32)
    ybf = T2(mk, name + "ybf", 64, T, BF16)
    ysq = T2(mk, name + "ysq", 64, T, BF16)
    grow = T2(mk, name + "grow", 1, 6 * 512, F32)
    murep = T2(mk, name + "murep", 64, 512, F32)
    rsrep = T2(mk, name + "rsrep", 64, 512, F32)
    dmix = T2(mk, name + "dmix", 128, 8, F32)
    gam = T2(mk, name + "gam", 128, 8, F32)
    bet = T2(mk, name + "bet", 128, 8, F32)
    idf = T2(mk, name + "idf", 128, 128, F32)
    idb = T2(mk, name + "idb", 128, 128, BF16)
    zt = T2(mk, name + "zt", 64, T, BF16)
    ysl = T2(mk, name + "ysl", 64, RT, F32)
    ddt2 = T2(mk, name + "ddt2", 128, RT, F32)
    hcols = T2(mk, name + "hcols", 64, 4, F32)
    onescol = T2(mk, name + "onc", 128, 1, BF16)
    onesrow_f = T2(mk, name + "onr", 1, 128, F32)

    pb = [mk.psum(f"{name}pb{i}", [128, 512], F32) for i in range(7)]
    pbt = mk.psum(f"{name}pbt", [128, 1024], BF16)
    mk.op("gpsimd", lambda e: e.memset(onescol.h[:, :], 1.0), writes=[onescol.tag()])
    mk.op("gpsimd", lambda e: e.memset(onesrow_f.h[:, :], 1.0), writes=[onesrow_f.tag()])
    mk.dma("sync", cc.h[:, :], C_h, writes=[cc.tag(0)])
    mk.dma("sync", cs.h[:, :], Cs_h, writes=[cs.tag(0)])
    mk.dma("sync", dmix.h[:, :], dmix_h, writes=[dmix.tag()])
    mk.op("scalar", lambda e, o=dmix.h[:, :], i_=dmix.h[:, :]: e.activation(o, i_, AF.Sigmoid),
          reads=[dmix.tag()], writes=[dmix.tag()])
    mk.dma("sync", gam.h[:, :], gam_h, writes=[gam.tag()])
    mk.dma("sync", bet.h[:, :], bet_h, writes=[bet.tag()])
    mk.dma("sync", idf.h[:, :], idf_h, writes=[idf.tag()])
    mk.dma("sync", idb.h[:, :], idb_h, writes=[idb.tag()])
    for src, dst in ((cc, cct), (cs, cst)):
        for st in range(NSC):
            mk.op("tensor", lambda e, o=pb[6][0:128, 0:16],
                  i_=src.ap(0, st * 128, (st + 1) * 128), idn=idf.h[0:16, 0:16]:
                  e.transpose(o, i_, idn), reads=[src.tag(0), idf.tag()], writes=[f"{name}b6"])
            mk.op("vector", lambda e, o=dst.ap(st), i_=pb[6][0:128, 0:16]:
                  e.tensor_copy(o, i_), reads=[f"{name}b6"], writes=[dst.tag(st)])
    cct_tags = [cct.tag(st) for st in range(NSC)]
    cst_tags = [cst.tag(st) for st in range(NSC)]

    GNEPS = 1e-5 * 64.0
    ceps, teps = mk.constcol(GNEPS, parts=1)
    zt_tags = []
    for h in range(NH):
        hofs = 64 * (h % 2)
        for ci, t2src in ((0, dmix), (1, gam), (2, bet)):
            mk.op("vector", lambda e, o=hcols.h[:, ci:ci + 1],
                  i_=t2src.h[hofs:hofs + 64, h // 2:h // 2 + 1]: e.tensor_copy(o, i_),
                  reads=[t2src.tag()], writes=[hcols.tag()])
        mk.dma("sync", rT.h[:, :], rT_h[64 * h:64 * h + 64, :], writes=rT.tags())
        mk.dma("sync", kT.h[:, :], kT_h[64 * h:64 * h + 64, :], writes=kT.tags())
        mk.dma("sync", gT.h[:, :], gT_h[64 * h:64 * h + 64, :], writes=gT.tags())
        mk.dma("sync", vfh.h[:, :], vT_h[64 * h:64 * h + 64, :], writes=vfh.tags())
        for st in range(NSC // 2):
            for half in range(2):
                st2 = 2 * st + half
                mk.op("tensor", lambda e, o=pbt[:, half * 64:half * 64 + 64],
                      i_=vfh.h[:, st2 * 128:(st2 + 1) * 128], idn=idb.h[0:64, 0:64]:
                      e.transpose(o, i_, idn),
                      reads=vfh.tags() + [idb.tag()], writes=[f"{name}bt"])
            mk.op("vector", lambda e, o=vtk.ap(2 * st), i_=pbt[:, 0:64]:
                  e.tensor_copy(o, i_), reads=[f"{name}bt"], writes=[vtk.tag(2 * st)])
            mk.op("vector", lambda e, o=vtk.ap(2 * st + 1), i_=pbt[:, 64:128]:
                  e.tensor_copy(o, i_), reads=[f"{name}bt"], writes=[vtk.tag(2 * st + 1)])
        for tc in range(T // 512):
            mk.dma("gpsimd", c0.h[0:1, :], C_h[h:h + 1, tc * 512:(tc + 1) * 512],
                   writes=[c0.tag(0)])
            mk.op("tensor", lambda e, o=pb[6][:, 0:512], l=onesrow_f.ap(),
                  r=c0.ap(0): e.matmul(o, l, r, start=True, stop=True),
                  reads=[onesrow_f.tag(), c0.tag(0)], writes=[f"{name}b6"])
            mk.op("vector", lambda e, o=crep.ap(0, tc * 512, (tc + 1) * 512), i_=pb[6][:, 0:512]:
                  e.tensor_copy(o, i_), reads=[f"{name}b6"], writes=[crep.tag(0)])
            mk.dma("gpsimd", cs0.h[0:1, :], Cs_h[h:h + 1, tc * 512:(tc + 1) * 512],
                   writes=[cs0.tag(0)])
            mk.op("tensor", lambda e, o=pb[6][:, 0:512], l=onesrow_f.ap(),
                  r=cs0.ap(0): e.matmul(o, l, r, start=True, stop=True),
                  reads=[onesrow_f.tag(), cs0.tag(0)], writes=[f"{name}b6"])
            mk.op("vector", lambda e, o=csrep.ap(0, tc * 512, (tc + 1) * 512), i_=pb[6][:, 0:512]:
                  e.tensor_copy(o, i_), reads=[f"{name}b6"], writes=[csrep.tag(0)])

        for R in range(NR):
            t0 = R * RT
            RLO, RUP, RLOS, RUPS, NLO, NUP, NLOS, NUPS = range(8)
            sct = lambda i: f"{name}scol.{i}"
            for (idx, nidx, reptile, rtag, pos) in (
                    (RLO, NLO, crep, crep.tag(0), t0), (RUP, NUP, crep, crep.tag(0), t0 + RT - 1),
                    (RLOS, NLOS, csrep, csrep.tag(0), t0), (RUPS, NUPS, csrep, csrep.tag(0), t0 + RT - 1)):
                mk.op("vector", lambda e, o=scol.h[:, idx:idx + 1], i_=reptile.ap(0, pos, pos + 1):
                      e.tensor_copy(o, i_), reads=[rtag], writes=[sct(idx)])
                mk.op("vector", lambda e, o=scol.h[:, nidx:nidx + 1], i_=scol.h[:, idx:idx + 1]:
                      e.tensor_scalar(o, i_, -1.0, None, ALU.mult),
                      reads=[sct(idx)], writes=[sct(nidx)])
            for (ofs, t2, t2tags, refidx, sc) in (
                    (0, cct, cct_tags, RLO, -1.0), (32, cst, cst_tags, RLOS, -1.0),
                    (64, cct, cct_tags, RUP, 1.0), (96, cst, cst_tags, RUPS, 1.0)):
                bt = f"{name}bcols.{ofs}"
                clampop = ALU.max if sc < 0 else ALU.min
                mk.op("vector", lambda e, o=bcols.h[:, ofs:ofs + 32], a=t2.h[:, h::16],
                      s=scol.h[:, refidx:refidx + 1], cl=clampop:
                      e.tensor_scalar(o, a, s, 0.0, ALU.subtract, cl),
                      reads=t2tags + [sct(refidx)], writes=[bt])
                mk.op("scalar", lambda e, o=bcols.h[:, ofs:ofs + 32], i_=bcols.h[:, ofs:ofs + 32],
                      sc=sc: e.activation(o, i_, AF.Exp, scale=sc),
                      reads=[bt], writes=[bt])
            mk.op("scalar", lambda e, o=arep.ap(0, 0, RT, p0=0, p1=64),
                  i_=crep.ap(0, t0, t0 + RT, p0=0, p1=64), b=scol.h[0:64, NLO:NLO + 1]:
                  e.activation(o, i_, AF.Exp, bias=b),
                  reads=[crep.tag(0), sct(NLO)], writes=[f"{name}arep.lo"])
            mk.op("scalar", lambda e, o=arep.ap(0, 0, RT, p0=64, p1=128),
                  i_=csrep.ap(0, t0, t0 + RT, p0=64, p1=128), b=scol.h[64:128, NLOS:NLOS + 1]:
                  e.activation(o, i_, AF.Exp, bias=b),
                  reads=[csrep.tag(0), sct(NLOS)], writes=[f"{name}arep.lo"])
            mk.op("scalar", lambda e, o=arep.ap(0, RT, 2 * RT, p0=0, p1=64),
                  i_=crep.ap(0, t0, t0 + RT, p0=0, p1=64), b=scol.h[0:64, RUP:RUP + 1]:
                  e.activation(o, i_, AF.Exp, bias=b, scale=-1.0),
                  reads=[crep.tag(0), sct(RUP)], writes=[f"{name}arep.up"])
            mk.op("scalar", lambda e, o=arep.ap(0, RT, 2 * RT, p0=64, p1=128),
                  i_=csrep.ap(0, t0, t0 + RT, p0=64, p1=128), b=scol.h[64:128, RUPS:RUPS + 1]:
                  e.activation(o, i_, AF.Exp, bias=b, scale=-1.0),
                  reads=[csrep.tag(0), sct(RUPS)], writes=[f"{name}arep.up"])

            nlo = nup = nnear = 0
            tot_lo, tot_up = 4 * R, NSC - 4 * R - 4
            for C in range(NSC):
                kind = "lo" if C < 4 * R else ("near" if C < 4 * R + 4 else "up")
                par = C % 2
                ttile = tt_.ap(0, par * RT, (par + 1) * RT)
                tttag = f"{name}tt.{par}"
                ps, pstag = pb[par], f"{name}b{par}"
                mk.op("tensor", lambda e, o=ps[:, 0:RT], l=kT.h[:, C * 128:(C + 1) * 128],
                      r=rT.h[:, t0:t0 + RT]: e.matmul(o, l, r, start=True, stop=True),
                      reads=kT.tags() + rT.tags(), writes=[pstag])
                mk.op("scalar", lambda e, o=ttile, i_=ps[:, 0:RT]:
                      e.activation(o, i_, AF.Identity, scale=1.0 / 8.0),
                      reads=[pstag], writes=[tttag])
                if kind in ("lo", "up"):
                    boff = 0 if kind == "lo" else 64
                    ybank, ytag = (2, f"{name}b2") if kind == "lo" else (3, f"{name}b3")
                    vs = vstat.ap(0, par * 128, (par + 1) * 128)
                    vstag = f"{name}vstat.{par}"
                    mk.op("vector", lambda e, o=vstat.ap(0, par * 128, par * 128 + 64),
                          a=vtk.ap(C), s=bcols.h[:, boff + C:boff + C + 1]:
                          e.tensor_scalar(o, a, s, None, ALU.mult),
                          reads=[vtk.tag(C), f"{name}bcols.{boff}"], writes=[vstag])
                    mk.op("vector", lambda e, o=vstat.ap(0, par * 128 + 64, par * 128 + 128),
                          a=vtk.ap(C), s=bcols.h[:, boff + 32 + C:boff + 32 + C + 1]:
                          e.tensor_scalar(o, a, s, None, ALU.mult),
                          reads=[vtk.tag(C), f"{name}bcols.{boff + 32}"], writes=[vstag])
                    cnt = nlo if kind == "lo" else nup
                    tot = tot_lo if kind == "lo" else tot_up
                    mk.op("tensor", lambda e, o=pb[ybank][:, 0:RT], l=vs, r=ttile,
                          s=(cnt == 0), p=(cnt == tot - 1): e.matmul(o, l, r, start=s, stop=p),
                          reads=[vstag, tttag], writes=[ytag])
                    if kind == "lo":
                        nlo += 1
                    else:
                        nup += 1
                else:
                    for (mu, reptile, rtags2, t2, t2tags, bank, btag) in (
                            ("f", crep, [crep.tag(0)], cct, cct_tags, 4, f"{name}b4"),
                            ("s", csrep, [csrep.tag(0)], cst, cst_tags, 5, f"{name}b5")):
                        dmf = dmt.ap(0, (0 if mu == "f" else 1) * RT,
                                     ((0 if mu == "f" else 1) + 1) * RT)
                        dmftag = f"{name}dmt.{mu}"
                        mk.op("vector", lambda e, o=ddt.h[:, :], a=reptile.ap(0, t0, t0 + RT),
                              s=t2.h[:, h::16][:, C:C + 1]:
                              e.tensor_scalar(o, a, s, None, ALU.subtract),
                              reads=rtags2 + t2tags, writes=[f"{name}ddt"])
                        mk.op("gpsimd", lambda e, o=ddt2.h[:, :], a=ddt.h[:, :]:
                              e.tensor_scalar(o, a, -1.0, None, ALU.mult),
                              reads=[f"{name}ddt"], writes=[f"{name}ddt2"])
                        mk.op("vector", lambda e, o=ddt.h[:, :], a=ddt.h[:, :], b=ddt2.h[:, :]:
                              e.tensor_max(o, a, b),
                              reads=[f"{name}ddt", f"{name}ddt2"], writes=[f"{name}ddt"])
                        mk.op("scalar", lambda e, o=dmf, i_=ddt.h[:, :]:
                              e.activation(o, i_, AF.Exp, scale=-1.0),
                              reads=[f"{name}ddt"], writes=[dmftag])
                        mk.op("gpsimd", lambda e, o=dmf, a=dmf, b=ttile: e.tensor_mul(o, a, b),
                              reads=[dmftag, tttag], writes=[dmftag])
                        vs2 = vstat.ap(0, par * 128, par * 128 + 64)
                        vstag = f"{name}vstat.{par}"
                        if mu == "f":
                            mk.op("vector", lambda e, o=vs2, i_=vtk.ap(C):
                                  e.tensor_copy(o, i_), reads=[vtk.tag(C)], writes=[vstag])
                        mk.op("tensor", lambda e, o=pb[bank][0:64, 0:RT], l=vs2, r=dmf,
                              s=(nnear == 0), p=(nnear == 3): e.matmul(o, l, r, start=s, stop=p),
                              reads=[vstag, dmftag], writes=[btag])
                    nnear += 1
            ct = f"{name}ycmb"
            if tot_lo > 0:
                mk.op("vector", lambda e, o=ycmb.h[:, :], a=pb[2][:, 0:RT], b=arep.ap(0, 0, RT):
                      e.tensor_mul(o, a, b),
                      reads=[f"{name}b2", f"{name}arep.lo"], writes=[ct])
                if tot_up > 0:
                    mk.op("vector", lambda e, o=ddt.h[:, :], a=pb[3][:, 0:RT], b=arep.ap(0, RT, 2 * RT):
                          e.tensor_mul(o, a, b),
                          reads=[f"{name}b3", f"{name}arep.up"], writes=[f"{name}ddt"])
                    mk.op("vector", lambda e, o=ycmb.h[:, :], a=ycmb.h[:, :], b=ddt.h[:, :]:
                          e.tensor_add(o, a, b), reads=[ct, f"{name}ddt"], writes=[ct])
            else:
                mk.op("vector", lambda e, o=ycmb.h[:, :], a=pb[3][:, 0:RT], b=arep.ap(0, RT, 2 * RT):
                      e.tensor_mul(o, a, b),
                      reads=[f"{name}b3", f"{name}arep.up"], writes=[ct])
            mk.op("vector", lambda e, o=ycmb.h[0:64, :], a=ycmb.h[0:64, :], b=pb[4][0:64, 0:RT]:
                  e.tensor_add(o, a, b), reads=[ct, f"{name}b4"], writes=[ct])
            mk.op("vector", lambda e, o=ycmb.h[64:128, :], a=ycmb.h[64:128, :], b=pb[5][0:64, 0:RT]:
                  e.tensor_add(o, a, b), reads=[ct, f"{name}b5"], writes=[ct])
            mk.op("vector", lambda e, o=ysl.h[:, :], i_=ycmb.h[64:128, :]: e.tensor_copy(o, i_),
                  reads=[ct], writes=[ysl.tag(0)])
            mk.op("vector", lambda e, o=ddt.h[0:64, :], a=ycmb.h[0:64, :], b=ysl.h[:, :]:
                  e.tensor_sub(o, a, b), reads=[ct, ysl.tag(0)], writes=[f"{name}ddt"])
            mk.op("vector", lambda e, o=yh.ap(0, t0, t0 + RT), a=ddt.h[0:64, :],
                  s=hcols.h[:, 0:1], b=ysl.h[:, :]:
                  e.scalar_tensor_tensor(o, a, s, b, ALU.mult, ALU.add),
                  reads=[f"{name}ddt", ysl.tag(0), hcols.tag()], writes=[yh.tag(0)])
        mk.op("scalar", lambda e, o=ybf.h[:, :], i_=yh.h[:, :]: e.activation(o, i_, AF.Identity),
              reads=[yh.tag(0)], writes=[ybf.tag(0)])
        mk.op("scalar", lambda e, o=ysq.h[:, :], i_=ybf.h[:, :]: e.activation(o, i_, AF.Square),
              reads=[ybf.tag(0)], writes=[ysq.tag(0)])
        gcol = lambda t2: hcols.h[:, 1:2] if t2 is gam else hcols.h[:, 2:3]
        for tc in range(T // 512):
            sl = slice(tc * 512, (tc + 1) * 512)
            MU = grow.ap(0, 0, 512); SQ = grow.ap(0, 512, 1024)
            V_ = grow.ap(0, 1024, 1536); RS = grow.ap(0, 1536, 2048)
            MN = grow.ap(0, 2048, 2560)
            mk.op("tensor", lambda e, o=pb[6][0:1, 0:512], l=onescol.ap(p1=64),
                  r=ybf.h[:, sl]: e.matmul(o, l, r, start=True, stop=True),
                  reads=[ybf.tag(0), onescol.tag()], writes=[f"{name}b6"])
            mk.op("vector", lambda e, o=MU, i_=pb[6][0:1, 0:512]:
                  e.tensor_scalar(o, i_, 1.0 / 64.0, None, ALU.mult),
                  reads=[f"{name}b6"], writes=[f"{name}grow.mu"])
            mk.op("tensor", lambda e, o=pb[6][0:1, 0:512], l=onescol.ap(p1=64),
                  r=ysq.h[:, sl]: e.matmul(o, l, r, start=True, stop=True),
                  reads=[ysq.tag(0), onescol.tag(), f"{name}grow.mu"], writes=[f"{name}b6"])
            mk.op("vector", lambda e, o=SQ, i_=pb[6][0:1, 0:512]:
                  e.tensor_scalar(o, i_, 1.0 / 64.0, None, ALU.mult),
                  reads=[f"{name}b6"], writes=[f"{name}grow.sq"])
            mk.op("vector", lambda e, o=V_, a=MU, b=MU: e.tensor_mul(o, a, b),
                  reads=[f"{name}grow.mu"], writes=[f"{name}grow.v"])
            mk.op("vector", lambda e, o=V_, a=SQ, b=V_: e.tensor_sub(o, a, b),
                  reads=[f"{name}grow.sq", f"{name}grow.v"], writes=[f"{name}grow.v"])
            mk.op("scalar", lambda e, o=RS, i_=V_, b=ceps: e.activation(o, i_, AF.Sqrt, bias=b),
                  reads=[f"{name}grow.v", teps], writes=[f"{name}grow.rs"])
            mk.op("vector", lambda e, o=RS, i_=RS: e.reciprocal(o, i_),
                  reads=[f"{name}grow.rs"], writes=[f"{name}grow.rs"])
            mk.op("vector", lambda e, o=MN, a=MU: e.tensor_scalar(o, a, -1.0, None, ALU.mult),
                  reads=[f"{name}grow.mu"], writes=[f"{name}grow.mn"])
            mk.op("tensor", lambda e, o=pb[6][0:64, 0:512], l=onesrow_f.ap(ce=64), r=RS:
                  e.matmul(o, l, r, start=True, stop=True),
                  reads=[onesrow_f.tag(), f"{name}grow.rs"], writes=[f"{name}b6"])
            mk.op("vector", lambda e, o=rsrep.h[:, :], i_=pb[6][0:64, 0:512]:
                  e.tensor_copy(o, i_), reads=[f"{name}b6"], writes=[rsrep.tag(0)])
            mk.op("tensor", lambda e, o=pb[6][0:64, 0:512], l=onesrow_f.ap(ce=64), r=MN:
                  e.matmul(o, l, r, start=True, stop=True),
                  reads=[onesrow_f.tag(), f"{name}grow.mn", rsrep.tag(0)], writes=[f"{name}b6"])
            mk.op("vector", lambda e, o=murep.h[:, :], i_=pb[6][0:64, 0:512]:
                  e.tensor_copy(o, i_), reads=[f"{name}b6"], writes=[murep.tag(0)])
            mk.op("vector", lambda e, o=yh.h[:, sl], a=yh.h[:, sl], b=murep.h[:, :]:
                  e.tensor_add(o, a, b), reads=[yh.tag(0), murep.tag(0)], writes=[yh.tag(0)])
            mk.op("gpsimd", lambda e, o=yh.h[:, sl], a=yh.h[:, sl], b=rsrep.h[:, :]:
                  e.tensor_mul(o, a, b), reads=[yh.tag(0), rsrep.tag(0)], writes=[yh.tag(0)])
            mk.op("scalar", lambda e, o=ysq.h[:, sl], i_=yh.h[:, sl], s=gcol(gam), b=gcol(bet):
                  e.activation(o, i_, AF.Identity, scale=s, bias=b),
                  reads=[yh.tag(0), hcols.tag()], writes=[ysq.tag(0)])
            mk.op("vector", lambda e, o=zt.h[:, sl], a=ysq.h[:, sl], b=gT.h[:, sl]:
                  e.tensor_mul(o, a, b), reads=[ysq.tag(0)] + gT.tags(), writes=[zt.tag(0)])
        mk.dma("scalar", z_h[64 * h:64 * h + 64, :], zt.h[:, :],
               reads=[zt.tag(0)], writes=[f"hbm.{name}z.{h}"])
        zt_tags.append(f"hbm.{name}z.{h}")
    return zt_tags


def emit_wo_residual(mk, x_h, x_tag, z_h, z_tag, xout_h, xout_tag, wo_h,
                     scale=None, name="r3"):
    DM, T = 1024, 4096
    NCH = DM // 128
    TC = 512
    wo = T2(mk, name + "wo", DM, DM, BF16)
    cx = T2(mk, name + "cx", DM, TC, BF16)
    yc = T2(mk, name + "yc", DM, TC, F32)
    pb = [mk.psum(f"{name}pb{i}", [128, 512], F32) for i in range(4)]
    mk.dma("gpsimd", wo.ap3(), fm_hbm_view(wo_h), writes=wo.tags())
    tags = []
    for it in range(T // TC):
        t0 = it * TC
        mk.dma("sync", cx.ap3(), fm_hbm_view(z_h)[:, :, t0:t0 + TC],
               reads=[z_tag], writes=cx.tags())
        mk.dma("scalar", yc.ap3(), fm_hbm_view(x_h)[:, :, t0:t0 + TC],
               reads=[x_tag], writes=yc.tags())
        for m in range(NCH):
            bk = m % 2
            ps, pstag = pb[bk], f"{name}b{bk}"
            for k in range(NCH):
                mk.op("tensor", lambda e, o=ps[:, 0:TC], l=wo.ap(k, m * 128, (m + 1) * 128),
                      r=cx.ap(k), s=(k == 0), p=(k == NCH - 1):
                      e.matmul(o, l, r, start=s, stop=p),
                      reads=[wo.tag(k), cx.tag(k)], writes=[pstag])
            mk.op("vector", lambda e, o=yc.ap(m), a=ps[:, 0:TC], b=yc.ap(m):
                  e.tensor_add(o, a, b),
                  reads=[pstag, yc.tag(m)], writes=[yc.tag(m)])
        mk.dma("scalar", fm_hbm_view(xout_h)[:, :, t0:t0 + TC], yc.ap3(),
               reads=yc.tags(), writes=[f"{xout_tag}.{it}"])
        tags.append(f"{xout_tag}.{it}")
    return tags


def build_rw1_program():
    nc = bass.Bass("TRN2", target_bir_lowering=False)
    mk = MK(nc)
    x = mk.dram("xt", [1024, 4096], F32, kind="ExternalInput")
    prm = {
        "maa_cols": mk.dram("maa_cols", [128, 48], F32, kind="ExternalInput"),
        "maa_w1": mk.dram("maa_w1", [1024, 160], F32, kind="ExternalInput"),
        "maa_w2": mk.dram("maa_w2", [5, 32, 1024], F32, kind="ExternalInput"),
        "dw1": mk.dram("dw1", [1024, 64], F32, kind="ExternalInput"),
        "dw2": mk.dram("dw2", [64, 1024], F32, kind="ExternalInput"),
        "decay_cols": mk.dram("decay_cols", [128, 8], F32, kind="ExternalInput"),
        "hdb": mk.dram("hdb", [16, 1], F32, kind="ExternalInput"),
        "slow": mk.dram("slow", [16, 1], F32, kind="ExternalInput"),
        "wr": mk.dram("wr", [1024, 1024], F32, kind="ExternalInput"),
        "wk": mk.dram("wk", [1024, 1024], F32, kind="ExternalInput"),
        "wv": mk.dram("wv", [1024, 1024], F32, kind="ExternalInput"),
        "wg": mk.dram("wg", [1024, 1024], F32, kind="ExternalInput"),
    }
    outs = {
        "xln": mk.dram("xln", [1024, 4096], BF16, kind="ExternalOutput"),
        "rT": mk.dram("rT", [1024, 4096], BF16, kind="ExternalOutput"),
        "kT": mk.dram("kT", [1024, 4096], BF16, kind="ExternalOutput"),
        "gT": mk.dram("gT", [1024, 4096], BF16, kind="ExternalOutput"),
        "vT": mk.dram("vT", [1024, 4096], BF16, kind="ExternalOutput"),
        "C": mk.dram("C", [16, 4096], F32, kind="ExternalOutput"),
        "Cs": mk.dram("Cs", [16, 4096], F32, kind="ExternalOutput"),
    }
    tags = emit_rw1(mk, x, "hbm.xt", outs, prm)
    mk.barrier_read(tags)
    mk.build()
    return nc


def build_rw2_program():
    nc = bass.Bass("TRN2", target_bir_lowering=False)
    mk = MK(nc)
    rT = mk.dram("rT", [1024, 4096], BF16, kind="ExternalInput")
    kT = mk.dram("kT", [1024, 4096], BF16, kind="ExternalInput")
    vt = mk.dram("vT", [1024, 4096], BF16, kind="ExternalInput")
    gT = mk.dram("gT", [1024, 4096], BF16, kind="ExternalInput")
    C = mk.dram("C", [16, 4096], F32, kind="ExternalInput")
    Cs = mk.dram("Cs", [16, 4096], F32, kind="ExternalInput")
    dmix = mk.dram("dmix", [128, 8], F32, kind="ExternalInput")
    gam = mk.dram("gam", [128, 8], F32, kind="ExternalInput")
    bet = mk.dram("bet", [128, 8], F32, kind="ExternalInput")
    idf = mk.dram("idf", [128, 128], F32, kind="ExternalInput")
    idb = mk.dram("idb", [128, 128], BF16, kind="ExternalInput")
    z = mk.dram("z", [1024, 4096], BF16, kind="ExternalOutput")
    tags = emit_rw2(mk, rT, kT, vt, gT, C, Cs, z, dmix, gam, bet, idf, idb)
    mk.barrier_read(tags)
    mk.build()
    return nc


def build_rw3_program():
    nc = bass.Bass("TRN2", target_bir_lowering=False)
    mk = MK(nc)
    x = mk.dram("xt", [1024, 4096], F32, kind="ExternalInput")
    z = mk.dram("z", [1024, 4096], BF16, kind="ExternalInput")
    wo = mk.dram("wo", [1024, 1024], F32, kind="ExternalInput")
    xo = mk.dram("xo", [1024, 4096], F32, kind="ExternalOutput")
    tags = emit_wo_residual(mk, x, "hbm.xt", z, "hbm.z", xo, "hbm.xo", wo)
    mk.barrier_read(tags)
    mk.build()
    return nc



def build_mega_program():
    nc = bass.Bass("TRN2", target_bir_lowering=False)
    mk = MK(nc)
    D = lambda n, sh, dt, kind="ExternalInput": mk.dram(n, sh, dt, kind=kind)
    F = lambda n, sh: D(n, sh, F32)

    xt = F("xt", [1024, 4096])
    xo = D("xo", [1024, 4096], F32, kind="ExternalOutput")
    idf = F("idf", [128, 128])
    idb = D("idb", [128, 128], BF16)
    f_wg = D("f_wg", [1024, 3072], I32); f_wu = D("f_wu", [1024, 3072], I32)
    f_wd = D("f_wd", [3072, 1024], I32)
    prm = {"maa_cols": F("r_maa", [128, 48]), "maa_w1": F("r_w1", [1024, 160]),
           "maa_w2": F("r_w2", [5, 32, 1024]), "dw1": F("r_dw1", [1024, 64]),
           "dw2": F("r_dw2", [64, 1024]), "decay_cols": F("r_dec", [128, 8]),
           "hdb": F("r_hdb", [16, 1]), "slow": F("r_slow", [16, 1]),
           "wr": F("r_wr", [1024, 1024]), "wk": F("r_wk", [1024, 1024]),
           "wv": F("r_wv", [1024, 1024]), "wg": F("r_wgx", [1024, 1024])}
    r_wo = F("r_wo", [1024, 1024])
    r_dmix = F("r_dmix", [128, 8]); r_gam = F("r_gam", [128, 8]); r_bet = F("r_bet", [128, 8])
    n_wq = F("n_wq", [1024, 1024]); n_wk = F("n_wk", [1024, 1024])
    n_wv = F("n_wv", [1024, 1024]); n_wo = F("n_wo", [1024, 1024])
    n_ln1s = F("n_ln1s", [128, 8]); n_ln1b = F("n_ln1b", [128, 8])
    n_bo = F("n_bo", [128, 8]); n_gw = F("n_gw", [1024, 8])
    n_w1 = F("n_w1", [8, 1024, 2048]); n_w2 = F("n_w2", [8, 1024, 2048])
    n_w3 = F("n_w3", [8, 2048, 1024])
    n_ln2s = F("n_ln2s", [128, 8]); n_ln2b = F("n_ln2b", [128, 8])
    n_cm = D("n_cm", [128, 2048], BF16)
    p_wq = F("p_wq", [1024, 4096]); p_wk = F("p_wk", [1024, 4096])
    p_wv = F("p_wv", [1024, 4096]); p_wo = F("p_wo", [4096, 1024])
    p_bq = F("p_bq", [128, 32]); p_bk = F("p_bk", [128, 32]); p_bv = F("p_bv", [128, 32])
    p_bo = F("p_bo", [128, 8]); p_ph = F("p_ph", [128, 32])

    I = lambda n, sh, dt=F32: mk.dram(n, sh, dt, kind="Internal")
    x1, x2, x3, x4, x5 = (I(f"x{i}", [1024, 4096]) for i in range(1, 6))
    rw_outs = {"xln": I("i_xln", [1024, 4096], BF16), "rT": I("i_rT", [1024, 4096], BF16),
               "kT": I("i_kT", [1024, 4096], BF16), "gT": I("i_gT", [1024, 4096], BF16),
               "vT": I("i_vT", [1024, 4096], BF16), "C": I("i_C", [16, 4096], F32),
               "Cs": I("i_Cs", [16, 4096], F32)}
    i_z = I("i_z", [1024, 4096], BF16)
    mini_bufs = {}
    for tag in ("a", "b"):
        mini_bufs[tag] = {"y": I(f"i_y{tag}", [1024, 4096], F32),
                          "q": I(f"i_q{tag}", [1024, 4096], BF16),
                          "k": I(f"i_k{tag}", [1024, 4096], BF16),
                          "vt": I(f"i_vt{tag}", [4096, 1024], BF16),
                          "ctx": I(f"i_ctx{tag}", [1024, 4096], BF16),
                          "y2": I(f"i_y2{tag}", [1024, 4096], F32)}
    i_oh = I("i_oh", [4, 1024, 4096], BF16)

    def scope(fn):
        mk.push_scope()
        fn()
        mk.pop_scope()

    scope(lambda: emit_ffn0(mk, xt, "hbm.xt", x1, "hbm.x1", f_wg, f_wu, f_wd, name="f1"))
    scope(lambda: emit_rw1(mk, x1, "hbm.x1", rw_outs, prm, name="r1"))
    scope(lambda: emit_rw2(mk, rw_outs["rT"], rw_outs["kT"], rw_outs["vT"], rw_outs["gT"],
                           rw_outs["C"], rw_outs["Cs"], i_z, r_dmix, r_gam, r_bet,
                           idf, idb, name="r2"))
    scope(lambda: emit_wo_residual(mk, x1, "hbm.x1", i_z, "hbm.i_z", x2, "hbm.x2",
                                   r_wo, name="r3"))
    scope(lambda: emit_ffn0(mk, x2, "hbm.x2", x3, "hbm.x3", f_wg, f_wu, f_wd, name="f3"))

    def mini(tag, xin, xin_t, xout, xout_t):
        b = mini_bufs[tag]
        scope(lambda: emit_ma1(mk, xin, xin_t, b["y"], f"hbm.i_y{tag}", b["q"], f"hbm.i_q{tag}",
                               b["k"], f"hbm.i_k{tag}", b["vt"], f"hbm.i_vt{tag}",
                               n_wq, n_wk, n_wv, n_ln1s, n_ln1b, idb, name=f"n1{tag}"))
        scope(lambda: emit_ma2(mk, b["q"], f"hbm.i_q{tag}", b["k"], f"hbm.i_k{tag}",
                               b["vt"], f"hbm.i_vt{tag}", b["ctx"], f"hbm.i_ctx{tag}",
                               n_cm, name=f"n2{tag}"))
        scope(lambda: emit_ma3(mk, b["y"], f"hbm.i_y{tag}", b["ctx"], f"hbm.i_ctx{tag}",
                               b["y2"], f"hbm.i_y2{tag}", n_wo, n_bo, name=f"n3{tag}"))
        scope(lambda: emit_moe(mk, xin, xin_t, b["y2"], f"hbm.i_y2{tag}", xout, xout_t,
                               n_gw, n_w1, n_w2, n_w3, n_ln2s, n_ln2b, idf, name=f"m{tag}"))

    mini("a", x3, "hbm.x3", x4, "hbm.x4")
    scope(lambda: emit_pmx_a(mk, x4, "hbm.x4", i_oh, "hbm.i_oh", p_wq, p_wk, p_wv,
                             p_bq, p_bk, p_bv, p_ph, idf, name="pa"))
    scope(lambda: emit_pmx_b(mk, x4, "hbm.x4", i_oh, "hbm.i_oh", x5, "hbm.x5",
                             p_wo, p_bo, name="pb"))
    mini("b", x5, "hbm.x5", xo, "hbm.xo")
    mk.barrier_read(["hbm.xo"])
    mk.build()
    return nc
from concourse.bass_utils import run_bass_kernel_spmd
import ml_dtypes

_PROGRAMS = {}

def _prog(name, builder):
    if name not in _PROGRAMS:
        _PROGRAMS[name] = builder()
    return _PROGRAMS[name]


def _colstack(v):
    v = np.asarray(v, np.float32).reshape(-1)
    return np.ascontiguousarray(v.reshape(-1, 128).T)


def _cmask_np():
    cm = np.zeros((128, 2048), dtype=np.float32)
    for j in range(4):
        for srow in range(128):
            if 128 * j + srow < 512:
                cm[srow, j * 512 + 128 * j + srow:(j + 1) * 512] = 1
    return cm.astype(ml_dtypes.bfloat16)


def kernel(**inp):
    x0 = inp["L0_attn_x"]
    idf = np.eye(128, dtype=np.float32)
    idb = idf.astype(ml_dtypes.bfloat16)
    tp = lambda a: np.ascontiguousarray(np.asarray(a).T)
    A = lambda k: np.asarray(inp[k])

    shared = {
        "idf": idf, "idb": idb,
        "f_wg": tp(inp["L0_ffn_w_gate"]), "f_wu": tp(inp["L0_ffn_w_up"]),
        "f_wd": tp(inp["L0_ffn_w_down"]),
        "r_maa": np.concatenate([_colstack(inp[f"L0_attn_time_maa_{n}"]) for n in "xwkvrg"], axis=1),
        "r_w1": A("L0_attn_time_maa_w1"), "r_w2": A("L0_attn_time_maa_w2"),
        "r_dw1": A("L0_attn_time_decay_w1"), "r_dw2": A("L0_attn_time_decay_w2"),
        "r_dec": _colstack(inp["L0_attn_time_decay"]),
        "r_hdb": np.asarray(inp["L0_attn_head_decay_bias"], np.float32).reshape(16, 1),
        "r_slow": np.asarray(inp["L0_attn_slow_scale"], np.float32).reshape(16, 1),
        "r_wr": A("L0_attn_W_r"), "r_wk": A("L0_attn_W_k"), "r_wv": A("L0_attn_W_v"),
        "r_wgx": A("L0_attn_W_g"), "r_wo": A("L0_attn_W_o"),
        "r_dmix": _colstack(inp["L0_attn_decay_mix"]),
        "r_gam": _colstack(inp["L0_attn_ln_gamma"]), "r_bet": _colstack(inp["L0_attn_ln_beta"]),
        "n_wq": A("L1_ffn_wq"), "n_wk": A("L1_ffn_wk"), "n_wv": A("L1_ffn_wv"),
        "n_wo": A("L1_ffn_wo"), "n_bo": _colstack(inp["L1_ffn_bo"]),
        "n_ln1s": _colstack(inp["L1_ffn_ln1_scale"]), "n_ln1b": _colstack(inp["L1_ffn_ln1_shift"]),
        "n_gw": A("L1_ffn_gate_w"), "n_w1": A("L1_ffn_w1"), "n_w2": A("L1_ffn_w2"),
        "n_w3": A("L1_ffn_w3"),
        "n_ln2s": _colstack(inp["L1_ffn_ln2_scale"]), "n_ln2b": _colstack(inp["L1_ffn_ln2_shift"]),
        "n_cm": _cmask_np(),
        "p_wq": A("L1_attn_Wq"), "p_wk": A("L1_attn_Wk"), "p_wv": A("L1_attn_Wv"),
        "p_wo": A("L1_attn_Wo"),
        "p_bq": _colstack(inp["L1_attn_bq"]), "p_bk": _colstack(inp["L1_attn_bk"]),
        "p_bv": _colstack(inp["L1_attn_bv"]), "p_bo": _colstack(inp["L1_attn_bo"]),
        "p_ph": _colstack(np.asarray(inp["L1_attn_phase"]).reshape(-1)),
    }
    nc = _prog("mega", build_mega_program)
    in_maps = []
    for c in range(8):
        m = dict(shared)
        m["xt"] = np.ascontiguousarray(x0[c].T)
        in_maps.append(m)
    res = run_bass_kernel_spmd(nc, in_maps, core_ids=list(range(8)))
    out = np.stack([np.asarray(res.results[c]["xo"]).T for c in range(8)], axis=0)
    return out.astype(np.float32)


# version 08
# speedup vs baseline: 1.1863x; 1.1725x over previous
import contextlib
import numpy as np
import concourse.bass as bass
import concourse.mybir as mybir

F32 = mybir.dt.float32
BF16 = mybir.dt.bfloat16
I32 = mybir.dt.int32
AF = mybir.ActivationFunctionType
ALU = mybir.AluOpType
AX = mybir.AxisListType

ENGINES = ["tensor", "vector", "scalar", "gpsimd", "sync"]
DMA_QUEUES = ["gpsimd", "sync", "scalar"]


DMA_SLOTS = 8
SEM_SETS = 3

class Op:
    __slots__ = ("eng", "fn", "reads", "writes", "idx", "dma", "deps",
                 "awaited", "sem_val", "vclock", "slot", "epoch")
    def __init__(self, eng, fn, reads, writes, idx, dma, epoch):
        self.eng = eng; self.fn = fn; self.reads = reads; self.writes = writes
        self.idx = idx; self.dma = dma
        self.deps = set(); self.awaited = False; self.sem_val = None
        self.vclock = None; self.slot = None; self.epoch = epoch


class MK:
    def __init__(self, nc):
        self.nc = nc
        self.stack = contextlib.ExitStack()
        self._scopes = []
        self.ops = []
        self.last_write = {}
        self.readers = {}
        self.sbuf_bytes = 0
        self._consts = {}
        self._const_tile = None
        self._const_used = 0
        self.epoch = 0
        self._last_op_eng = {}
        self._last_dma_q = {}
        self._const_tile = self.stack.enter_context(
            self.nc.sbuf_tensor("mk_consts", [128, 64], F32))

    def push_scope(self):
        self._scopes.append(contextlib.ExitStack())

    def pop_scope(self):
        self.fence()
        self._scopes.pop().close()

    def fence(self):
        deps = list(self._last_op_eng.values())
        for lst in self._last_dma_q.values():
            deps.extend(lst)
        for e in ENGINES:
            self.op(e, lambda eng: eng.nop(), reads=[], writes=[], extra_deps=deps)
        self._last_op_eng = {}
        self._last_dma_q = {}
        self.epoch += 1

    def sbuf(self, name, shape, dtype):
        st = self._scopes[-1] if self._scopes else self.stack
        h = st.enter_context(self.nc.sbuf_tensor(name, list(shape), dtype))
        bpe = 2 if dtype == BF16 else 4
        self.sbuf_bytes += int(np.prod(shape[1:])) * bpe
        return h

    def psum(self, name, shape, dtype=F32):
        st = self._scopes[-1] if self._scopes else self.stack
        return st.enter_context(self.nc.psum_tensor(name, list(shape), dtype))

    def sem(self, name):
        return self.stack.enter_context(self.nc.semaphore(name))

    def dram(self, name, shape, dtype, kind="Internal"):
        return self.nc.dram_tensor(name, list(shape), dtype, kind=kind).ap()

    def constcol(self, value, parts=128):
        key = float(value)
        if key in self._consts:
            ap128, tag = self._consts[key]
            return ap128[0:parts, 0:1], tag
        i = self._const_used; self._const_used += 1
        assert i < 64
        full = self._const_tile[:, i:i+1]
        tag = f"const.{i}"
        self.op("gpsimd", lambda e, full=full, v=float(value): e.memset(full, v),
                reads=[], writes=[tag])
        self._consts[key] = (full, tag)
        return full[0:parts, 0:1], tag

    def op(self, eng, fn, reads=(), writes=(), dma=False, extra_deps=()):
        assert eng in ENGINES
        idx = len(self.ops)
        o = Op(eng, fn, list(reads), list(writes), idx, dma, self.epoch)
        for d in extra_deps:
            o.deps.add(d)
        for t in o.reads:
            if t in self.last_write:
                o.deps.add(self.last_write[t])
        for t in o.writes:
            if t in self.last_write:
                o.deps.add(self.last_write[t])
            per_eng_last = {}
            for r in self.readers.get(t, ()):
                ro = self.ops[r]
                if ro.dma:
                    o.deps.add(r)
                else:
                    k = ro.eng
                    if k not in per_eng_last or per_eng_last[k] < r:
                        per_eng_last[k] = r
            for r in per_eng_last.values():
                o.deps.add(r)
        for t in o.reads:
            self.readers.setdefault(t, []).append(idx)
        for t in o.writes:
            self.last_write[t] = idx
            self.readers[t] = []
        self.ops.append(o)
        if dma:
            self._last_dma_q.setdefault(eng, []).append(idx)
            if len(self._last_dma_q[eng]) > DMA_SLOTS:
                self._last_dma_q[eng].pop(0)
        else:
            self._last_op_eng[eng] = idx
        return idx

    def dma(self, queue, out, in_, reads=(), writes=(), **kw):
        def fn(e, out=out, in_=in_, kw=kw):
            return e.dma_start(out=out, in_=in_, **kw)
        reads = list(reads); writes = list(writes)
        if type(in_.tensor).__name__ == "DRamTensorHandle":
            reads.append(f"hbm.{in_.tensor.name}")
        if type(out.tensor).__name__ == "DRamTensorHandle":
            writes.append(f"hbm.{out.tensor.name}")
        return self.op(queue, fn, reads, writes, dma=True)

    def barrier_read(self, tags):
        deps = list(self._last_op_eng.values())
        for lst in self._last_dma_q.values():
            deps.extend(lst)
        self.op("gpsimd", lambda e: e.nop(), reads=list(tags), writes=[], extra_deps=deps)

    def build(self, verbose=True):
        ops = self.ops
        n_eng = len(ENGINES)
        eidx = {e: i for i, e in enumerate(ENGINES)}

        def needs_edge(dep, o):
            if dep.dma:
                return True
            if dep.eng == o.eng == "tensor":
                return False
            return True

        for o in ops:
            for d in o.deps:
                dep = ops[d]
                if needs_edge(dep, o):
                    dep.awaited = True
        n_epochs = self.epoch + 1
        eng_await_ctr = {}
        q_issue_ctr = {}
        slot_ctr = {}
        for o in ops:
            si = o.epoch % SEM_SETS
            if o.dma:
                qk = (si, o.eng)
                q_issue_ctr[qk] = q_issue_ctr.get(qk, 0)
                o.slot = q_issue_ctr[qk] % DMA_SLOTS
                q_issue_ctr[qk] += 1
                sk = (si, o.eng, o.slot)
                slot_ctr[sk] = slot_ctr.get(sk, 0) + 16
                o.sem_val = slot_ctr[sk]
            elif o.awaited:
                ek = (si, o.eng)
                eng_await_ctr[ek] = eng_await_ctr.get(ek, 0) + 1
                o.sem_val = eng_await_ctr[ek]

        qidx = {q: i for i, q in enumerate(DMA_QUEUES)}
        NCH = n_eng + len(DMA_QUEUES) * DMA_SLOTS
        dchan = lambda q, sl: n_eng + qidx[q] * DMA_SLOTS + sl
        qidx = {q: i for i, q in enumerate(DMA_QUEUES)}
        NCH = n_eng + len(DMA_QUEUES) * DMA_SLOTS
        dchan = lambda q, sl: n_eng + qidx[q] * DMA_SLOTS + sl
        eng_run = {e: {} for e in ENGINES}
        waits = {}
        n_waits = 0
        for o in ops:
            run = eng_run[o.eng]
            w = []
            def know(dep):
                if dep.vclock is not None:
                    for kk, vv in dep.vclock.items():
                        if run.get(kk, 0) < vv:
                            run[kk] = vv
            for d in sorted(o.deps):
                dep = ops[d]
                if dep.dma:
                    key = (dep.epoch, dchan(dep.eng, dep.slot))
                    val = dep.sem_val
                elif not needs_edge(dep, o):
                    know(dep)
                    continue
                else:
                    key = (dep.epoch, eidx[dep.eng])
                    val = dep.sem_val
                if run.get(key, 0) < val:
                    assert dep.epoch == o.epoch, (
                        f"cross-epoch wait not pruned: op {o.idx} (ep {o.epoch}) "
                        f"dep {d} (ep {dep.epoch})")
                    w.append((key[0], key[1], val))
                    run[key] = val
                    know(dep)
            if o.dma and o.sem_val > 16:
                key = (o.epoch, dchan(o.eng, o.slot))
                if run.get(key, 0) < o.sem_val - 16:
                    w.append((o.epoch, key[1], o.sem_val - 16))
                    run[key] = o.sem_val - 16
            waits[o.idx] = w
            n_waits += len(w)
            o.vclock = dict(run)

        self.stats = dict(
            n_ops=len(ops), awaited=sum(1 for o in ops if o.awaited),
            dmas=sum(1 for o in ops if o.dma), waits=n_waits, epochs=self.epoch + 1,
            per_eng={e: sum(1 for o in ops if o.eng == e) for e in ENGINES},
            sbuf_kb=self.sbuf_bytes / 1024)
        if verbose:
            print("MK stats:", self.stats)

        esems = {(si, e): self.sem(f"sem{si}_{e}") for si in range(SEM_SETS) for e in ENGINES}
        dsem_pool = {(si, q, sl): self.sem(f"dsem{si}_{q}{sl}")
                     for si in range(SEM_SETS) for q in DMA_QUEUES for sl in range(DMA_SLOTS)}

        def chan_sem(ep, ch):
            si = ep % SEM_SETS
            if ch < n_eng:
                return esems[(si, ENGINES[ch])]
            r = ch - n_eng
            return dsem_pool[(si, DMA_QUEUES[r // DMA_SLOTS], r % DMA_SLOTS)]

        by_eng = {e: [o for o in ops if o.eng == e] for e in ENGINES}

        def emit(engine_obj, ename):
            for o in by_eng[ename]:
                for ep, ch, val in waits[o.idx]:
                    engine_obj.wait_ge(chan_sem(ep, ch), val)
                inst = o.fn(engine_obj)
                if o.dma:
                    inst.then_inc(chan_sem(o.epoch, dchan(o.eng, o.slot)), 16)
                elif o.awaited:
                    inst.then_inc(chan_sem(o.epoch, eidx[o.eng]), 1)

        block = self.stack.enter_context(self.nc.Block())
        @block.tensor
        def _(t): emit(t, "tensor")
        @block.vector
        def _(v): emit(v, "vector")
        @block.scalar
        def _(s): emit(s, "scalar")
        @block.gpsimd
        def _(g): emit(g, "gpsimd")
        @block.sync
        def _(s): emit(s, "sync")
        self.stack.close()
        return self.nc


class T2:
    def __init__(self, mk, name, rows, cols, dtype):
        assert rows % 128 == 0 or rows < 128
        self.name = name; self.rows = rows; self.cols = cols; self.dtype = dtype
        self.nch = max(1, rows // 128)
        self.parts = min(rows, 128)
        self.h = mk.sbuf(name, [self.parts, self.nch * cols], dtype)
    def ap(self, chunk=0, cs=None, ce=None, p0=0, p1=None):
        cs = 0 if cs is None else cs
        ce = self.cols if ce is None else ce
        p1 = self.parts if p1 is None else p1
        return self.h[p0:p1, chunk * self.cols + cs: chunk * self.cols + ce]
    def ap3(self, cs=0, ce=None):
        ce = self.cols if ce is None else ce
        v = self.h[:, :].rearrange("p (c t) -> p c t", c=self.nch)
        return v[:, :, cs:ce]
    def tag(self, chunk=0):
        return f"{self.name}.{chunk}"
    def tags(self):
        return [self.tag(c) for c in range(self.nch)]


def fm_hbm_view(hbm_ap, p=128):
    return hbm_ap.rearrange("(c p) t -> p c t", p=p)



def emit_ln(mk, x, xb, xn, u, rows, rep, onescol, onesrow_f, bsum, bsq, brr, brm,
            TC, scale_col=None, shift_col=None, sc_tags=()):
    NCH = x.nch
    ceps, teps = mk.constcol(1e-5, parts=1)
    mk.op("scalar", lambda e, a=xb.h[:, :], b=x.h[:, :]: e.activation(a, b, AF.Copy),
          reads=x.tags(), writes=xb.tags())
    mk.op("scalar", lambda e, a=xn.h[:, :], b=xb.h[:, :]: e.activation(a, b, AF.Square),
          reads=xb.tags(), writes=xn.tags())
    for k in range(NCH):
        mk.op("tensor", lambda e, l=onescol.ap(), r=xb.ap(k), o=bsum[0][0:1, 0:TC],
              s=(k == 0), p=(k == NCH - 1): e.matmul(o, l, r, start=s, stop=p),
              reads=[onescol.tag(), xb.tag(k)], writes=[bsum[1]])
    for k in range(NCH):
        mk.op("tensor", lambda e, l=onescol.ap(), r=xn.ap(k), o=bsq[0][0:1, 0:TC],
              s=(k == 0), p=(k == NCH - 1): e.matmul(o, l, r, start=s, stop=p),
              reads=[onescol.tag(), xn.tag(k)], writes=[bsq[1]])
    R = lambda i: rows.ap(0, i * TC, (i + 1) * TC)
    rt = lambda i: f"{rows.name}.r{i}"
    mk.op("vector", lambda e, o=R(0), i_=bsum[0][0:1, 0:TC]:
          e.tensor_scalar(o, i_, 1.0 / 1024, None, ALU.mult),
          reads=[bsum[1]], writes=[rt(0)])
    mk.op("vector", lambda e, o=R(1), i_=bsq[0][0:1, 0:TC]:
          e.tensor_scalar(o, i_, 1.0 / 1024, None, ALU.mult),
          reads=[bsq[1]], writes=[rt(1)])
    mk.op("vector", lambda e, o=R(4), a=R(0): e.tensor_mul(o, a, a),
          reads=[rt(0)], writes=[rt(4)])
    mk.op("vector", lambda e, o=R(1), a=R(1), b=R(4): e.tensor_sub(o, a, b),
          reads=[rt(1), rt(4)], writes=[rt(1)])
    mk.op("scalar", lambda e, o=R(5), i_=R(1), b=ceps: e.activation(o, i_, AF.Sqrt, bias=b),
          reads=[rt(1), teps], writes=[rt(5)])
    mk.op("vector", lambda e, o=R(2), i_=R(5): e.reciprocal(o, i_),
          reads=[rt(5)], writes=[rt(2)])
    mk.op("vector", lambda e, o=R(3), a=R(0): e.tensor_scalar(o, a, -1.0, None, ALU.mult),
          reads=[rt(0)], writes=[rt(3)])
    mk.op("tensor", lambda e, o=brr[0][:, 0:TC], l=onesrow_f.ap(), r=R(2):
          e.matmul(o, l, r, start=True, stop=True),
          reads=[onesrow_f.tag(), rt(2)], writes=[brr[1]])
    mk.op("tensor", lambda e, o=brm[0][:, 0:TC], l=onesrow_f.ap(), r=R(3):
          e.matmul(o, l, r, start=True, stop=True),
          reads=[onesrow_f.tag(), rt(3)], writes=[brm[1]])
    REP_R, REP_M = rep.ap(0, 0, TC), rep.ap(0, TC, 2 * TC)
    mk.op("vector", lambda e, o=REP_R, i_=brr[0][:, 0:TC]: e.tensor_copy(o, i_),
          reads=[brr[1]], writes=[f"{rep.name}.r"])
    mk.op("vector", lambda e, o=REP_M, i_=brm[0][:, 0:TC]: e.tensor_copy(o, i_),
          reads=[brm[1]], writes=[f"{rep.name}.m"])
    for c in range(NCH):
        mk.op("gpsimd", lambda e, o=u.ap(c), a=x.ap(c), b=REP_M: e.tensor_add(o, a, b),
              reads=[x.tag(c), f"{rep.name}.m"], writes=[u.tag(c)])
        if scale_col is None:
            mk.op("vector", lambda e, o=xn.ap(c), a=u.ap(c), b=REP_R: e.tensor_mul(o, a, b),
                  reads=[u.tag(c), f"{rep.name}.r"], writes=[xn.tag(c)])
        else:
            mk.op("vector", lambda e, o=u.ap(c), a=u.ap(c), b=REP_R: e.tensor_mul(o, a, b),
                  reads=[u.tag(c), f"{rep.name}.r"], writes=[u.tag(c)])
            mk.op("scalar", lambda e, o=xn.ap(c), i_=u.ap(c), s=scale_col[c], b=shift_col[c]:
                  e.activation(o, i_, AF.Identity, bias=b, scale=s),
                  reads=[u.tag(c), *sc_tags], writes=[xn.tag(c)])



def emit_ffn0(mk, xin_h, xin_tag, xout_h, xout_tag, wg_h, wu_h, wd_h,
              TC=256, name="f", load_weights=True, wts=None):
    DM, T, HID = 1024, 4096, 3072
    NCH, NH = DM // 128, HID // 128
    NT = T // TC

    if wts is None:
        wg = T2(mk, name + "wg", DM, HID, BF16)
        wu = T2(mk, name + "wu", DM, HID, BF16)
        wd = T2(mk, name + "wd", HID, DM, BF16)
    else:
        wg, wu, wd = wts
    x = T2(mk, name + "x", DM, TC, F32)
    xb = T2(mk, name + "xb", DM, TC, BF16)
    xn = T2(mk, name + "xn", DM, TC, BF16)
    u = T2(mk, name + "u", DM, TC, F32)
    hs = T2(mk, name + "hs", 256, TC, BF16)
    hh = T2(mk, name + "hh", HID, TC, BF16)
    rep = T2(mk, name + "rep", 128, 2 * TC, F32)
    rows = T2(mk, name + "rows", 1, 6 * TC, F32)
    onescol = T2(mk, name + "onc", 128, 1, BF16)
    onesrow_f = T2(mk, name + "onr", 1, 128, F32)

    pb = [mk.psum(f"{name}pb{i}", [128, 512], F32) for i in range(8)]

    c127, t127 = mk.constcol(1.0 / 127.0)
    chalf127, thalf = mk.constcol(0.5 / 127.0)

    mk.op("gpsimd", lambda e: e.memset(onescol.h[:, :], 1.0), writes=[onescol.tag()])
    mk.op("gpsimd", lambda e: e.memset(onesrow_f.h[:, :], 1.0), writes=[onesrow_f.tag()])

    if load_weights:
        for w, hbm in ((wg, wg_h), (wu, wu_h), (wd, wd_h)):
            mk.dma("gpsimd", w.ap3(), fm_hbm_view(hbm), reads=[], writes=w.tags())

    for it in range(NT):
        t0 = it * TC
        mk.dma("sync", x.ap3(), fm_hbm_view(xin_h)[:, :, t0:t0 + TC],
               reads=[xin_tag], writes=x.tags())
        emit_ln(mk, x, xb, xn, u, rows, rep, onescol, onesrow_f,
                (pb[4], f"{name}b4"), (pb[5], f"{name}b5"),
                (pb[6], f"{name}b6"), (pb[7], f"{name}b7"), TC)
        for m in range(NH):
            gb, ub = 4 + (m % 2) * 2, 5 + (m % 2) * 2
            pg, pu = pb[gb], pb[ub]
            gt, ut = f"{name}b{gb}", f"{name}b{ub}"
            for k in range(NCH):
                mk.op("tensor", lambda e, o=pg[:, 0:TC], l=wg.ap(k, m * 128, (m + 1) * 128),
                      r=xn.ap(k), s=(k == 0), p=(k == NCH - 1): e.matmul(o, l, r, start=s, stop=p),
                      reads=[wg.tag(k), xn.tag(k)], writes=[gt])
            for k in range(NCH):
                mk.op("tensor", lambda e, o=pu[:, 0:TC], l=wu.ap(k, m * 128, (m + 1) * 128),
                      r=xn.ap(k), s=(k == 0), p=(k == NCH - 1): e.matmul(o, l, r, start=s, stop=p),
                      reads=[wu.tag(k), xn.tag(k)], writes=[ut])
            mk.op("scalar", lambda e, o=hs.ap(m % 2), i_=pg[:, 0:TC], s=c127:
                  e.activation(o, i_, AF.Silu, scale=s),
                  reads=[gt, t127], writes=[hs.tag(m % 2)])
            mk.op("vector", lambda e, o=hh.ap(m), a=pu[:, 0:TC], b=hs.ap(m % 2), s=1.0 / 127.0:
                  e.scalar_tensor_tensor(o, a, s, b, ALU.mult, ALU.mult),
                  reads=[ut, hs.tag(m % 2)], writes=[hh.tag(m)])
        for m2 in range(NCH):
            db = m2 % 2
            pd = pb[db]
            dt = f"{name}b{db}"
            for k2 in range(NH):
                mk.op("tensor", lambda e, o=pd[:, 0:TC], l=wd.ap(k2, m2 * 128, (m2 + 1) * 128),
                      r=hh.ap(k2), s=(k2 == 0), p=(k2 == NH - 1): e.matmul(o, l, r, start=s, stop=p),
                      reads=[wd.tag(k2), hh.tag(k2)], writes=[dt])
            mk.op("vector", lambda e, o=x.ap(m2), a=pd[:, 0:TC], b=x.ap(m2), s=0.5 / 127.0:
                  e.scalar_tensor_tensor(o, a, s, b, ALU.mult, ALU.add),
                  reads=[dt, x.tag(m2)], writes=[x.tag(m2)])
        mk.dma("scalar", fm_hbm_view(xout_h)[:, :, t0:t0 + TC], x.ap3(),
               reads=x.tags(), writes=[f"{xout_tag}.{it}"])
    return [f"{xout_tag}.{it}" for it in range(NT)]


def build_ffn0_program():
    nc = bass.Bass("TRN2", target_bir_lowering=False)
    mk = MK(nc)
    xin = mk.dram("xt", [1024, 4096], F32, kind="ExternalInput")
    wg = mk.dram("wgT", [1024, 3072], I32, kind="ExternalInput")
    wu = mk.dram("wuT", [1024, 3072], I32, kind="ExternalInput")
    wd = mk.dram("wdT", [3072, 1024], I32, kind="ExternalInput")
    xout = mk.dram("xo", [1024, 4096], F32, kind="ExternalOutput")
    out_tags = emit_ffn0(mk, xin, "hbm.xt", xout, "hbm.xo", wg, wu, wd)
    mk.barrier_read(out_tags)
    mk.build()
    return nc



def emit_moe(mk, x_h, x_tag, y2_h, y2_tag, xout_h, xout_tag,
             gw_h, w1_h, w2_h, w3_h, ln2s_h, ln2b_h, idf_h, name="m"):
    DM, T, HID, NE = 1024, 4096, 2048, 8
    NCH, NHC = DM // 128, HID // 128
    QT, TC = 1024, 512
    NQ = T // QT

    acc = T2(mk, name + "acc", DM, QT, F32)
    xn2 = T2(mk, name + "xn2", DM, QT, BF16)
    hh = T2(mk, name + "hh", HID, QT, BF16)
    w1s = T2(mk, name + "w1s", DM, 512, BF16)
    w2s = T2(mk, name + "w2s", DM, 512, BF16)
    w3s = T2(mk, name + "w3s", HID, 256, BF16)
    wrep = T2(mk, name + "wrep", 128, QT, F32)
    y2c = T2(mk, name + "y2c", DM, TC, F32)
    u = T2(mk, name + "u", DM, TC, F32)
    xb = T2(mk, name + "xb", DM, TC, BF16)
    xnc = T2(mk, name + "xnc", DM, TC, BF16)
    rep = T2(mk, name + "rep", 128, 2 * TC, F32)
    rows = T2(mk, name + "rows", 1, 6 * TC, F32)
    hs = T2(mk, name + "hs", 256, TC, BF16)
    hu = T2(mk, name + "hu", 256, TC, BF16)
    gsb = T2(mk, name + "gsb", 8, QT, F32)
    gst = T2(mk, name + "gst", 128, 8 * 8, F32)
    gsc = T2(mk, name + "gsc", 128, 8, F32)
    gsc2 = T2(mk, name + "gsc2", 128, 8, F32)
    grows = T2(mk, name + "grows", 128, 8, F32)
    wtok = T2(mk, name + "wtok", 128, 8 * 8, F32)
    wrow = T2(mk, name + "wrow", 8, QT, F32)
    wrowe = T2(mk, name + "wrowe", 1, QT, F32)
    gw_s = T2(mk, name + "gw", DM, 8, BF16)
    ln2s = T2(mk, name + "ln2s", 128, 8, F32)
    ln2b = T2(mk, name + "ln2b", 128, 8, F32)
    idf = T2(mk, name + "idf", 128, 128, F32)
    onescol = T2(mk, name + "onc", 128, 1, BF16)
    onesrow_f = T2(mk, name + "onr", 1, 128, F32)

    pb = [mk.psum(f"{name}pb{i}", [128, 512], F32) for i in range(8)]
    B = lambda i: (pb[i], f"{name}b{i}")

    mk.op("gpsimd", lambda e: e.memset(onescol.h[:, :], 1.0), writes=[onescol.tag()])
    mk.op("gpsimd", lambda e: e.memset(onesrow_f.h[:, :], 1.0), writes=[onesrow_f.tag()])
    mk.dma("gpsimd", gw_s.ap3(), fm_hbm_view(gw_h), writes=gw_s.tags())
    mk.dma("sync", ln2s.h[:, :], ln2s_h, writes=[ln2s.tag()])
    mk.dma("sync", ln2b.h[:, :], ln2b_h, writes=[ln2b.tag()])
    mk.dma("sync", idf.h[:, :], idf_h, writes=[idf.tag()])
    sc_cols = [ln2s.h[:, c:c+1] for c in range(NCH)]
    sh_cols = [ln2b.h[:, c:c+1] for c in range(NCH)]
    out_tags = []

    for q in range(NQ):
        t0 = q * QT
        for half in range(QT // TC):
            h0 = half * TC
            mk.dma("sync", y2c.ap3(), fm_hbm_view(y2_h)[:, :, t0 + h0:t0 + h0 + TC],
                   reads=[y2_tag], writes=y2c.tags())
            mk.dma("scalar", u.ap3(), fm_hbm_view(x_h)[:, :, t0 + h0:t0 + h0 + TC],
                   reads=[x_tag], writes=u.tags())
            for c in range(NCH):
                mk.op("vector", lambda e, o=acc.ap(c, h0, h0 + TC), a=y2c.ap(c), b=u.ap(c):
                      e.scalar_tensor_tensor(o, a, 0.5, b, ALU.mult, ALU.add),
                      reads=[y2c.tag(c), u.tag(c)], writes=[acc.tag(c)])
            emit_ln(mk, y2c, xb, xnc, u, rows, rep, onescol, onesrow_f,
                    B(4), B(5), B(6), B(7), TC,
                    scale_col=sc_cols, shift_col=sh_cols, sc_tags=[ln2s.tag(), ln2b.tag()])
            for c in range(NCH):
                mk.op("gpsimd", lambda e, o=xn2.ap(c, h0, h0 + TC), a=xnc.ap(c):
                      e.tensor_copy(o, a),
                      reads=[xnc.tag(c)], writes=[xn2.tag(c)])

        for tc in range(QT // TC):
            for k in range(NCH):
                mk.op("tensor", lambda e, o=pb[6][0:8, 0:TC], l=gw_s.ap(k),
                      r=xn2.ap(k, tc * TC, (tc + 1) * TC),
                      s=(k == 0), p=(k == NCH - 1): e.matmul(o, l, r, start=s, stop=p),
                      reads=[gw_s.tag(k), xn2.tag(k)], writes=[f"{name}b6"])
            mk.op("vector", lambda e, o=gsb.ap(0, tc * TC, (tc + 1) * TC), i_=pb[6][0:8, 0:TC]:
                  e.tensor_copy(o, i_), reads=[f"{name}b6"], writes=[gsb.tag(0)])
        for st in range(QT // 128):
            mk.op("tensor", lambda e, o=pb[7][0:128, 0:8],
                  i_=gsb.ap(0, st * 128, (st + 1) * 128, p0=0, p1=8), idn=idf.h[0:8, 0:8]:
                  e.transpose(o, i_, idn),
                  reads=[gsb.tag(0), idf.tag()], writes=[f"{name}b7"])
            mk.op("vector", lambda e, o=gst.ap(0, st * 8, (st + 1) * 8), i_=pb[7][0:128, 0:8]:
                  e.tensor_copy(o, i_), reads=[f"{name}b7"], writes=[gst.tag(0) + f".{st}"])
        G = lambda i: grows.ap(0, i, i + 1)
        gt = lambda i: f"{name}grows.{i}"
        for st in range(QT // 128):
            sap = gst.ap(0, st * 8, (st + 1) * 8)
            stag = gst.tag(0) + f".{st}"
            wt = wtok.ap(0, st * 8, (st + 1) * 8)
            wttag = wtok.tag(0) + f".{st}"
            mk.op("vector", lambda e, o=G(0), i_=sap: e.reduce_max(o, i_, AX.X),
                  reads=[stag], writes=[gt(0)])
            mk.op("vector", lambda e, o=gsc.h[:, :], a=sap, b=G(0):
                  e.tensor_scalar(o, a, b, None, ALU.is_equal),
                  reads=[stag, gt(0)], writes=[gsc.tag(0)])
            mk.op("vector", lambda e, o=gsc2.h[:, :], a=gsc.h[:, :], b=sap:
                  e.scalar_tensor_tensor(o, a, -1e30, b, ALU.mult, ALU.add),
                  reads=[gsc.tag(0), stag], writes=[gsc2.tag(0)])
            mk.op("vector", lambda e, o=G(1), i_=gsc2.h[:, :]: e.reduce_max(o, i_, AX.X),
                  reads=[gsc2.tag(0)], writes=[gt(1)])
            mk.op("vector", lambda e, o=G(2), i_=G(0): e.tensor_scalar(o, i_, -1.0, None, ALU.mult),
                  reads=[gt(0)], writes=[gt(2)])
            mk.op("scalar", lambda e, o=gsc.h[:, :], i_=sap, b=G(2):
                  e.activation(o, i_, AF.Exp, bias=b),
                  reads=[stag, gt(2)], writes=[gsc.tag(0)])
            mk.op("vector", lambda e, o=G(3), a=G(1), b=G(0): e.tensor_sub(o, a, b),
                  reads=[gt(1), gt(0)], writes=[gt(3)])
            mk.op("scalar", lambda e, o=G(4), i_=G(3): e.activation(o, i_, AF.Exp),
                  reads=[gt(3)], writes=[gt(4)])
            mk.op("vector", lambda e, o=G(4), i_=G(4): e.tensor_scalar(o, i_, 1.0, None, ALU.add),
                  reads=[gt(4)], writes=[gt(4)])
            mk.op("vector", lambda e, o=G(5), i_=G(4): e.reciprocal(o, i_),
                  reads=[gt(4)], writes=[gt(5)])
            mk.op("vector", lambda e, o=G(5), i_=G(5): e.tensor_scalar(o, i_, 0.5, None, ALU.mult),
                  reads=[gt(5)], writes=[gt(5)])
            mk.op("vector", lambda e, o=gsc2.h[:, :], a=sap, b=G(1):
                  e.tensor_scalar(o, a, b, None, ALU.is_ge),
                  reads=[stag, gt(1)], writes=[gsc2.tag(0)])
            mk.op("vector", lambda e, o=wt, a=gsc.h[:, :], b=gsc2.h[:, :], s=G(5):
                  e.scalar_tensor_tensor(o, a, s, b, ALU.mult, ALU.mult),
                  reads=[gsc.tag(0), gsc2.tag(0), gt(5)], writes=[wttag])
        for st in range(QT // 128):
            mk.op("tensor", lambda e, o=pb[7][0:8, 0:128],
                  i_=wtok.ap(0, st * 8, (st + 1) * 8), idn=idf.h[:, :]:
                  e.transpose(o, i_, idn),
                  reads=[wtok.tag(0) + f".{st}", idf.tag()], writes=[f"{name}b7"])
            mk.op("vector", lambda e, o=wrow.ap(0, st * 128, (st + 1) * 128, p0=0, p1=8),
                  i_=pb[7][0:8, 0:128]: e.tensor_copy(o, i_),
                  reads=[f"{name}b7"], writes=[wrow.tag(0)])

        for ex in range(NE):
            mk.dma("gpsimd", wrowe.h[0:1, :], wrow.h[ex:ex+1, :],
                   reads=[wrow.tag(0)], writes=[wrowe.tag(0)])
            for tc in range(QT // TC):
                mk.op("tensor", lambda e, o=pb[6][:, 0:TC], l=onesrow_f.ap(),
                      r=wrowe.ap(0, tc * TC, (tc + 1) * TC),
                      : e.matmul(o, l, r, start=True, stop=True),
                      reads=[onesrow_f.tag(), wrowe.tag(0)], writes=[f"{name}b6"])
                mk.op("vector", lambda e, o=wrep.ap(0, tc * TC, (tc + 1) * TC), i_=pb[6][:, 0:TC]:
                      e.tensor_copy(o, i_), reads=[f"{name}b6"], writes=[wrep.tag(0)])
            for ms in range(NHC // 2):
                par = ms % 2
                mk.dma("sync", w1s.h[:, :].rearrange("p (c n) -> p c n", c=8)[:, :, par * 256:(par + 1) * 256],
                       fm_hbm_view(w1_h[ex, :, :])[:, :, ms * 256:(ms + 1) * 256],
                       writes=[f"{name}w1s.{par}"])
                mk.dma("scalar", w2s.h[:, :].rearrange("p (c n) -> p c n", c=8)[:, :, par * 256:(par + 1) * 256],
                       fm_hbm_view(w2_h[ex, :, :])[:, :, ms * 256:(ms + 1) * 256],
                       writes=[f"{name}w2s.{par}"])
                for mi in range(2):
                    m = ms * 2 + mi
                    for tc in range(QT // TC):
                        gi = (m * 2 + tc) % 2
                        pg, gtag = pb[0 + gi], f"{name}b{0 + gi}"
                        pu, utag = pb[2 + gi], f"{name}b{2 + gi}"
                        for k in range(NCH):
                            mk.op("tensor", lambda e, o=pg[:, 0:TC],
                                  l=w1s.h[:, (k * 512) + par * 256 + mi * 128: (k * 512) + par * 256 + mi * 128 + 128],
                                  r=xn2.ap(k, tc * TC, (tc + 1) * TC),
                                  s=(k == 0), p=(k == NCH - 1): e.matmul(o, l, r, start=s, stop=p),
                                  reads=[f"{name}w1s.{par}", xn2.tag(k)], writes=[gtag])
                        for k in range(NCH):
                            mk.op("tensor", lambda e, o=pu[:, 0:TC],
                                  l=w2s.h[:, (k * 512) + par * 256 + mi * 128: (k * 512) + par * 256 + mi * 128 + 128],
                                  r=xn2.ap(k, tc * TC, (tc + 1) * TC),
                                  s=(k == 0), p=(k == NCH - 1): e.matmul(o, l, r, start=s, stop=p),
                                  reads=[f"{name}w2s.{par}", xn2.tag(k)], writes=[utag])
                        hpar = (m * 2 + tc) % 2
                        mk.op("scalar", lambda e, o=hs.ap(hpar), i_=pg[:, 0:TC]:
                              e.activation(o, i_, AF.Silu),
                              reads=[gtag], writes=[hs.tag(hpar)])
                        mk.op("vector", lambda e, o=hu.ap(hpar), a=pu[:, 0:TC], b=hs.ap(hpar):
                              e.tensor_mul(o, a, b),
                              reads=[utag, hs.tag(hpar)], writes=[hu.tag(hpar)])
                        mk.op("vector", lambda e, o=hh.ap(m, tc * TC, (tc + 1) * TC),
                              a=hu.ap(hpar), b=wrep.ap(0, tc * TC, (tc + 1) * TC):
                              e.tensor_mul(o, a, b),
                              reads=[hu.tag(hpar), wrep.tag(0)], writes=[hh.tag(m)])
            for m2 in range(NCH):
                par = m2 % 2
                mk.dma("sync", w3s.h[:, :].rearrange("p (c n) -> p c n", c=16)[:, :, par * 128:(par + 1) * 128],
                       fm_hbm_view(w3_h[ex, :, :])[:, :, m2 * 128:(m2 + 1) * 128],
                       writes=[f"{name}w3s.{par}"])
                for tc in range(QT // TC):
                    db = 4 + (m2 * 2 + tc) % 2
                    pd, dtag = pb[db], f"{name}b{db}"
                    for k2 in range(NHC):
                        mk.op("tensor", lambda e, o=pd[:, 0:TC],
                              l=w3s.h[:, k2 * 256 + par * 128: k2 * 256 + par * 128 + 128],
                              r=hh.ap(k2, tc * TC, (tc + 1) * TC),
                              s=(k2 == 0), p=(k2 == NHC - 1): e.matmul(o, l, r, start=s, stop=p),
                              reads=[f"{name}w3s.{par}", hh.tag(k2)], writes=[dtag])
                    mk.op("vector", lambda e, o=acc.ap(m2, tc * TC, (tc + 1) * TC), a=pd[:, 0:TC],
                          b=acc.ap(m2, tc * TC, (tc + 1) * TC): e.tensor_add(o, a, b),
                          reads=[dtag, acc.tag(m2)], writes=[acc.tag(m2)])
        mk.dma("scalar", fm_hbm_view(xout_h)[:, :, t0:t0 + QT], acc.ap3(),
               reads=acc.tags(), writes=[f"{xout_tag}.q{q}"])
        out_tags.append(f"{xout_tag}.q{q}")
    return out_tags


def build_moe_program():
    nc = bass.Bass("TRN2", target_bir_lowering=False)
    mk = MK(nc)
    x = mk.dram("xt", [1024, 4096], F32, kind="ExternalInput")
    y2 = mk.dram("y2t", [1024, 4096], F32, kind="ExternalInput")
    gw = mk.dram("gw", [1024, 8], F32, kind="ExternalInput")
    w1 = mk.dram("w1", [8, 1024, 2048], BF16, kind="ExternalInput")
    w2 = mk.dram("w2", [8, 1024, 2048], BF16, kind="ExternalInput")
    w3 = mk.dram("w3", [8, 2048, 1024], BF16, kind="ExternalInput")
    ln2s = mk.dram("ln2s", [128, 8], F32, kind="ExternalInput")
    ln2b = mk.dram("ln2b", [128, 8], F32, kind="ExternalInput")
    idf = mk.dram("idf", [128, 128], F32, kind="ExternalInput")
    xo = mk.dram("xo", [1024, 4096], F32, kind="ExternalOutput")
    tags = emit_moe(mk, x, "hbm.xt", y2, "hbm.y2t", xo, "hbm.xo",
                    gw, w1, w2, w3, ln2s, ln2b, idf)
    mk.barrier_read(tags)
    mk.build()
    return nc



def emit_pmx_a(mk, x_h, x_tag, o_hbm, o_tag, wq_h, wk_h, wv_h, bq_h, bk_h, bv_h,
               ph_h, idf_h, name="pa"):
    DM, T, HD = 1024, 4096, 4096
    NCH, NQC = DM // 128, HD // 128
    TC = 256
    NT = T // TC

    x = T2(mk, name + "x", DM, TC, F32)
    xb = T2(mk, name + "xb", DM, TC, BF16)
    xn = T2(mk, name + "xn", DM, TC, BF16)
    u = T2(mk, name + "u", DM, TC, F32)
    rep = T2(mk, name + "rep", 128, 2 * TC, F32)
    rows = T2(mk, name + "rows", 1, 6 * TC, F32)
    qq = T2(mk, name + "qq", HD, TC, BF16)
    kk = T2(mk, name + "kk", HD, TC, BF16)
    vv = T2(mk, name + "vv", HD, TC, BF16)
    ws = [T2(mk, name + f"ws{i}", DM, 2 * 512, BF16) for i in range(3)]
    bcols = T2(mk, name + "bcols", 128, 3 * 32, F32)
    phc = T2(mk, name + "phc", 128, 32, F32)
    cph = T2(mk, name + "cph", 128, 32, F32)
    sph = T2(mk, name + "sph", 128, 32, F32)
    ct = T2(mk, name + "ct", 128, 16 * 8, F32)
    ctb = T2(mk, name + "ctb", 128, 16 * 8, BF16)
    pp = T2(mk, name + "pp", 128, 8 * TC, BF16)
    tmp8 = T2(mk, name + "tmp8", 128, 8 * TC, BF16)
    srow = T2(mk, name + "srow", 1, 16 * TC, F32)
    s16 = T2(mk, name + "s16", 16, TC, F32)
    stok = T2(mk, name + "stok", 128, 2 * 16, F32)
    atok = T2(mk, name + "atok", 128, 2 * 16, F32)
    arow = T2(mk, name + "arow", 16, TC, F32)
    arow0 = T2(mk, name + "arow0", 1, 16 * TC, F32)
    repw = T2(mk, name + "repw", 128, 2 * TC, F32)
    oo = T2(mk, name + "oo", HD, TC, BF16)
    tmp = T2(mk, name + "tmp", 128, 2 * TC, BF16)
    idf = T2(mk, name + "idf", 128, 128, F32)
    onescol = T2(mk, name + "onc", 128, 1, BF16)
    onesrow_f = T2(mk, name + "onr", 1, 128, F32)

    pb = [mk.psum(f"{name}pb{i}", [128, 512], F32) for i in range(8)]
    B = lambda i: (pb[i], f"{name}b{i}")

    mk.op("gpsimd", lambda e: e.memset(onescol.h[:, :], 1.0), writes=[onescol.tag()])
    mk.op("gpsimd", lambda e: e.memset(onesrow_f.h[:, :], 1.0), writes=[onesrow_f.tag()])
    mk.dma("sync", idf.h[:, :], idf_h, writes=[idf.tag()])
    mk.dma("sync", bcols.h[:, 0:32], bq_h, writes=[bcols.tag()])
    mk.dma("sync", bcols.h[:, 32:64], bk_h, writes=[bcols.tag()])
    mk.dma("sync", bcols.h[:, 64:96], bv_h, writes=[bcols.tag()])
    mk.dma("sync", phc.h[:, :], ph_h, writes=[phc.tag()])
    chalfpi, thalfpi = mk.constcol(np.pi / 2)
    mk.op("scalar", lambda e, o=cph.h[:, :], i_=phc.h[:, :], b=chalfpi:
          e.activation(o, i_, AF.Sin, bias=b), reads=[phc.tag(), thalfpi], writes=[cph.tag()])
    mk.op("scalar", lambda e, o=sph.h[:, :], i_=phc.h[:, :]:
          e.activation(o, i_, AF.Sin), reads=[phc.tag()], writes=[sph.tag()])
    for h in range(4):
        for g in range(4):
            hg = 4 * h + g
            mk.op("vector", lambda e, o=ct.ap(0, hg * 8, hg * 8 + 8),
                  a=cph.h[:, 8 * h:8 * h + 8], b=cph.h[:, 8 * g:8 * g + 8]:
                  e.tensor_mul(o, a, b), reads=[cph.tag()], writes=[ct.tag()])
            mk.op("vector", lambda e, o=pp.h[:, 0:8],
                  a=sph.h[:, 8 * h:8 * h + 8], b=sph.h[:, 8 * g:8 * g + 8]:
                  e.tensor_mul(o, a, b), reads=[sph.tag()], writes=[pp.tag(0)])
            mk.op("vector", lambda e, o=ct.ap(0, hg * 8, hg * 8 + 8),
                  a=ct.ap(0, hg * 8, hg * 8 + 8), b=pp.h[:, 0:8]:
                  e.tensor_add(o, a, b), reads=[ct.tag(), pp.tag(0)], writes=[ct.tag()])
    mk.op("vector", lambda e, o=ctb.h[:, :], i_=ct.h[:, :]: e.tensor_copy(o, i_),
          reads=[ct.tag()], writes=[ctb.tag()])

    WSL = 512
    for it in range(NT):
        t0 = it * TC
        mk.dma("sync", x.ap3(), fm_hbm_view(x_h)[:, :, t0:t0 + TC],
               reads=[x_tag], writes=x.tags())
        emit_ln(mk, x, xb, xn, u, rows, rep, onescol, onesrow_f,
                B(4), B(5), B(6), B(7), TC)
        for wi, (wh, out_t, bofs) in enumerate(((wq_h, qq, 0), (wk_h, kk, 32), (wv_h, vv, 64))):
            for sl in range(HD // WSL):
                par = sl % 2
                wtag = f"{name}ws{wi}.{par}"
                mk.dma("sync", ws[wi].h[:, :].rearrange("p (c n) -> p c n", c=8)[:, :, par * WSL:(par + 1) * WSL],
                       fm_hbm_view(wh)[:, :, sl * WSL:(sl + 1) * WSL], writes=[wtag])
                for mi in range(WSL // 128):
                    m = sl * (WSL // 128) + mi
                    bk = m % 2
                    pg, gtag = pb[bk], f"{name}b{bk}"
                    for k in range(NCH):
                        mk.op("tensor", lambda e, o=pg[:, 0:TC],
                              l=ws[wi].h[:, k * 1024 + par * WSL + mi * 128: k * 1024 + par * WSL + mi * 128 + 128],
                              r=xn.ap(k), s=(k == 0), p=(k == NCH - 1):
                              e.matmul(o, l, r, start=s, stop=p),
                              reads=[wtag, xn.tag(k)], writes=[gtag])
                    mk.op("scalar", lambda e, o=out_t.ap(m), i_=pg[:, 0:TC],
                          b=bcols.h[:, bofs + m:bofs + m + 1]:
                          e.activation(o, i_, AF.Identity, bias=b),
                          reads=[gtag, bcols.tag()], writes=[out_t.tag(m)])
        for h in range(4):
            for g in range(4):
                hg = 4 * h + g
                bk = 2 + hg % 2
                ps, pstag = pb[bk], f"{name}b{bk}"
                pptag = pp.tag(0)
                mk.op("vector", lambda e, o=pp.h[:, 0:8 * TC],
                      a=qq.h[:, 8 * h * TC:(8 * h + 8) * TC],
                      b=kk.h[:, 8 * g * TC:(8 * g + 8) * TC]:
                      e.tensor_mul(o, a, b),
                      reads=[qq.tag(8 * h + k) for k in range(8)] +
                            [kk.tag(8 * g + k) for k in range(8)], writes=[pptag])
                for k in range(NCH):
                    mk.op("tensor", lambda e, o=ps[0:1, 0:TC], l=ctb.ap(0, hg * 8 + k, hg * 8 + k + 1),
                          r=pp.ap(0, k * TC, (k + 1) * TC), s=(k == 0), p=(k == NCH - 1):
                          e.matmul(o, l, r, start=s, stop=p),
                          reads=[ctb.tag(), pptag], writes=[pstag])
                mk.op("vector", lambda e, o=srow.ap(0, hg * TC, (hg + 1) * TC), i_=ps[0:1, 0:TC],
                      s=1.0 / 32.0: e.tensor_scalar(o, i_, s, None, ALU.mult),
                      reads=[pstag], writes=[srow.tag(0)])
        mk.dma("gpsimd", s16.h[:, :],
               srow.h[0:1, :].rearrange("o (hg t) -> o hg t", hg=16),
               reads=[srow.tag(0)], writes=[s16.tag(0)])
        for st in range(TC // 128):
            mk.op("tensor", lambda e, o=pb[6][0:128, 0:16],
                  i_=s16.ap(0, st * 128, (st + 1) * 128), idn=idf.h[0:16, 0:16]:
                  e.transpose(o, i_, idn), reads=[s16.tag(0), idf.tag()], writes=[f"{name}b6"])
            mk.op("vector", lambda e, o=stok.ap(0, st * 16, (st + 1) * 16), i_=pb[6][0:128, 0:16]:
                  e.tensor_copy(o, i_), reads=[f"{name}b6"], writes=[stok.tag(0) + f".{st}"])
        for st in range(TC // 128):
            sap = stok.ap(0, st * 16, (st + 1) * 16)
            stag = stok.tag(0) + f".{st}"
            aap = atok.ap(0, st * 16, (st + 1) * 16)
            atag = atok.tag(0) + f".{st}"
            m4 = rows.ap(0, 5 * TC + st * 8, 5 * TC + st * 8 + 4)
            mx = repw.ap(0, 0, 4); mxt = f"{name}repw.mx"
            sm = repw.ap(0, 4, 8); smt = f"{name}repw.sm"
            mk.op("vector", lambda e, o=mx.rearrange("p (h o) -> p h o", h=4),
                  i_=sap.rearrange("p (h g) -> p h g", h=4):
                  e.reduce_max(o, i_, AX.X), reads=[stag], writes=[mxt])
            for g in range(4):
                mk.op("vector", lambda e, o=aap[:, g:16:4], a=sap[:, g:16:4], b=mx:
                      e.tensor_sub(o, a, b), reads=[stag, mxt], writes=[atag])
            mk.op("scalar", lambda e, o=aap, i_=aap: e.activation(o, i_, AF.Exp),
                  reads=[atag], writes=[atag])
            mk.op("vector", lambda e, o=sm.rearrange("p (h o) -> p h o", h=4),
                  i_=aap.rearrange("p (h g) -> p h g", h=4):
                  e.reduce_sum(o, i_, AX.X), reads=[atag], writes=[smt])
            mk.op("vector", lambda e, o=sm, i_=sm: e.reciprocal(o, i_),
                  reads=[smt], writes=[smt])
            for g in range(4):
                mk.op("vector", lambda e, o=aap[:, g:16:4], a=aap[:, g:16:4], b=sm:
                      e.tensor_mul(o, a, b), reads=[atag, smt], writes=[atag])
        for st in range(TC // 128):
            mk.op("tensor", lambda e, o=pb[7][0:16, 0:128],
                  i_=atok.ap(0, st * 16, (st + 1) * 16), idn=idf.h[:, :]:
                  e.transpose(o, i_, idn),
                  reads=[atok.tag(0) + f".{st}", idf.tag()], writes=[f"{name}b7"])
            mk.op("vector", lambda e, o=arow.ap(0, st * 128, (st + 1) * 128, p0=0, p1=16),
                  i_=pb[7][0:16, 0:128]: e.tensor_copy(o, i_),
                  reads=[f"{name}b7"], writes=[arow.tag(0)])
        mk.dma("gpsimd", arow0.h[0:1, :].rearrange("o (hg t) -> o hg t", hg=16),
               arow.h[:, :], reads=[arow.tag(0)], writes=[arow0.tag(0)])
        for h in range(4):
            for g in range(4):
                hg = 4 * h + g
                par = hg % 2
                rbk = 2 + par
                mk.op("tensor", lambda e, o=pb[rbk][:, 0:TC], l=onesrow_f.ap(),
                      r=arow0.ap(0, hg * TC, (hg + 1) * TC):
                      e.matmul(o, l, r, start=True, stop=True),
                      reads=[onesrow_f.tag(), arow0.tag(0)], writes=[f"{name}b{rbk}"])
                mk.op("vector", lambda e, o=repw.ap(0, par * TC, (par + 1) * TC), i_=pb[rbk][:, 0:TC]:
                      e.tensor_copy(o, i_), reads=[f"{name}b{rbk}"], writes=[f"{name}repw.{par}"])
                rbc = lambda: repw.ap(0, par * TC, (par + 1) * TC).rearrange(
                    "p (o w) -> p o w", o=1).broadcast_to([128, 8, TC])
                ooview = oo.h[:, 8 * h * TC:(8 * h + 8) * TC]
                vvview = vv.h[:, 8 * g * TC:(8 * g + 8) * TC]
                ootags = [oo.tag(8 * h + d) for d in range(8)]
                vvtags = [vv.tag(8 * g + d) for d in range(8)]
                if g == 0:
                    mk.op("gpsimd", lambda e, o=ooview.rearrange("p (c w) -> p c w", c=8),
                          a=vvview.rearrange("p (c w) -> p c w", c=8), b=rbc():
                          e.tensor_tensor(o, a, b, ALU.mult),
                          reads=vvtags + [f"{name}repw.{par}"], writes=ootags)
                else:
                    mk.op("gpsimd", lambda e, o=tmp8.h[:, :].rearrange("p (c w) -> p c w", c=8),
                          a=vvview.rearrange("p (c w) -> p c w", c=8), b=rbc():
                          e.tensor_tensor(o, a, b, ALU.mult),
                          reads=vvtags + [f"{name}repw.{par}"], writes=[tmp8.tag(0)])
                    mk.op("vector", lambda e, o=ooview, a=ooview, b=tmp8.h[:, :]:
                          e.tensor_add(o, a, b),
                          reads=ootags + [tmp8.tag(0)], writes=ootags)
        mk.dma("scalar",
               o_hbm.rearrange("h (c p) s -> p h c s", p=128)[:, :, :, t0:t0 + TC],
               oo.h[:, :].rearrange("p (h c t) -> p h c t", h=4, c=8),
               reads=oo.tags(), writes=[f"{o_tag}.{it}"])
    return [f"{o_tag}.{it}" for it in range(NT)]



def emit_pmx_b(mk, x_h, x_tag, o_hbm, o_tag, xout_h, xout_tag, wo_h, bo_h, name="pb"):
    DM, T = 1024, 4096
    NCH = DM // 128
    wo = T2(mk, name + "wo", 4096, DM, BF16)
    ow = T2(mk, name + "ow", DM, 2048, BF16)
    xc = T2(mk, name + "xc", DM, 512, F32)
    bo = T2(mk, name + "bo", 128, 8, F32)
    pb = [mk.psum(f"{name}pb{i}", [128, 512], F32) for i in range(8)]

    mk.dma("gpsimd", wo.ap3(), fm_hbm_view(wo_h), writes=wo.tags())
    mk.dma("sync", bo.h[:, :], bo_h, writes=[bo.tag()])

    for h in range(4):
        for qw in range(2):
            q0 = h * 1024 + qw * 512
            s0 = qw * 2048
            mk.dma("sync", ow.ap3(), o_hbm[h, :, :].rearrange("(c p) s -> p c s", p=128)[:, :, s0:s0 + 2048],
                   reads=[o_tag], writes=ow.tags())
            mk.dma("scalar", xc.ap3(), fm_hbm_view(x_h)[:, :, q0:q0 + 512],
                   reads=[x_tag], writes=xc.tags())
            for n in range(NCH):
                bk = n % 2
                ps, pstag = pb[bk], f"{name}b{bk}"
                first = True
                for r in range(4):
                    for dch in range(NCH):
                        jc = 8 * r + dch
                        mk.op("tensor", lambda e, o=ps[:, 0:512],
                              l=wo.ap(jc, n * 128, (n + 1) * 128),
                              r=ow.h[:, dch * 2048 + r: (dch + 1) * 2048: 4],
                              s=first, p=(r == 3 and dch == NCH - 1):
                              e.matmul(o, l, r, start=s, stop=p),
                              reads=[wo.tag(jc), ow.tag(dch)], writes=[pstag])
                        first = False
                mk.op("vector", lambda e, o=xc.ap(n), a=ps[:, 0:512], b=xc.ap(n),
                      s=bo.h[:, n:n + 1]:
                      e.scalar_tensor_tensor(o, a, s, b, ALU.add, ALU.add),
                      reads=[pstag, xc.tag(n), bo.tag()], writes=[xc.tag(n)])
            mk.dma("scalar", fm_hbm_view(xout_h)[:, :, q0:q0 + 512], xc.ap3(),
                   reads=xc.tags(), writes=[f"{xout_tag}.{h}.{qw}"])
    return [f"{xout_tag}.{h}.{qw}" for h in range(4) for qw in range(2)]


def build_pmx_a_program():
    nc = bass.Bass("TRN2", target_bir_lowering=False)
    mk = MK(nc)
    x = mk.dram("xt", [1024, 4096], F32, kind="ExternalInput")
    wq = mk.dram("wq", [1024, 4096], BF16, kind="ExternalInput")
    wk = mk.dram("wk", [1024, 4096], BF16, kind="ExternalInput")
    wv = mk.dram("wv", [1024, 4096], BF16, kind="ExternalInput")
    bq = mk.dram("bq", [128, 32], F32, kind="ExternalInput")
    bk = mk.dram("bk", [128, 32], F32, kind="ExternalInput")
    bv = mk.dram("bv", [128, 32], F32, kind="ExternalInput")
    ph = mk.dram("ph", [128, 32], F32, kind="ExternalInput")
    idf = mk.dram("idf", [128, 128], F32, kind="ExternalInput")
    oh = mk.dram("oh", [4, 1024, 4096], BF16, kind="ExternalOutput")
    tags = emit_pmx_a(mk, x, "hbm.xt", oh, "hbm.oh", wq, wk, wv, bq, bk, bv, ph, idf)
    mk.barrier_read(tags)
    mk.build()
    return nc


def build_pmx_b_program():
    nc = bass.Bass("TRN2", target_bir_lowering=False)
    mk = MK(nc)
    x = mk.dram("xt", [1024, 4096], F32, kind="ExternalInput")
    oh = mk.dram("oh", [4, 1024, 4096], BF16, kind="ExternalInput")
    wo = mk.dram("wo", [4096, 1024], F32, kind="ExternalInput")
    bo = mk.dram("bo", [128, 8], F32, kind="ExternalInput")
    xo = mk.dram("xo", [1024, 4096], F32, kind="ExternalOutput")
    tags = emit_pmx_b(mk, x, "hbm.xt", oh, "hbm.oh", xo, "hbm.xo", wo, bo)
    mk.barrier_read(tags)
    mk.build()
    return nc



def emit_ma1(mk, x_h, x_tag, y_h, y_tag, q_h, q_tag, k_h, k_tag, vt_h, vt_tag,
             wq_h, wk_h, wv_h, ln1s_h, ln1b_h, idb_h, name="n1"):
    DM, T = 1024, 4096
    NCH = DM // 128
    TC = 512
    NT = T // TC

    x = T2(mk, name + "x", DM, TC, F32)
    xb = T2(mk, name + "xb", DM, TC, BF16)
    yb = T2(mk, name + "yb", DM, TC, F32)
    h1 = T2(mk, name + "h1", DM, TC, BF16)
    u = T2(mk, name + "u", DM, TC, F32)
    u2 = T2(mk, name + "u2", DM, TC, F32)
    rep = T2(mk, name + "rep", 128, 2 * TC, F32)
    rep2 = T2(mk, name + "rep2", 128, 2 * TC, F32)
    rows = T2(mk, name + "rows", 1, 6 * TC, F32)
    rows2 = T2(mk, name + "rows2", 1, 6 * TC, F32)
    qq = T2(mk, name + "qq", DM, TC, BF16)
    kk = T2(mk, name + "kk", DM, TC, BF16)
    vvf = T2(mk, name + "vvf", DM, TC, BF16)
    vtk = T2(mk, name + "vtk", TC, DM, BF16)
    wq = T2(mk, name + "wq", DM, DM, BF16)
    wk = T2(mk, name + "wk", DM, DM, BF16)
    wv = T2(mk, name + "wv", DM, DM, BF16)
    ln1s = T2(mk, name + "ln1s", 128, 8, F32)
    ln1b = T2(mk, name + "ln1b", 128, 8, F32)
    idb = T2(mk, name + "idb", 128, 128, BF16)
    onescol = T2(mk, name + "onc", 128, 1, BF16)
    onesrow_f = T2(mk, name + "onr", 1, 128, F32)

    pb = [mk.psum(f"{name}pb{i}", [128, 512], F32) for i in range(6)]
    pbt = mk.psum(f"{name}pbt", [128, 512], BF16)
    B = lambda i: (pb[i], f"{name}b{i}")

    mk.op("gpsimd", lambda e: e.memset(onescol.h[:, :], 1.0), writes=[onescol.tag()])
    mk.op("gpsimd", lambda e: e.memset(onesrow_f.h[:, :], 1.0), writes=[onesrow_f.tag()])
    mk.dma("sync", idb.h[:, :], idb_h, writes=[idb.tag()])
    mk.dma("sync", ln1s.h[:, :], ln1s_h, writes=[ln1s.tag()])
    mk.dma("sync", ln1b.h[:, :], ln1b_h, writes=[ln1b.tag()])
    for w, hbm in ((wq, wq_h), (wk, wk_h), (wv, wv_h)):
        mk.dma("gpsimd", w.ap3(), fm_hbm_view(hbm), writes=w.tags())
    sc_cols = [ln1s.h[:, c:c+1] for c in range(NCH)]
    sh_cols = [ln1b.h[:, c:c+1] for c in range(NCH)]

    for it in range(NT):
        t0 = it * TC
        mk.dma("sync", x.ap3(), fm_hbm_view(x_h)[:, :, t0:t0 + TC],
               reads=[x_tag], writes=x.tags())
        emit_ln(mk, x, xb, h1, u, rows, rep, onescol, onesrow_f,
                B(2), B(3), B(4), B(5), TC)
        REP_R = rep.ap(0, 0, TC)
        for c in range(NCH):
            mk.op("gpsimd", lambda e, o=yb.ap(c), a=u.ap(c), b=REP_R: e.tensor_mul(o, a, b),
                  reads=[u.tag(c), f"{name}rep.r"], writes=[yb.tag(c)])
        mk.dma("scalar", fm_hbm_view(y_h)[:, :, t0:t0 + TC], yb.ap3(),
               reads=yb.tags(), writes=[f"{y_tag}.{it}"])
        emit_ln(mk, yb, xb, h1, u2, rows2, rep2, onescol, onesrow_f,
                B(2), B(3), B(4), B(5), TC,
                scale_col=sc_cols, shift_col=sh_cols, sc_tags=[ln1s.tag(), ln1b.tag()])
        for w, out_t in ((wq, qq), (wk, kk), (wv, vvf)):
            for m in range(NCH):
                bk = m % 2
                pg, gtag = pb[bk], f"{name}b{bk}"
                for k in range(NCH):
                    mk.op("tensor", lambda e, o=pg[:, 0:TC], l=w.ap(k, m * 128, (m + 1) * 128),
                          r=h1.ap(k), s=(k == 0), p=(k == NCH - 1):
                          e.matmul(o, l, r, start=s, stop=p),
                          reads=[w.tag(k), h1.tag(k)], writes=[gtag])
                mk.op("scalar", lambda e, o=out_t.ap(m), i_=pg[:, 0:TC]:
                      e.activation(o, i_, AF.Identity),
                      reads=[gtag], writes=[out_t.tag(m)])
        mk.dma("scalar", fm_hbm_view(q_h)[:, :, t0:t0 + TC], qq.ap3(),
               reads=qq.tags(), writes=[f"{q_tag}.{it}"])
        mk.dma("scalar", fm_hbm_view(k_h)[:, :, t0:t0 + TC], kk.ap3(),
               reads=kk.tags(), writes=[f"{k_tag}.{it}"])
        for c in range(NCH):
            for tt in range(TC // 128):
                mk.op("tensor", lambda e, o=pbt[:, 0:128],
                      i_=vvf.ap(c, tt * 128, (tt + 1) * 128), idn=idb.h[:, :]:
                      e.transpose(o, i_, idn),
                      reads=[vvf.tag(c), idb.tag()], writes=[f"{name}bt"])
                mk.op("vector", lambda e, o=vtk.ap(tt, c * 128, (c + 1) * 128), i_=pbt[:, 0:128]:
                      e.tensor_copy(o, i_), reads=[f"{name}bt"], writes=[vtk.tag(tt)])
        mk.dma("scalar", vt_h[t0:t0 + TC, :].rearrange("(tc p) c -> p tc c", p=128),
               vtk.ap3(), reads=vtk.tags(), writes=[f"{vt_tag}.{it}"])
    return ([f"{y_tag}.{it}" for it in range(NT)] + [f"{q_tag}.{it}" for it in range(NT)]
            + [f"{k_tag}.{it}" for it in range(NT)] + [f"{vt_tag}.{it}" for it in range(NT)])



def emit_ma2(mk, q_h, q_tag, k_h, k_tag, vt_h, vt_tag, ctx_h, ctx_tag, cm_h, name="n2"):
    DM, T, NH, HK = 1024, 4096, 16, 64
    RT = 512
    NR = T // RT
    NSC = T // 128

    kT = T2(mk, name + "kT", 64, T, BF16)
    qT = T2(mk, name + "qT", 64, T, BF16)
    vtk = T2(mk, name + "vtk", T, DM, BF16)
    ksq = T2(mk, name + "ksq", 64, T, BF16)
    vsall = T2(mk, name + "vsall", 128, NSC * 65, BF16)
    at = T2(mk, name + "at", 128, NSC * RT, BF16)
    nrow = T2(mk, name + "nrow", 1, 2 * T + 8, F32)
    brow = T2(mk, name + "brow", 128, 4, F32)
    rrow = T2(mk, name + "rrow", 1, RT, F32)
    rrep = T2(mk, name + "rrep", 64, RT, F32)
    ctxt = T2(mk, name + "ctxt", 64, RT, BF16)
    cmask = T2(mk, name + "cmask", 128, 4 * RT, BF16)
    onescol = T2(mk, name + "onc", 128, 1, BF16)
    onesrow_f = T2(mk, name + "onr", 1, 128, F32)

    pbig = [mk.psum(f"{name}pg{i}", [128, 1024], F32) for i in range(2)]
    pb4 = mk.psum(f"{name}pb4", [128, 512], F32)
    pb5 = mk.psum(f"{name}pb5", [128, 512], F32)
    PG = lambda i: f"{name}pg{i}"

    mk.op("gpsimd", lambda e: e.memset(onescol.h[:, :], 1.0), writes=[onescol.tag()])
    mk.op("gpsimd", lambda e: e.memset(onesrow_f.h[:, :], 1.0), writes=[onesrow_f.tag()])
    mk.dma("sync", cmask.h[:, :], cm_h, writes=[cmask.tag(0)])
    mk.dma("sync", vtk.ap3(), fm_hbm_view(vt_h), writes=vtk.tags())

    out_tags = []
    for h in range(NH):
        mk.dma("sync", kT.h[:, :], k_h[64 * h:64 * h + 64, :], writes=kT.tags())
        mk.dma("sync", qT.h[:, :], q_h[64 * h:64 * h + 64, :], writes=qT.tags())
        mk.op("vector", lambda e,
              o=vsall.h[:, :].rearrange("p (c w) -> p c w", c=NSC)[:, :, 0:64],
              i_=vtk.h[:, :].rearrange("p (c k) -> p c k", c=NSC)[:, :, 64 * h:64 * h + 64]:
              e.tensor_copy(o, i_), reads=vtk.tags(), writes=[vsall.tag(0)])
        mk.op("gpsimd", lambda e,
              o=vsall.h[:, :].rearrange("p (c w) -> p c w", c=NSC)[:, :, 64:65]:
              e.memset(o, 1.0), writes=[vsall.tag(0)])
        mk.op("scalar", lambda e, o=ksq.h[:, :], i_=kT.h[:, :]: e.activation(o, i_, AF.Square),
              reads=kT.tags(), writes=ksq.tags())
        for tc in range(T // 512):
            mk.op("tensor", lambda e, o=pbig[0][0:1, 0:512], l=onescol.ap(p1=64),
                  r=ksq.h[:, tc * 512:(tc + 1) * 512]: e.matmul(o, l, r, start=True, stop=True),
                  reads=ksq.tags() + [onescol.tag()], writes=[PG(0)])
            mk.op("vector", lambda e, o=nrow.ap(0, tc * 512, (tc + 1) * 512), i_=pbig[0][0:1, 0:512]:
                  e.tensor_copy(o, i_), reads=[PG(0)], writes=[nrow.tag(0) + ".k"])
        mk.op("scalar", lambda e, o=ksq.h[:, :], i_=qT.h[:, :]: e.activation(o, i_, AF.Square),
              reads=qT.tags(), writes=ksq.tags())
        for tc in range(T // 512):
            mk.op("tensor", lambda e, o=pbig[1][0:1, 0:512], l=onescol.ap(p1=64),
                  r=ksq.h[:, tc * 512:(tc + 1) * 512]: e.matmul(o, l, r, start=True, stop=True),
                  reads=ksq.tags() + [onescol.tag()], writes=[PG(1)])
            mk.op("vector", lambda e, o=nrow.ap(0, T + tc * 512, T + (tc + 1) * 512),
                  i_=pbig[1][0:1, 0:512]:
                  e.tensor_copy(o, i_), reads=[PG(1)], writes=[nrow.tag(0) + ".q"])
        N_ = lambda i: nrow.ap(0, 2 * T + i, 2 * T + i + 1)
        mk.op("vector", lambda e, o=N_(0), i_=nrow.ap(0, 0, T):
              e.reduce_max(o, i_, AX.X), reads=[nrow.tag(0) + ".k"], writes=[nrow.tag(0) + ".mk"])
        mk.op("vector", lambda e, o=N_(1), i_=nrow.ap(0, T, 2 * T):
              e.reduce_max(o, i_, AX.X), reads=[nrow.tag(0) + ".q"], writes=[nrow.tag(0) + ".mq"])
        mk.op("vector", lambda e, o=N_(2), a=N_(0), b=N_(1): e.tensor_mul(o, a, b),
              reads=[nrow.tag(0) + ".mk", nrow.tag(0) + ".mq"], writes=[nrow.tag(0) + ".p"])
        mk.op("scalar", lambda e, o=N_(3), i_=N_(2): e.activation(o, i_, AF.Sqrt),
              reads=[nrow.tag(0) + ".p"], writes=[nrow.tag(0) + ".sq"])
        mk.op("vector", lambda e, o=N_(2), i_=N_(3):
              e.tensor_scalar(o, i_, -1.0 / 8.0, None, ALU.mult),
              reads=[nrow.tag(0) + ".sq"], writes=[nrow.tag(0) + ".nb"])
        mk.op("tensor", lambda e, o=pbig[0][0:128, 0:1], l=onesrow_f.ap(), r=N_(2):
              e.matmul(o, l, r, start=True, stop=True),
              reads=[onesrow_f.tag(), nrow.tag(0) + ".nb"], writes=[PG(0)])
        mk.op("vector", lambda e, o=brow.h[:, 0:1], i_=pbig[0][0:128, 0:1]: e.tensor_copy(o, i_),
              reads=[PG(0)], writes=[brow.tag(0)])

        for R in range(NR):
            t0 = R * RT
            ncs = 4 * R + 4
            for Cp in range(ncs // 2):
                par = Cp % 2
                for half in range(2):
                    C = 2 * Cp + half
                    mk.op("tensor", lambda e, o=pbig[par][:, half * RT:(half + 1) * RT],
                          l=kT.h[:, C * 128:(C + 1) * 128],
                          r=qT.h[:, t0:t0 + RT]: e.matmul(o, l, r, start=True, stop=True),
                          reads=kT.tags() + qT.tags(), writes=[PG(par)])
                C0 = 2 * Cp
                mk.op("scalar", lambda e, o=at.ap(0, C0 * RT, (C0 + 2) * RT), i_=pbig[par][:, :],
                      b=brow.h[:, 0:1], s=1.0 / 8.0:
                      e.activation(o, i_, AF.Exp, bias=b, scale=s),
                      reads=[PG(par), brow.tag(0)],
                      writes=[at.tag(0) + f".{C0}", at.tag(0) + f".{C0 + 1}"])
            mk.op("vector", lambda e, o=at.ap(0, 4 * R * RT, (4 * R + 4) * RT),
                  a=at.ap(0, 4 * R * RT, (4 * R + 4) * RT), b=cmask.h[:, :]:
                  e.tensor_mul(o, a, b),
                  reads=[at.tag(0) + f".{4 * R + j}" for j in range(4)] + [cmask.tag(0)],
                  writes=[at.tag(0) + f".{4 * R + j}" for j in range(4)])
            for C in range(ncs):
                mk.op("tensor", lambda e, o=pb4[0:65, 0:RT], l=vsall.ap(0, C * 65, (C + 1) * 65),
                      r=at.ap(0, C * RT, (C + 1) * RT), s=(C == 0), p=(C == ncs - 1):
                      e.matmul(o, l, r, start=s, stop=p),
                      reads=[vsall.tag(0), at.tag(0) + f".{C}"], writes=[f"{name}b4"])
            mk.op("vector", lambda e, o=rrow.ap(0), i_=pb4[64:65, 0:RT]:
                  e.reciprocal(o, i_), reads=[f"{name}b4"], writes=[rrow.tag(0)])
            mk.op("tensor", lambda e, o=pb5[0:64, 0:RT], l=onesrow_f.ap(ce=64),
                  r=rrow.ap(0): e.matmul(o, l, r, start=True, stop=True),
                  reads=[onesrow_f.tag(), rrow.tag(0)], writes=[f"{name}b5"])
            mk.op("vector", lambda e, o=rrep.h[:, :], i_=pb5[0:64, 0:RT]: e.tensor_copy(o, i_),
                  reads=[f"{name}b5"], writes=[rrep.tag(0)])
            mk.op("vector", lambda e, o=ctxt.h[:, :], a=pb4[0:64, 0:RT], b=rrep.h[:, :]:
                  e.tensor_mul(o, a, b),
                  reads=[f"{name}b4", rrep.tag(0)], writes=[ctxt.tag(0)])
            mk.dma("scalar", ctx_h[64 * h:64 * h + 64, t0:t0 + RT], ctxt.h[:, :],
                   reads=[ctxt.tag(0)], writes=[f"{ctx_tag}.{h}.{R}"])
            out_tags.append(f"{ctx_tag}.{h}.{R}")
    return out_tags



def emit_ma3(mk, y_h, y_tag, ctx_h, ctx_tag, y2_h, y2_tag, wo_h, bo_h, name="n3"):
    DM, T = 1024, 4096
    NCH = DM // 128
    TC = 512
    wo = T2(mk, name + "wo", DM, DM, BF16)
    cx = T2(mk, name + "cx", DM, TC, BF16)
    yc = T2(mk, name + "yc", DM, TC, F32)
    bo = T2(mk, name + "bo", 128, 8, F32)
    pb = [mk.psum(f"{name}pb{i}", [128, 512], F32) for i in range(4)]
    mk.dma("gpsimd", wo.ap3(), fm_hbm_view(wo_h), writes=wo.tags())
    mk.dma("sync", bo.h[:, :], bo_h, writes=[bo.tag()])
    tags = []
    for it in range(T // TC):
        t0 = it * TC
        mk.dma("sync", cx.ap3(), fm_hbm_view(ctx_h)[:, :, t0:t0 + TC],
               reads=[ctx_tag], writes=cx.tags())
        mk.dma("scalar", yc.ap3(), fm_hbm_view(y_h)[:, :, t0:t0 + TC],
               reads=[y_tag], writes=yc.tags())
        for m in range(NCH):
            bk = m % 2
            ps, pstag = pb[bk], f"{name}b{bk}"
            for k in range(NCH):
                mk.op("tensor", lambda e, o=ps[:, 0:TC], l=wo.ap(k, m * 128, (m + 1) * 128),
                      r=cx.ap(k), s=(k == 0), p=(k == NCH - 1):
                      e.matmul(o, l, r, start=s, stop=p),
                      reads=[wo.tag(k), cx.tag(k)], writes=[pstag])
            mk.op("vector", lambda e, o=yc.ap(m), a=ps[:, 0:TC], b=yc.ap(m), s=bo.h[:, m:m+1]:
                  e.scalar_tensor_tensor(o, a, s, b, ALU.add, ALU.add),
                  reads=[pstag, yc.tag(m), bo.tag()], writes=[yc.tag(m)])
        mk.dma("scalar", fm_hbm_view(y2_h)[:, :, t0:t0 + TC], yc.ap3(),
               reads=yc.tags(), writes=[f"{y2_tag}.{it}"])
        tags.append(f"{y2_tag}.{it}")
    return tags


def build_ma1_program():
    nc = bass.Bass("TRN2", target_bir_lowering=False)
    mk = MK(nc)
    x = mk.dram("xt", [1024, 4096], F32, kind="ExternalInput")
    wq = mk.dram("wq", [1024, 1024], F32, kind="ExternalInput")
    wk = mk.dram("wk", [1024, 1024], F32, kind="ExternalInput")
    wv = mk.dram("wv", [1024, 1024], F32, kind="ExternalInput")
    ln1s = mk.dram("ln1s", [128, 8], F32, kind="ExternalInput")
    ln1b = mk.dram("ln1b", [128, 8], F32, kind="ExternalInput")
    idb = mk.dram("idb", [128, 128], BF16, kind="ExternalInput")
    y = mk.dram("y", [1024, 4096], F32, kind="ExternalOutput")
    q = mk.dram("q", [1024, 4096], BF16, kind="ExternalOutput")
    k = mk.dram("k", [1024, 4096], BF16, kind="ExternalOutput")
    vt = mk.dram("vt", [4096, 1024], BF16, kind="ExternalOutput")
    tags = emit_ma1(mk, x, "hbm.xt", y, "hbm.y", q, "hbm.q", k, "hbm.k", vt, "hbm.vt",
                    wq, wk, wv, ln1s, ln1b, idb)
    mk.barrier_read(tags)
    mk.build()
    return nc


def build_ma2_program():
    nc = bass.Bass("TRN2", target_bir_lowering=False)
    mk = MK(nc)
    q = mk.dram("q", [1024, 4096], BF16, kind="ExternalInput")
    k = mk.dram("k", [1024, 4096], BF16, kind="ExternalInput")
    vt = mk.dram("vt", [4096, 1024], BF16, kind="ExternalInput")
    cm = mk.dram("cm", [128, 2048], BF16, kind="ExternalInput")
    ctx = mk.dram("ctx", [1024, 4096], BF16, kind="ExternalOutput")
    tags = emit_ma2(mk, q, "hbm.q", k, "hbm.k", vt, "hbm.vt", ctx, "hbm.ctx", cm)
    mk.barrier_read(tags)
    mk.build()
    return nc


def build_ma3_program():
    nc = bass.Bass("TRN2", target_bir_lowering=False)
    mk = MK(nc)
    y = mk.dram("y", [1024, 4096], F32, kind="ExternalInput")
    ctx = mk.dram("ctx", [1024, 4096], BF16, kind="ExternalInput")
    wo = mk.dram("wo", [1024, 1024], F32, kind="ExternalInput")
    bo = mk.dram("bo", [128, 8], F32, kind="ExternalInput")
    y2 = mk.dram("y2", [1024, 4096], F32, kind="ExternalOutput")
    tags = emit_ma3(mk, y, "hbm.y", ctx, "hbm.ctx", y2, "hbm.y2", wo, bo)
    mk.barrier_read(tags)
    mk.build()
    return nc



def emit_rw1(mk, x_h, x_tag, outs, prm, name="r1"):
    DM, T = 1024, 4096
    NCH = DM // 128
    TC = 512
    NT = T // TC

    x = T2(mk, name + "x", DM, TC, F32)
    xb = T2(mk, name + "xb", DM, TC, BF16)
    xn = T2(mk, name + "xn", DM, TC, BF16)
    u = T2(mk, name + "u", DM, TC, F32)
    rep = T2(mk, name + "rep", 128, 2 * TC, F32)
    rows = T2(mk, name + "rows", 1, 6 * TC, F32)
    xle = T2(mk, name + "xle", DM, TC + 2, BF16)
    dx = T2(mk, name + "dx", DM, TC, F32)
    xmix = T2(mk, name + "xmix", DM, TC, BF16)
    mixt = T2(mk, name + "mixt", 256, TC, BF16)
    wdec = T2(mk, name + "wdec", 64, TC, BF16)
    qout = T2(mk, name + "qout", DM, TC, BF16)
    cpre = T2(mk, name + "cpre", 16, T, F32)
    czer = T2(mk, name + "czer", 16, 512, F32)
    negd = T2(mk, name + "negd", 128, 8 * 16, BF16)
    combc = T2(mk, name + "combc", 128, 8, F32)
    hrow = T2(mk, name + "hrow", 1, 16, F32)
    maa = T2(mk, name + "maa", 128, 48, F32)
    mw1 = T2(mk, name + "mw1", DM, 160, BF16)
    mw2 = T2(mk, name + "mw2", 128, 5 * 1024, BF16)
    dw1 = T2(mk, name + "dw1", DM, 64, BF16)
    dw2 = T2(mk, name + "dw2", 64, DM, BF16)
    dcol = T2(mk, name + "dcol", 128, 8, F32)
    wr = T2(mk, name + "wr", DM, DM, BF16)
    wk = T2(mk, name + "wk", DM, DM, BF16)
    wv = T2(mk, name + "wv", DM, DM, BF16)
    wg = T2(mk, name + "wg", DM, DM, BF16)
    onescol = T2(mk, name + "onc", 128, 1, BF16)
    onesrow_f = T2(mk, name + "onr", 1, 128, F32)

    pb = [mk.psum(f"{name}pb{i}", [128, 512], F32) for i in range(6)]
    B = lambda i: (pb[i], f"{name}b{i}")

    mk.op("gpsimd", lambda e: e.memset(onescol.h[:, :], 1.0), writes=[onescol.tag()])
    mk.op("gpsimd", lambda e: e.memset(onesrow_f.h[:, :], 1.0), writes=[onesrow_f.tag()])
    mk.op("gpsimd", lambda e: e.memset(czer.h[:, :], 0.0), writes=[czer.tag(0)])
    mk.dma("sync", maa.h[:, :], prm["maa_cols"], writes=[maa.tag()])
    mk.dma("gpsimd", mw1.ap3(), fm_hbm_view(prm["maa_w1"]), writes=mw1.tags())
    MIXSL0 = [(0, 0), (0, 32), (0, 64), (1, 0), (1, 32)]
    for f in range(5):
        mofs = MIXSL0[f][1]
        mk.dma("gpsimd", mw2.h[mofs:mofs + 32, f * 1024:(f + 1) * 1024],
               prm["maa_w2"][f, :, :], writes=[mw2.tag(0)])
    mk.dma("gpsimd", dw1.ap3(), fm_hbm_view(prm["dw1"]), writes=dw1.tags())
    mk.dma("gpsimd", dw2.h[:, :], prm["dw2"], writes=[dw2.tag(0)])
    mk.dma("sync", dcol.h[:, :], prm["decay_cols"], writes=[dcol.tag()])
    for w, key in ((wr, "wr"), (wk, "wk"), (wv, "wv"), (wg, "wg")):
        mk.dma("gpsimd", w.ap3(), fm_hbm_view(prm[key]), writes=w.tags())
    mk.dma("gpsimd", hrow.h[0:1, :], prm["hdb"].rearrange("a b -> b a"), writes=[hrow.tag(0)])
    mk.op("tensor", lambda e, o=pb[0][:, 0:16], l=onesrow_f.ap(), r=hrow.ap(0):
          e.matmul(o, l, r, start=True, stop=True),
          reads=[onesrow_f.tag(), hrow.tag(0)], writes=[f"{name}b0"])
    for c in range(NCH):
        for j in range(2):
            mk.op("vector", lambda e, o=combc.h[64 * j:64 * j + 64, c:c + 1],
                  i_=pb[0][64 * j:64 * j + 64, 2 * c + j:2 * c + j + 1]: e.tensor_copy(o, i_),
                  reads=[f"{name}b0"], writes=[combc.tag()])
    mk.op("vector", lambda e, o=combc.h[:, :], a=combc.h[:, :], b=dcol.h[:, :]:
          e.tensor_add(o, a, b), reads=[combc.tag(), dcol.tag()], writes=[combc.tag()])
    mk.op("gpsimd", lambda e: e.memset(negd.h[:, :], 0.0), writes=[negd.tag()])
    for c in range(NCH):
        for j in range(2):
            mk.op("gpsimd", lambda e, o=negd.h[64 * j:64 * j + 64, c * 16 + 2 * c + j: c * 16 + 2 * c + j + 1]:
                  e.memset(o, -1.0 / 64.0), writes=[negd.tag()])

    MAAC = lambda i, c: maa.h[:, i * 8 + c:i * 8 + c + 1]

    for it in range(NT):
        t0 = it * TC
        mk.dma("sync", x.ap3(), fm_hbm_view(x_h)[:, :, t0:t0 + TC],
               reads=[x_tag], writes=x.tags())
        emit_ln(mk, x, xb, xn, u, rows, rep, onescol, onesrow_f,
                B(2), B(3), B(4), B(5), TC)
        mk.dma("scalar", fm_hbm_view(outs["xln"])[:, :, t0:t0 + TC], xn.ap3(),
               reads=xn.tags(), writes=[f"hbm.{name}xln.{it}"])

    for it in range(NT):
        t0 = it * TC
        lo = max(0, t0 - 1)
        hi = min(T, t0 + TC + 1)
        ofs = 1 if t0 == 0 else 0
        if t0 == 0:
            mk.op("gpsimd", lambda e: e.memset(xle.ap3(cs=0, ce=1), 0.0), writes=xle.tags())
        if hi == T:
            mk.op("gpsimd", lambda e: e.memset(xle.ap3(cs=TC + 1, ce=TC + 2), 0.0), writes=xle.tags())
        mk.dma("sync", xle.ap3(cs=ofs, ce=ofs + (hi - lo)),
               fm_hbm_view(outs["xln"])[:, :, lo:hi],
               reads=[f"hbm.{name}xln.{it2}" for it2 in range(NT)], writes=xle.tags())
        for c in range(NCH):
            mk.op("vector", lambda e, o=dx.ap(c), a=xle.ap(c, 0, TC), b=xle.ap(c, 2, TC + 2):
                  e.tensor_add(o, a, b), reads=xle.tags(), writes=[dx.tag(c)])
            mk.op("vector", lambda e, o=dx.ap(c), a=dx.ap(c), b=xle.ap(c, 1, TC + 1):
                  e.scalar_tensor_tensor(o, a, 0.5, b, ALU.mult, ALU.subtract),
                  reads=[dx.tag(c)] + xle.tags(), writes=[dx.tag(c)])
        for c in range(NCH):
            mk.op("vector", lambda e, o=xmix.ap(c), a=dx.ap(c), s=MAAC(0, c), b=xle.ap(c, 1, TC + 1):
                  e.scalar_tensor_tensor(o, a, s, b, ALU.mult, ALU.add),
                  reads=[dx.tag(c), maa.tag()] + xle.tags(), writes=[xmix.tag(c)])
        for part, (m0, mw) in enumerate(((0, 96), (96, 64))):
            bk = part
            for k in range(NCH):
                mk.op("tensor", lambda e, o=pb[bk][0:mw, 0:TC], l=mw1.ap(k, m0, m0 + mw),
                      r=xmix.ap(k), s=(k == 0), p=(k == NCH - 1):
                      e.matmul(o, l, r, start=s, stop=p),
                      reads=[mw1.tag(k), xmix.tag(k)], writes=[f"{name}b{bk}"])
        mk.op("scalar", lambda e, o=mixt.ap(0, 0, TC, p0=0, p1=96), i_=pb[0][0:96, 0:TC]:
              e.activation(o, i_, AF.Tanh), reads=[f"{name}b0"], writes=[mixt.tag(0)])
        mk.op("scalar", lambda e, o=mixt.ap(1, 0, TC, p0=0, p1=64), i_=pb[1][0:64, 0:TC]:
              e.activation(o, i_, AF.Tanh), reads=[f"{name}b1"], writes=[mixt.tag(1)])
        MIXSL = [(0, 0), (0, 32), (0, 64), (1, 0), (1, 32)]
        for f, (proj_w, out_is_g, maa_i) in enumerate(
                ((None, False, 1), (wk, False, 2), (wv, False, 3), (wr, False, 4), (wg, True, 5))):
            mch, mofs = MIXSL[f]
            for m in range(NCH):
                bk = 2 + m % 2
                mk.op("tensor", lambda e, o=pb[bk][:, 0:TC],
                      l=mw2.h[mofs:mofs + 32, f * 1024 + m * 128: f * 1024 + (m + 1) * 128],
                      r=mixt.ap(mch, 0, TC, p0=mofs, p1=mofs + 32),
                      s=True, p=True: e.matmul(o, l, r, start=s, stop=p),
                      reads=[mw2.tag(0), mixt.tag(mch)], writes=[f"{name}b{bk}"])
                mk.op("vector", lambda e, o=u.ap(m), a=pb[bk][:, 0:TC], s=MAAC(maa_i, m), b=dx.ap(m):
                      e.scalar_tensor_tensor(o, a, s, b, ALU.add, ALU.mult),
                      reads=[f"{name}b{bk}", maa.tag(), dx.tag(m)], writes=[u.tag(m)])
                mk.op("gpsimd", lambda e, o=xmix.ap(m), a=u.ap(m), b=xle.ap(m, 1, TC + 1):
                      e.tensor_add(o, a, b),
                      reads=[u.tag(m)] + xle.tags(), writes=[xmix.tag(m)])
            if f == 0:
                for k in range(NCH):
                    mk.op("tensor", lambda e, o=pb[4][0:64, 0:TC], l=dw1.ap(k),
                          r=xmix.ap(k), s=(k == 0), p=(k == NCH - 1):
                          e.matmul(o, l, r, start=s, stop=p),
                          reads=[dw1.tag(k), xmix.tag(k)], writes=[f"{name}b4"])
                mk.op("scalar", lambda e, o=wdec.h[:, :], i_=pb[4][0:64, 0:TC]:
                      e.activation(o, i_, AF.Tanh), reads=[f"{name}b4"], writes=[wdec.tag(0)])
                for m in range(NCH):
                    mk.op("tensor", lambda e, o=pb[5][:, 0:TC], l=dw2.h[:, m * 128:(m + 1) * 128],
                          r=wdec.h[:, :], s=True, p=True: e.matmul(o, l, r, start=s, stop=p),
                          reads=[dw2.tag(0), wdec.tag(0)], writes=[f"{name}b5"])
                    mk.op("scalar", lambda e, o=qout.ap(m), i_=pb[5][:, 0:TC], b=combc.h[:, m:m+1]:
                          e.activation(o, i_, AF.Exp, bias=b),
                          reads=[f"{name}b5", combc.tag()], writes=[qout.tag(m)])
                for m in range(NCH):
                    mk.op("tensor", lambda e, o=pb[4][0:16, 0:TC],
                          l=negd.h[:, m * 16:(m + 1) * 16], r=qout.ap(m),
                          s=(m == 0), p=(m == NCH - 1): e.matmul(o, l, r, start=s, stop=p),
                          reads=[negd.tag(), qout.tag(m)], writes=[f"{name}b4"])
                mk.op("vector", lambda e, o=cpre.ap(0, t0, t0 + TC), i_=pb[4][0:16, 0:TC]:
                      e.tensor_copy(o, i_), reads=[f"{name}b4"], writes=[cpre.tag(0) + f".{it}"])
            else:
                for m in range(NCH):
                    bk2 = 4 + m % 2
                    for k in range(NCH):
                        mk.op("tensor", lambda e, o=pb[bk2][:, 0:TC], l=proj_w.ap(k, m * 128, (m + 1) * 128),
                              r=xmix.ap(k), s=(k == 0), p=(k == NCH - 1):
                              e.matmul(o, l, r, start=s, stop=p),
                              reads=[proj_w.tag(k), xmix.tag(k)], writes=[f"{name}b{bk2}"])
                    mk.op("scalar", lambda e, o=qout.ap(m), i_=pb[bk2][:, 0:TC],
                          fn=(AF.Silu if out_is_g else AF.Identity):
                          e.activation(o, i_, fn),
                          reads=[f"{name}b{bk2}"], writes=[qout.tag(m)])
                okey = {1: "kT", 2: "vT", 3: "rT", 4: "gT"}[f]
                mk.dma("scalar", fm_hbm_view(outs[okey])[:, :, t0:t0 + TC], qout.ap3(),
                       reads=qout.tags(), writes=[f"hbm.{name}{okey}.{it}"])
    mk.dma("gpsimd", combc.h[0:16, 0:1], prm["slow"], writes=[f"{name}slowcol"])
    mk.op("scalar", lambda e, o=combc.h[0:16, 1:2], i_=combc.h[0:16, 0:1]:
          e.activation(o, i_, AF.Sigmoid), reads=[f"{name}slowcol"], writes=[f"{name}slowsig"])
    for sc_i in range(NT):
        init = 0.0 if sc_i == 0 else cpre.h[0:16, sc_i * TC - 1: sc_i * TC]
        mk.op("vector", lambda e, o=cpre.ap(0, sc_i * TC, (sc_i + 1) * TC),
              a=cpre.ap(0, sc_i * TC, (sc_i + 1) * TC), b=czer.h[0:16, :], ini=init:
              e.tensor_tensor_scan(o, a, b, ini, ALU.add, ALU.add),
              reads=[cpre.tag(0) + f".{it}" for it in range(NT)] + [czer.tag(0)]
                    + ([cpre.tag(0)] if sc_i else []),
              writes=[cpre.tag(0)])
    mk.dma("scalar", outs["C"], cpre.h[:, :], reads=[cpre.tag(0)], writes=["hbm.rwC"])
    mk.op("vector", lambda e, o=cpre.h[:, :], a=cpre.h[:, :], s=combc.h[0:16, 1:2]:
          e.tensor_scalar(o, a, s, None, ALU.mult),
          reads=[cpre.tag(0), f"{name}slowsig"], writes=[cpre.tag(0)])
    mk.dma("scalar", outs["Cs"], cpre.h[:, :], reads=[cpre.tag(0)], writes=["hbm.rwCs"])
    return ([f"hbm.{name}{k}.{it}" for k in ("kT", "rT", "gT", "vT") for it in range(NT)]
            + ["hbm.rwC", "hbm.rwCs"])



def emit_rw2(mk, rT_h, kT_h, vT_h, gT_h, C_h, Cs_h, z_h, dmix_h, gam_h, bet_h,
             idf_h, idb_h, name="r2"):
    DM, T, NH = 1024, 4096, 16
    RT = 512
    NR = T // RT
    NSC = T // 128

    rT = T2(mk, name + "rT", 64, T, BF16)
    kT = T2(mk, name + "kT", 64, T, BF16)
    gT = T2(mk, name + "gT", 64, T, BF16)
    vfh = T2(mk, name + "vfh", 64, T, BF16)
    vtk = T2(mk, name + "vtk", T, 64, BF16)
    vsall = T2(mk, name + "vsall", 128, NSC * 128, BF16)
    cc = T2(mk, name + "cc", 16, T, F32)
    cs = T2(mk, name + "cs", 16, T, F32)
    cct = T2(mk, name + "cct", T, 16, F32)
    cst = T2(mk, name + "cst", T, 16, F32)
    c0 = T2(mk, name + "c0", 1, 512, F32)
    cs0 = T2(mk, name + "cs0", 1, 512, F32)
    crep = T2(mk, name + "crep", 128, T, F32)
    csrep = T2(mk, name + "csrep", 128, T, F32)
    bcols = T2(mk, name + "bcols", 128, 4 * 32, F32)
    arep = T2(mk, name + "arep", 128, 2 * RT, F32)
    scol = T2(mk, name + "scol", 128, 8, F32)
    tt_ = T2(mk, name + "tt", 128, 2 * 2 * RT, BF16)
    tdiag = T2(mk, name + "tdiag", 128, 4 * RT, BF16)
    dmall = T2(mk, name + "dmall", 128, 4 * RT, BF16)
    ddt = T2(mk, name + "ddt", 128, 4 * RT, F32)
    ddt2 = T2(mk, name + "ddt2", 128, 2 * RT, F32)
    ycmb = T2(mk, name + "ycmb", 128, RT, F32)
    yh = T2(mk, name + "yh", 64, T, F32)
    ybf = T2(mk, name + "ybf", 64, T, BF16)
    ysq = T2(mk, name + "ysq", 64, T, BF16)
    grow = T2(mk, name + "grow", 1, 2560, F32)
    murep = T2(mk, name + "murep", 64, 512, F32)
    rsrep = T2(mk, name + "rsrep", 64, 512, F32)
    dmix = T2(mk, name + "dmix", 128, 8, F32)
    gam = T2(mk, name + "gam", 128, 8, F32)
    bet = T2(mk, name + "bet", 128, 8, F32)
    idf = T2(mk, name + "idf", 128, 128, F32)
    idb = T2(mk, name + "idb", 128, 128, BF16)
    zt = T2(mk, name + "zt", 64, T, BF16)
    ysl = T2(mk, name + "ysl", 64, RT, F32)
    hcols = T2(mk, name + "hcols", 64, 4, F32)
    onescol = T2(mk, name + "onc", 128, 1, BF16)
    onesrow_f = T2(mk, name + "onr", 1, 128, F32)

    pbig = [mk.psum(f"{name}pg{i}", [128, 1024], F32) for i in range(2)]
    pyl = mk.psum(f"{name}pyl", [128, 512], F32)
    pyu = mk.psum(f"{name}pyu", [128, 512], F32)
    pnf = mk.psum(f"{name}pnf", [128, 512], F32)
    pns = mk.psum(f"{name}pns", [128, 512], F32)
    PG = lambda i: f"{name}pg{i}"

    mk.op("gpsimd", lambda e: e.memset(onescol.h[:, :], 1.0), writes=[onescol.tag()])
    mk.op("gpsimd", lambda e: e.memset(onesrow_f.h[:, :], 1.0), writes=[onesrow_f.tag()])
    mk.dma("sync", cc.h[:, :], C_h, writes=[cc.tag(0)])
    mk.dma("sync", cs.h[:, :], Cs_h, writes=[cs.tag(0)])
    mk.dma("sync", dmix.h[:, :], dmix_h, writes=[dmix.tag()])
    mk.op("scalar", lambda e, o=dmix.h[:, :], i_=dmix.h[:, :]: e.activation(o, i_, AF.Sigmoid),
          reads=[dmix.tag()], writes=[dmix.tag()])
    mk.dma("sync", gam.h[:, :], gam_h, writes=[gam.tag()])
    mk.dma("sync", bet.h[:, :], bet_h, writes=[bet.tag()])
    mk.dma("sync", idf.h[:, :], idf_h, writes=[idf.tag()])
    mk.dma("sync", idb.h[:, :], idb_h, writes=[idb.tag()])
    for src, dst in ((cc, cct), (cs, cst)):
        for st in range(NSC):
            mk.op("tensor", lambda e, o=pbig[0][0:128, 0:16],
                  i_=src.ap(0, st * 128, (st + 1) * 128), idn=idf.h[0:16, 0:16]:
                  e.transpose(o, i_, idn), reads=[src.tag(0), idf.tag()], writes=[PG(0)])
            mk.op("vector", lambda e, o=dst.ap(st), i_=pbig[0][0:128, 0:16]:
                  e.tensor_copy(o, i_), reads=[PG(0)], writes=[dst.tag(st)])
    cct_tags = [cct.tag(st) for st in range(NSC)]
    cst_tags = [cst.tag(st) for st in range(NSC)]

    GNEPS = 1e-5 * 64.0
    ceps, teps = mk.constcol(GNEPS, parts=1)
    V32 = lambda t2: t2.h[:, :].rearrange("p (c w) -> p c w", c=NSC)
    zt_tags = []
    for h in range(NH):
        hofs = 64 * (h % 2)
        for ci, t2src in ((0, dmix), (1, gam), (2, bet)):
            mk.op("vector", lambda e, o=hcols.h[:, ci:ci + 1],
                  i_=t2src.h[hofs:hofs + 64, h // 2:h // 2 + 1]: e.tensor_copy(o, i_),
                  reads=[t2src.tag()], writes=[hcols.tag()])
        mk.dma("sync", rT.h[:, :], rT_h[64 * h:64 * h + 64, :], writes=rT.tags())
        mk.dma("sync", kT.h[:, :], kT_h[64 * h:64 * h + 64, :], writes=kT.tags())
        mk.dma("sync", gT.h[:, :], gT_h[64 * h:64 * h + 64, :], writes=gT.tags())
        mk.dma("sync", vfh.h[:, :], vT_h[64 * h:64 * h + 64, :], writes=vfh.tags())
        for st in range(NSC // 2):
            for half in range(2):
                st2 = 2 * st + half
                mk.op("tensor", lambda e, o=pbig[0][:, half * 64:half * 64 + 64],
                      l=vfh.h[:, st2 * 128:(st2 + 1) * 128], idn=idb.h[0:64, 0:64]:
                      e.matmul(o, l, idn, start=True, stop=True),
                      reads=vfh.tags() + [idb.tag()], writes=[PG(0)])
            mk.op("vector", lambda e, o=vtk.ap(2 * st), i_=pbig[0][:, 0:64]:
                  e.tensor_copy(o, i_), reads=[PG(0)], writes=[vtk.tag(2 * st)])
            mk.op("vector", lambda e, o=vtk.ap(2 * st + 1), i_=pbig[0][:, 64:128]:
                  e.tensor_copy(o, i_), reads=[PG(0)], writes=[vtk.tag(2 * st + 1)])
        vtk_tags = [vtk.tag(i) for i in range(NSC)]
        for tc in range(T // 512):
            mk.dma("gpsimd", c0.h[0:1, :], C_h[h:h + 1, tc * 512:(tc + 1) * 512],
                   writes=[c0.tag(0)])
            mk.op("tensor", lambda e, o=pbig[0][:, 0:512], l=onesrow_f.ap(),
                  r=c0.ap(0): e.matmul(o, l, r, start=True, stop=True),
                  reads=[onesrow_f.tag(), c0.tag(0)], writes=[PG(0)])
            mk.op("vector", lambda e, o=crep.ap(0, tc * 512, (tc + 1) * 512), i_=pbig[0][:, 0:512]:
                  e.tensor_copy(o, i_), reads=[PG(0)], writes=[crep.tag(0)])
            mk.dma("gpsimd", cs0.h[0:1, :], Cs_h[h:h + 1, tc * 512:(tc + 1) * 512],
                   writes=[cs0.tag(0)])
            mk.op("tensor", lambda e, o=pbig[1][:, 0:512], l=onesrow_f.ap(),
                  r=cs0.ap(0): e.matmul(o, l, r, start=True, stop=True),
                  reads=[onesrow_f.tag(), cs0.tag(0)], writes=[PG(1)])
            mk.op("vector", lambda e, o=csrep.ap(0, tc * 512, (tc + 1) * 512), i_=pbig[1][:, 0:512]:
                  e.tensor_copy(o, i_), reads=[PG(1)], writes=[csrep.tag(0)])

        for R in range(NR):
            t0 = R * RT
            RLO, RUP, RLOS, RUPS, NLO, NUP, NLOS, NUPS = range(8)
            sct = lambda i: f"{name}scol"
            mk.op("vector", lambda e, o=scol.h[:, 0:2],
                  i_=crep.h[:, t0:t0 + RT:RT - 1]: e.tensor_copy(o, i_),
                  reads=[crep.tag(0)], writes=[f"{name}scol"])
            mk.op("vector", lambda e, o=scol.h[:, 2:4],
                  i_=csrep.h[:, t0:t0 + RT:RT - 1]: e.tensor_copy(o, i_),
                  reads=[csrep.tag(0)], writes=[f"{name}scol"])
            mk.op("vector", lambda e, o=scol.h[:, 4:8], i_=scol.h[:, 0:4]:
                  e.tensor_scalar(o, i_, -1.0, None, ALU.mult),
                  reads=[f"{name}scol"], writes=[f"{name}scol"])
            for (ofs, t2, t2tags, refidx, sc) in (
                    (0, cct, cct_tags, 0, -1.0), (32, cst, cst_tags, 2, -1.0),
                    (64, cct, cct_tags, 1, 1.0), (96, cst, cst_tags, 3, 1.0)):
                bt = f"{name}bcols.{ofs}"
                clampop = ALU.max if sc < 0 else ALU.min
                mk.op("vector", lambda e, o=bcols.h[:, ofs:ofs + 32], a=t2.h[:, h::16],
                      s=scol.h[:, refidx:refidx + 1], cl=clampop:
                      e.tensor_scalar(o, a, s, 0.0, ALU.subtract, cl),
                      reads=t2tags + [f"{name}scol"], writes=[bt])
                mk.op("scalar", lambda e, o=bcols.h[:, ofs:ofs + 32], i_=bcols.h[:, ofs:ofs + 32],
                      sc=sc: e.activation(o, i_, AF.Exp, scale=sc),
                      reads=[bt], writes=[bt])
            mk.op("scalar", lambda e, o=arep.ap(0, 0, RT, p0=0, p1=64),
                  i_=crep.ap(0, t0, t0 + RT, p0=0, p1=64), b=scol.h[0:64, 4:5]:
                  e.activation(o, i_, AF.Exp, bias=b),
                  reads=[crep.tag(0), f"{name}scol"], writes=[f"{name}arep.lo"])
            mk.op("scalar", lambda e, o=arep.ap(0, 0, RT, p0=64, p1=128),
                  i_=csrep.ap(0, t0, t0 + RT, p0=64, p1=128), b=scol.h[64:128, 6:7]:
                  e.activation(o, i_, AF.Exp, bias=b),
                  reads=[csrep.tag(0), f"{name}scol"], writes=[f"{name}arep.lo"])
            mk.op("scalar", lambda e, o=arep.ap(0, RT, 2 * RT, p0=0, p1=64),
                  i_=crep.ap(0, t0, t0 + RT, p0=0, p1=64), b=scol.h[0:64, 1:2]:
                  e.activation(o, i_, AF.Exp, bias=b, scale=-1.0),
                  reads=[crep.tag(0), f"{name}scol"], writes=[f"{name}arep.up"])
            mk.op("scalar", lambda e, o=arep.ap(0, RT, 2 * RT, p0=64, p1=128),
                  i_=csrep.ap(0, t0, t0 + RT, p0=64, p1=128), b=scol.h[64:128, 3:4]:
                  e.activation(o, i_, AF.Exp, bias=b, scale=-1.0),
                  reads=[csrep.tag(0), f"{name}scol"], writes=[f"{name}arep.up"])
            nlo_c, nup_c = 4 * R, NSC - 4 * R - 4
            if nlo_c > 0:
                mk.op("vector", lambda e, o=V32(vsall)[:, 0:nlo_c, 0:64],
                      a=V32(vtk)[:, 0:nlo_c, :],
                      b=bcols.h[:, 0:nlo_c].rearrange("p (c o) -> p c o", o=1).broadcast_to([128, nlo_c, 64]):
                      e.tensor_tensor(o, a, b, ALU.mult),
                      reads=vtk_tags + [f"{name}bcols.0"], writes=[vsall.tag(0)])
                mk.op("vector", lambda e, o=V32(vsall)[:, 0:nlo_c, 64:128],
                      a=V32(vtk)[:, 0:nlo_c, :],
                      b=bcols.h[:, 32:32 + nlo_c].rearrange("p (c o) -> p c o", o=1).broadcast_to([128, nlo_c, 64]):
                      e.tensor_tensor(o, a, b, ALU.mult),
                      reads=vtk_tags + [f"{name}bcols.32"], writes=[vsall.tag(0)])
            if nup_c > 0:
                u0 = 4 * R + 4
                mk.op("vector", lambda e, o=V32(vsall)[:, u0:NSC, 0:64],
                      a=V32(vtk)[:, u0:NSC, :],
                      b=bcols.h[:, 64 + u0:64 + NSC].rearrange("p (c o) -> p c o", o=1).broadcast_to([128, NSC - u0, 64]):
                      e.tensor_tensor(o, a, b, ALU.mult),
                      reads=vtk_tags + [f"{name}bcols.64"], writes=[vsall.tag(0)])
                mk.op("vector", lambda e, o=V32(vsall)[:, u0:NSC, 64:128],
                      a=V32(vtk)[:, u0:NSC, :],
                      b=bcols.h[:, 96 + u0:96 + NSC].rearrange("p (c o) -> p c o", o=1).broadcast_to([128, NSC - u0, 64]):
                      e.tensor_tensor(o, a, b, ALU.mult),
                      reads=vtk_tags + [f"{name}bcols.96"], writes=[vsall.tag(0)])
            nlo = nup = nnear = 0
            for Cp in range(NSC // 2):
                par = Cp % 2
                for half in range(2):
                    C = 2 * Cp + half
                    mk.op("tensor", lambda e, o=pbig[par][:, half * RT:(half + 1) * RT],
                          l=kT.h[:, C * 128:(C + 1) * 128],
                          r=rT.h[:, t0:t0 + RT]: e.matmul(o, l, r, start=True, stop=True),
                          reads=kT.tags() + rT.tags(), writes=[PG(par)])
                C0 = 2 * Cp
                kinds = ["lo" if C < 4 * R else ("near" if C < 4 * R + 4 else "up")
                         for C in (C0, C0 + 1)]
                if "near" in kinds:
                    j0 = C0 - 4 * R
                    mk.op("scalar", lambda e, o=tdiag.ap(0, j0 * RT, (j0 + 2) * RT),
                          i_=pbig[par][:, :]:
                          e.activation(o, i_, AF.Identity, scale=1.0 / 8.0),
                          reads=[PG(par)], writes=[f"{name}tdiag"])
                else:
                    eng = "scalar" if par == 0 else "vector"
                    if eng == "scalar":
                        mk.op("scalar", lambda e, o=tt_.ap(0, par * 2 * RT, (par + 1) * 2 * RT),
                              i_=pbig[par][:, :]:
                              e.activation(o, i_, AF.Identity, scale=1.0 / 8.0),
                              reads=[PG(par)], writes=[f"{name}tt.{par}"])
                    else:
                        mk.op("vector", lambda e, o=tt_.ap(0, par * 2 * RT, (par + 1) * 2 * RT),
                              i_=pbig[par][:, :]:
                              e.tensor_scalar(o, i_, 1.0 / 8.0, None, ALU.mult),
                              reads=[PG(par)], writes=[f"{name}tt.{par}"])
                    for half in range(2):
                        C = 2 * Cp + half
                        kind = kinds[half]
                        ttile = tt_.ap(0, par * 2 * RT + half * RT, par * 2 * RT + (half + 1) * RT)
                        if kind == "lo":
                            cnt, tot = nlo, nlo_c
                            bankap, banktag = pyl, f"{name}pyl"
                            nlo += 1
                        else:
                            cnt, tot = nup, nup_c
                            bankap, banktag = pyu, f"{name}pyu"
                            nup += 1
                        mk.op("tensor", lambda e, o=bankap[:, 0:RT],
                              l=vsall.ap(0, C * 128, (C + 1) * 128), r=ttile,
                              s=(cnt == 0), p=(cnt == tot - 1): e.matmul(o, l, r, start=s, stop=p),
                              reads=[vsall.tag(0), f"{name}tt.{par}"], writes=[banktag])
            CTH = lambda t2: t2.h[:, h::16]
            for (mu, reptile, rtag, t2, t2tags, bankap, banktag) in (
                    ("f", crep, crep.tag(0), cct, cct_tags, pnf, f"{name}pnf"),
                    ("s", csrep, csrep.tag(0), cst, cst_tags, pns, f"{name}pns")):
                mk.op("vector", lambda e, o=ddt.h[:, :],
                      a=reptile.ap(0, t0, t0 + RT).rearrange("p (o w) -> p o w", o=1).broadcast_to([128, 4, RT]),
                      b=CTH(t2)[:, 4 * R:4 * R + 4].rearrange("p (c o) -> p c o", o=1).broadcast_to([128, 4, RT]):
                      e.tensor_tensor(o, a, b, ALU.subtract),
                      reads=[rtag] + t2tags, writes=[f"{name}ddt"])
                for hf in range(2):
                    hsl = slice(hf * 2 * RT, (hf + 1) * 2 * RT)
                    mk.op("gpsimd", lambda e, o=ddt2.h[:, :], a=ddt.h[:, hsl]:
                          e.tensor_scalar(o, a, -1.0, None, ALU.mult),
                          reads=[f"{name}ddt"], writes=[f"{name}ddt2"])
                    mk.op("vector", lambda e, o=ddt.h[:, hsl], a=ddt.h[:, hsl], b=ddt2.h[:, :]:
                          e.tensor_max(o, a, b),
                          reads=[f"{name}ddt", f"{name}ddt2"], writes=[f"{name}ddt"])
                mk.op("scalar", lambda e, o=dmall.h[:, :], i_=ddt.h[:, :]:
                      e.activation(o, i_, AF.Exp, scale=-1.0),
                      reads=[f"{name}ddt"], writes=[f"{name}dmall"])
                mk.op("vector", lambda e, o=dmall.h[:, :], a=dmall.h[:, :], b=tdiag.h[:, :]:
                      e.tensor_mul(o, a, b),
                      reads=[f"{name}dmall", f"{name}tdiag"], writes=[f"{name}dmall"])
                for j in range(4):
                    C = 4 * R + j
                    mk.op("tensor", lambda e, o=bankap[0:64, 0:RT], l=vtk.ap(C),
                          r=dmall.ap(0, j * RT, (j + 1) * RT),
                          s=(j == 0), p=(j == 3): e.matmul(o, l, r, start=s, stop=p),
                          reads=[vtk.tag(C), f"{name}dmall"], writes=[banktag])
            ct = f"{name}ycmb"
            if nlo_c > 0:
                mk.op("vector", lambda e, o=ycmb.h[:, :], a=pyl[:, 0:RT], b=arep.ap(0, 0, RT):
                      e.tensor_mul(o, a, b),
                      reads=[f"{name}pyl", f"{name}arep.lo"], writes=[ct])
                if nup_c > 0:
                    mk.op("vector", lambda e, o=ddt.h[:, 0:RT], a=pyu[:, 0:RT], b=arep.ap(0, RT, 2 * RT):
                          e.tensor_mul(o, a, b),
                          reads=[f"{name}pyu", f"{name}arep.up"], writes=[f"{name}ddt"])
                    mk.op("vector", lambda e, o=ycmb.h[:, :], a=ycmb.h[:, :], b=ddt.h[:, 0:RT]:
                          e.tensor_add(o, a, b), reads=[ct, f"{name}ddt"], writes=[ct])
            else:
                mk.op("vector", lambda e, o=ycmb.h[:, :], a=pyu[:, 0:RT], b=arep.ap(0, RT, 2 * RT):
                      e.tensor_mul(o, a, b),
                      reads=[f"{name}pyu", f"{name}arep.up"], writes=[ct])
            mk.op("vector", lambda e, o=ycmb.h[0:64, :], a=ycmb.h[0:64, :], b=pnf[0:64, 0:RT]:
                  e.tensor_add(o, a, b), reads=[ct, f"{name}pnf"], writes=[ct])
            mk.op("vector", lambda e, o=ysl.h[:, :], a=pns[0:64, 0:RT], b=ycmb.h[64:128, :]:
                  e.tensor_add(o, a, b), reads=[ct, f"{name}pns"], writes=[ysl.tag(0)])
            mk.op("vector", lambda e, o=ddt.h[0:64, 0:RT], a=ycmb.h[0:64, :], b=ysl.h[:, :]:
                  e.tensor_sub(o, a, b), reads=[ct, ysl.tag(0)], writes=[f"{name}ddt"])
            mk.op("vector", lambda e, o=yh.ap(0, t0, t0 + RT), a=ddt.h[0:64, 0:RT],
                  s=hcols.h[:, 0:1], b=ysl.h[:, :]:
                  e.scalar_tensor_tensor(o, a, s, b, ALU.mult, ALU.add),
                  reads=[f"{name}ddt", ysl.tag(0), hcols.tag()], writes=[yh.tag(0)])
        mk.op("scalar", lambda e, o=ybf.h[:, :], i_=yh.h[:, :]: e.activation(o, i_, AF.Identity),
              reads=[yh.tag(0)], writes=[ybf.tag(0)])
        mk.op("scalar", lambda e, o=ysq.h[:, :], i_=ybf.h[:, :]: e.activation(o, i_, AF.Square),
              reads=[ybf.tag(0)], writes=[ysq.tag(0)])
        gcol = lambda t2: hcols.h[:, 1:2] if t2 is gam else hcols.h[:, 2:3]
        for tc in range(T // 512):
            sl = slice(tc * 512, (tc + 1) * 512)
            MU = grow.ap(0, 0, 512); SQ = grow.ap(0, 512, 1024)
            V_ = grow.ap(0, 1024, 1536); RS = grow.ap(0, 1536, 2048)
            MN = grow.ap(0, 2048, 2560)
            mk.op("tensor", lambda e, o=pbig[0][0:1, 0:512], l=onescol.ap(p1=64),
                  r=ybf.h[:, sl]: e.matmul(o, l, r, start=True, stop=True),
                  reads=[ybf.tag(0), onescol.tag()], writes=[PG(0)])
            mk.op("vector", lambda e, o=MU, i_=pbig[0][0:1, 0:512]:
                  e.tensor_scalar(o, i_, 1.0 / 64.0, None, ALU.mult),
                  reads=[PG(0)], writes=[f"{name}grow.mu"])
            mk.op("tensor", lambda e, o=pbig[1][0:1, 0:512], l=onescol.ap(p1=64),
                  r=ysq.h[:, sl]: e.matmul(o, l, r, start=True, stop=True),
                  reads=[ysq.tag(0), onescol.tag()], writes=[PG(1)])
            mk.op("vector", lambda e, o=SQ, i_=pbig[1][0:1, 0:512]:
                  e.tensor_scalar(o, i_, 1.0 / 64.0, None, ALU.mult),
                  reads=[PG(1)], writes=[f"{name}grow.sq"])
            mk.op("vector", lambda e, o=V_, a=MU, b=MU: e.tensor_mul(o, a, b),
                  reads=[f"{name}grow.mu"], writes=[f"{name}grow.v"])
            mk.op("vector", lambda e, o=V_, a=SQ, b=V_: e.tensor_sub(o, a, b),
                  reads=[f"{name}grow.sq", f"{name}grow.v"], writes=[f"{name}grow.v"])
            mk.op("scalar", lambda e, o=RS, i_=V_, b=ceps: e.activation(o, i_, AF.Sqrt, bias=b),
                  reads=[f"{name}grow.v", teps], writes=[f"{name}grow.rs"])
            mk.op("vector", lambda e, o=RS, i_=RS: e.reciprocal(o, i_),
                  reads=[f"{name}grow.rs"], writes=[f"{name}grow.rs"])
            mk.op("vector", lambda e, o=MN, a=MU: e.tensor_scalar(o, a, -1.0, None, ALU.mult),
                  reads=[f"{name}grow.mu"], writes=[f"{name}grow.mn"])
            mk.op("tensor", lambda e, o=pbig[0][0:64, 0:512], l=onesrow_f.ap(ce=64), r=RS:
                  e.matmul(o, l, r, start=True, stop=True),
                  reads=[onesrow_f.tag(), f"{name}grow.rs"], writes=[PG(0)])
            mk.op("vector", lambda e, o=rsrep.h[:, :], i_=pbig[0][0:64, 0:512]:
                  e.tensor_copy(o, i_), reads=[PG(0)], writes=[rsrep.tag(0)])
            mk.op("tensor", lambda e, o=pbig[1][0:64, 0:512], l=onesrow_f.ap(ce=64), r=MN:
                  e.matmul(o, l, r, start=True, stop=True),
                  reads=[onesrow_f.tag(), f"{name}grow.mn"], writes=[PG(1)])
            mk.op("vector", lambda e, o=murep.h[:, :], i_=pbig[1][0:64, 0:512]:
                  e.tensor_copy(o, i_), reads=[PG(1)], writes=[murep.tag(0)])
            mk.op("vector", lambda e, o=yh.h[:, sl], a=yh.h[:, sl], b=murep.h[:, :]:
                  e.tensor_add(o, a, b), reads=[yh.tag(0), murep.tag(0)], writes=[yh.tag(0)])
            mk.op("gpsimd", lambda e, o=yh.h[:, sl], a=yh.h[:, sl], b=rsrep.h[:, :]:
                  e.tensor_mul(o, a, b), reads=[yh.tag(0), rsrep.tag(0)], writes=[yh.tag(0)])
            mk.op("scalar", lambda e, o=ysq.h[:, sl], i_=yh.h[:, sl], s=gcol(gam), b=gcol(bet):
                  e.activation(o, i_, AF.Identity, scale=s, bias=b),
                  reads=[yh.tag(0), hcols.tag()], writes=[ysq.tag(0)])
            mk.op("vector", lambda e, o=zt.h[:, sl], a=ysq.h[:, sl], b=gT.h[:, sl]:
                  e.tensor_mul(o, a, b), reads=[ysq.tag(0)] + gT.tags(), writes=[zt.tag(0)])
        mk.dma("scalar", z_h[64 * h:64 * h + 64, :], zt.h[:, :],
               reads=[zt.tag(0)], writes=[f"hbm.{name}z.{h}"])
        zt_tags.append(f"hbm.{name}z.{h}")
    return zt_tags


def emit_wo_residual(mk, x_h, x_tag, z_h, z_tag, xout_h, xout_tag, wo_h,
                     scale=None, name="r3"):
    DM, T = 1024, 4096
    NCH = DM // 128
    TC = 512
    wo = T2(mk, name + "wo", DM, DM, BF16)
    cx = T2(mk, name + "cx", DM, TC, BF16)
    yc = T2(mk, name + "yc", DM, TC, F32)
    pb = [mk.psum(f"{name}pb{i}", [128, 512], F32) for i in range(4)]
    mk.dma("gpsimd", wo.ap3(), fm_hbm_view(wo_h), writes=wo.tags())
    tags = []
    for it in range(T // TC):
        t0 = it * TC
        mk.dma("sync", cx.ap3(), fm_hbm_view(z_h)[:, :, t0:t0 + TC],
               reads=[z_tag], writes=cx.tags())
        mk.dma("scalar", yc.ap3(), fm_hbm_view(x_h)[:, :, t0:t0 + TC],
               reads=[x_tag], writes=yc.tags())
        for m in range(NCH):
            bk = m % 2
            ps, pstag = pb[bk], f"{name}b{bk}"
            for k in range(NCH):
                mk.op("tensor", lambda e, o=ps[:, 0:TC], l=wo.ap(k, m * 128, (m + 1) * 128),
                      r=cx.ap(k), s=(k == 0), p=(k == NCH - 1):
                      e.matmul(o, l, r, start=s, stop=p),
                      reads=[wo.tag(k), cx.tag(k)], writes=[pstag])
            mk.op("vector", lambda e, o=yc.ap(m), a=ps[:, 0:TC], b=yc.ap(m):
                  e.tensor_add(o, a, b),
                  reads=[pstag, yc.tag(m)], writes=[yc.tag(m)])
        mk.dma("scalar", fm_hbm_view(xout_h)[:, :, t0:t0 + TC], yc.ap3(),
               reads=yc.tags(), writes=[f"{xout_tag}.{it}"])
        tags.append(f"{xout_tag}.{it}")
    return tags


def build_rw1_program():
    nc = bass.Bass("TRN2", target_bir_lowering=False)
    mk = MK(nc)
    x = mk.dram("xt", [1024, 4096], F32, kind="ExternalInput")
    prm = {
        "maa_cols": mk.dram("maa_cols", [128, 48], F32, kind="ExternalInput"),
        "maa_w1": mk.dram("maa_w1", [1024, 160], F32, kind="ExternalInput"),
        "maa_w2": mk.dram("maa_w2", [5, 32, 1024], F32, kind="ExternalInput"),
        "dw1": mk.dram("dw1", [1024, 64], F32, kind="ExternalInput"),
        "dw2": mk.dram("dw2", [64, 1024], F32, kind="ExternalInput"),
        "decay_cols": mk.dram("decay_cols", [128, 8], F32, kind="ExternalInput"),
        "hdb": mk.dram("hdb", [16, 1], F32, kind="ExternalInput"),
        "slow": mk.dram("slow", [16, 1], F32, kind="ExternalInput"),
        "wr": mk.dram("wr", [1024, 1024], F32, kind="ExternalInput"),
        "wk": mk.dram("wk", [1024, 1024], F32, kind="ExternalInput"),
        "wv": mk.dram("wv", [1024, 1024], F32, kind="ExternalInput"),
        "wg": mk.dram("wg", [1024, 1024], F32, kind="ExternalInput"),
    }
    outs = {
        "xln": mk.dram("xln", [1024, 4096], BF16, kind="ExternalOutput"),
        "rT": mk.dram("rT", [1024, 4096], BF16, kind="ExternalOutput"),
        "kT": mk.dram("kT", [1024, 4096], BF16, kind="ExternalOutput"),
        "gT": mk.dram("gT", [1024, 4096], BF16, kind="ExternalOutput"),
        "vT": mk.dram("vT", [1024, 4096], BF16, kind="ExternalOutput"),
        "C": mk.dram("C", [16, 4096], F32, kind="ExternalOutput"),
        "Cs": mk.dram("Cs", [16, 4096], F32, kind="ExternalOutput"),
    }
    tags = emit_rw1(mk, x, "hbm.xt", outs, prm)
    mk.barrier_read(tags)
    mk.build()
    return nc


def build_rw2_program():
    nc = bass.Bass("TRN2", target_bir_lowering=False)
    mk = MK(nc)
    rT = mk.dram("rT", [1024, 4096], BF16, kind="ExternalInput")
    kT = mk.dram("kT", [1024, 4096], BF16, kind="ExternalInput")
    vt = mk.dram("vT", [1024, 4096], BF16, kind="ExternalInput")
    gT = mk.dram("gT", [1024, 4096], BF16, kind="ExternalInput")
    C = mk.dram("C", [16, 4096], F32, kind="ExternalInput")
    Cs = mk.dram("Cs", [16, 4096], F32, kind="ExternalInput")
    dmix = mk.dram("dmix", [128, 8], F32, kind="ExternalInput")
    gam = mk.dram("gam", [128, 8], F32, kind="ExternalInput")
    bet = mk.dram("bet", [128, 8], F32, kind="ExternalInput")
    idf = mk.dram("idf", [128, 128], F32, kind="ExternalInput")
    idb = mk.dram("idb", [128, 128], BF16, kind="ExternalInput")
    z = mk.dram("z", [1024, 4096], BF16, kind="ExternalOutput")
    tags = emit_rw2(mk, rT, kT, vt, gT, C, Cs, z, dmix, gam, bet, idf, idb)
    mk.barrier_read(tags)
    mk.build()
    return nc


def build_rw3_program():
    nc = bass.Bass("TRN2", target_bir_lowering=False)
    mk = MK(nc)
    x = mk.dram("xt", [1024, 4096], F32, kind="ExternalInput")
    z = mk.dram("z", [1024, 4096], BF16, kind="ExternalInput")
    wo = mk.dram("wo", [1024, 1024], F32, kind="ExternalInput")
    xo = mk.dram("xo", [1024, 4096], F32, kind="ExternalOutput")
    tags = emit_wo_residual(mk, x, "hbm.xt", z, "hbm.z", xo, "hbm.xo", wo)
    mk.barrier_read(tags)
    mk.build()
    return nc



def build_mega_program():
    nc = bass.Bass("TRN2", target_bir_lowering=False)
    mk = MK(nc)
    D = lambda n, sh, dt, kind="ExternalInput": mk.dram(n, sh, dt, kind=kind)
    F = lambda n, sh: D(n, sh, F32)

    xt = F("xt", [1024, 4096])
    xo = D("xo", [1024, 4096], F32, kind="ExternalOutput")
    idf = F("idf", [128, 128])
    idb = D("idb", [128, 128], BF16)
    f_wg = D("f_wg", [1024, 3072], I32); f_wu = D("f_wu", [1024, 3072], I32)
    f_wd = D("f_wd", [3072, 1024], I32)
    prm = {"maa_cols": F("r_maa", [128, 48]), "maa_w1": F("r_w1", [1024, 160]),
           "maa_w2": F("r_w2", [5, 32, 1024]), "dw1": F("r_dw1", [1024, 64]),
           "dw2": F("r_dw2", [64, 1024]), "decay_cols": F("r_dec", [128, 8]),
           "hdb": F("r_hdb", [16, 1]), "slow": F("r_slow", [16, 1]),
           "wr": F("r_wr", [1024, 1024]), "wk": F("r_wk", [1024, 1024]),
           "wv": F("r_wv", [1024, 1024]), "wg": F("r_wgx", [1024, 1024])}
    r_wo = F("r_wo", [1024, 1024])
    r_dmix = F("r_dmix", [128, 8]); r_gam = F("r_gam", [128, 8]); r_bet = F("r_bet", [128, 8])
    n_wq = F("n_wq", [1024, 1024]); n_wk = F("n_wk", [1024, 1024])
    n_wv = F("n_wv", [1024, 1024]); n_wo = F("n_wo", [1024, 1024])
    n_ln1s = F("n_ln1s", [128, 8]); n_ln1b = F("n_ln1b", [128, 8])
    n_bo = F("n_bo", [128, 8]); n_gw = F("n_gw", [1024, 8])
    n_w1 = F("n_w1", [8, 1024, 2048]); n_w2 = F("n_w2", [8, 1024, 2048])
    n_w3 = F("n_w3", [8, 2048, 1024])
    n_ln2s = F("n_ln2s", [128, 8]); n_ln2b = F("n_ln2b", [128, 8])
    n_cm = D("n_cm", [128, 2048], BF16)
    p_wq = F("p_wq", [1024, 4096]); p_wk = F("p_wk", [1024, 4096])
    p_wv = F("p_wv", [1024, 4096]); p_wo = F("p_wo", [4096, 1024])
    p_bq = F("p_bq", [128, 32]); p_bk = F("p_bk", [128, 32]); p_bv = F("p_bv", [128, 32])
    p_bo = F("p_bo", [128, 8]); p_ph = F("p_ph", [128, 32])

    I = lambda n, sh, dt=F32: mk.dram(n, sh, dt, kind="Internal")
    x1, x2, x3, x4, x5 = (I(f"x{i}", [1024, 4096]) for i in range(1, 6))
    rw_outs = {"xln": I("i_xln", [1024, 4096], BF16), "rT": I("i_rT", [1024, 4096], BF16),
               "kT": I("i_kT", [1024, 4096], BF16), "gT": I("i_gT", [1024, 4096], BF16),
               "vT": I("i_vT", [1024, 4096], BF16), "C": I("i_C", [16, 4096], F32),
               "Cs": I("i_Cs", [16, 4096], F32)}
    i_z = I("i_z", [1024, 4096], BF16)
    mini_bufs = {}
    for tag in ("a", "b"):
        mini_bufs[tag] = {"y": I(f"i_y{tag}", [1024, 4096], F32),
                          "q": I(f"i_q{tag}", [1024, 4096], BF16),
                          "k": I(f"i_k{tag}", [1024, 4096], BF16),
                          "vt": I(f"i_vt{tag}", [4096, 1024], BF16),
                          "ctx": I(f"i_ctx{tag}", [1024, 4096], BF16),
                          "y2": I(f"i_y2{tag}", [1024, 4096], F32)}
    i_oh = I("i_oh", [4, 1024, 4096], BF16)

    n_w1b = I("n_w1b", [8, 1024, 2048], BF16)
    n_w2b = I("n_w2b", [8, 1024, 2048], BF16)
    n_w3b = I("n_w3b", [8, 2048, 1024], BF16)
    p_wqb = I("p_wqb", [1024, 4096], BF16)
    p_wkb = I("p_wkb", [1024, 4096], BF16)
    p_wvb = I("p_wvb", [1024, 4096], BF16)

    def scope(fn):
        mk.push_scope()
        fn()
        mk.pop_scope()

    flat = lambda ap: ap.rearrange("e d h -> (e d) h")
    scope(lambda: emit_prep_bf16(mk, [(flat(n_w1), flat(n_w1b)), (flat(n_w2), flat(n_w2b)),
                                      (flat(n_w3), flat(n_w3b)), (p_wq, p_wqb),
                                      (p_wk, p_wkb), (p_wv, p_wvb)]))
    scope(lambda: emit_ffn0(mk, xt, "hbm.xt", x1, "hbm.x1", f_wg, f_wu, f_wd, name="f1"))
    scope(lambda: emit_rw1(mk, x1, "hbm.x1", rw_outs, prm, name="r1"))
    scope(lambda: emit_rw2(mk, rw_outs["rT"], rw_outs["kT"], rw_outs["vT"], rw_outs["gT"],
                           rw_outs["C"], rw_outs["Cs"], i_z, r_dmix, r_gam, r_bet,
                           idf, idb, name="r2"))
    scope(lambda: emit_wo_residual(mk, x1, "hbm.x1", i_z, "hbm.i_z", x2, "hbm.x2",
                                   r_wo, name="r3"))
    scope(lambda: emit_ffn0(mk, x2, "hbm.x2", x3, "hbm.x3", f_wg, f_wu, f_wd, name="f3"))

    def mini(tag, xin, xin_t, xout, xout_t):
        b = mini_bufs[tag]
        scope(lambda: emit_ma1(mk, xin, xin_t, b["y"], f"hbm.i_y{tag}", b["q"], f"hbm.i_q{tag}",
                               b["k"], f"hbm.i_k{tag}", b["vt"], f"hbm.i_vt{tag}",
                               n_wq, n_wk, n_wv, n_ln1s, n_ln1b, idb, name=f"n1{tag}"))
        scope(lambda: emit_ma2(mk, b["q"], f"hbm.i_q{tag}", b["k"], f"hbm.i_k{tag}",
                               b["vt"], f"hbm.i_vt{tag}", b["ctx"], f"hbm.i_ctx{tag}",
                               n_cm, name=f"n2{tag}"))
        scope(lambda: emit_ma3(mk, b["y"], f"hbm.i_y{tag}", b["ctx"], f"hbm.i_ctx{tag}",
                               b["y2"], f"hbm.i_y2{tag}", n_wo, n_bo, name=f"n3{tag}"))
        scope(lambda: emit_moe(mk, xin, xin_t, b["y2"], f"hbm.i_y2{tag}", xout, xout_t,
                               n_gw, n_w1b, n_w2b, n_w3b, n_ln2s, n_ln2b, idf, name=f"m{tag}"))

    mini("a", x3, "hbm.x3", x4, "hbm.x4")
    scope(lambda: emit_pmx_a(mk, x4, "hbm.x4", i_oh, "hbm.i_oh", p_wqb, p_wkb, p_wvb,
                             p_bq, p_bk, p_bv, p_ph, idf, name="pa"))
    scope(lambda: emit_pmx_b(mk, x4, "hbm.x4", i_oh, "hbm.i_oh", x5, "hbm.x5",
                             p_wo, p_bo, name="pb"))
    mini("b", x5, "hbm.x5", xo, "hbm.xo")
    mk.barrier_read(["hbm.xo"])
    mk.build()
    return nc


def emit_prep_bf16(mk, pairs, name="pp"):
    buf = T2(mk, name + "buf", 128, 2 * 16384, BF16)
    i = 0
    for src, dst in pairs:
        R, C = src.tensor.shape[-2] if False else None, None
        v_in = fm_hbm_view(src)
        v_out = fm_hbm_view(dst)
        nch = v_in.shape[1]
        CW = v_in.shape[2]
        step = max(1, 16384 // CW)
        c0 = 0
        while c0 < nch:
            c1 = min(nch, c0 + step)
            par = i % 2
            bt = f"{name}buf.{par}"
            bview = buf.h[:, par * 16384: par * 16384 + (c1 - c0) * CW].rearrange(
                "p (c w) -> p c w", c=(c1 - c0))
            mk.dma("gpsimd", bview, v_in[:, c0:c1, :], writes=[bt])
            mk.dma("sync", v_out[:, c0:c1, :], bview, reads=[bt], writes=[])
            c0 = c1
            i += 1
from concourse.bass_utils import run_bass_kernel_spmd
import ml_dtypes

_PROGRAMS = {}

def _prog(name, builder):
    if name not in _PROGRAMS:
        _PROGRAMS[name] = builder()
    return _PROGRAMS[name]


def _colstack(v):
    v = np.asarray(v, np.float32).reshape(-1)
    return np.ascontiguousarray(v.reshape(-1, 128).T)


def _cmask_np():
    cm = np.zeros((128, 2048), dtype=np.float32)
    for j in range(4):
        for srow in range(128):
            if 128 * j + srow < 512:
                cm[srow, j * 512 + 128 * j + srow:(j + 1) * 512] = 1
    return cm.astype(ml_dtypes.bfloat16)


def kernel(**inp):
    x0 = inp["L0_attn_x"]
    idf = np.eye(128, dtype=np.float32)
    idb = idf.astype(ml_dtypes.bfloat16)
    tp = lambda a: np.ascontiguousarray(np.asarray(a).T)
    A = lambda k: np.asarray(inp[k])

    shared = {
        "idf": idf, "idb": idb,
        "f_wg": tp(inp["L0_ffn_w_gate"]), "f_wu": tp(inp["L0_ffn_w_up"]),
        "f_wd": tp(inp["L0_ffn_w_down"]),
        "r_maa": np.concatenate([_colstack(inp[f"L0_attn_time_maa_{n}"]) for n in "xwkvrg"], axis=1),
        "r_w1": A("L0_attn_time_maa_w1"), "r_w2": A("L0_attn_time_maa_w2"),
        "r_dw1": A("L0_attn_time_decay_w1"), "r_dw2": A("L0_attn_time_decay_w2"),
        "r_dec": _colstack(inp["L0_attn_time_decay"]),
        "r_hdb": np.asarray(inp["L0_attn_head_decay_bias"], np.float32).reshape(16, 1),
        "r_slow": np.asarray(inp["L0_attn_slow_scale"], np.float32).reshape(16, 1),
        "r_wr": A("L0_attn_W_r"), "r_wk": A("L0_attn_W_k"), "r_wv": A("L0_attn_W_v"),
        "r_wgx": A("L0_attn_W_g"), "r_wo": A("L0_attn_W_o"),
        "r_dmix": _colstack(inp["L0_attn_decay_mix"]),
        "r_gam": _colstack(inp["L0_attn_ln_gamma"]), "r_bet": _colstack(inp["L0_attn_ln_beta"]),
        "n_wq": A("L1_ffn_wq"), "n_wk": A("L1_ffn_wk"), "n_wv": A("L1_ffn_wv"),
        "n_wo": A("L1_ffn_wo"), "n_bo": _colstack(inp["L1_ffn_bo"]),
        "n_ln1s": _colstack(inp["L1_ffn_ln1_scale"]), "n_ln1b": _colstack(inp["L1_ffn_ln1_shift"]),
        "n_gw": A("L1_ffn_gate_w"), "n_w1": A("L1_ffn_w1"), "n_w2": A("L1_ffn_w2"),
        "n_w3": A("L1_ffn_w3"),
        "n_ln2s": _colstack(inp["L1_ffn_ln2_scale"]), "n_ln2b": _colstack(inp["L1_ffn_ln2_shift"]),
        "n_cm": _cmask_np(),
        "p_wq": A("L1_attn_Wq"), "p_wk": A("L1_attn_Wk"), "p_wv": A("L1_attn_Wv"),
        "p_wo": A("L1_attn_Wo"),
        "p_bq": _colstack(inp["L1_attn_bq"]), "p_bk": _colstack(inp["L1_attn_bk"]),
        "p_bv": _colstack(inp["L1_attn_bv"]), "p_bo": _colstack(inp["L1_attn_bo"]),
        "p_ph": _colstack(np.asarray(inp["L1_attn_phase"]).reshape(-1)),
    }
    nc = _prog("mega", build_mega_program)
    in_maps = []
    for c in range(8):
        m = dict(shared)
        m["xt"] = np.ascontiguousarray(x0[c].T)
        in_maps.append(m)
    res = run_bass_kernel_spmd(nc, in_maps, core_ids=list(range(8)))
    out = np.stack([np.asarray(res.results[c]["xo"]).T for c in range(8)], axis=0)
    return out.astype(np.float32)
